# version 15
# baseline (speedup 1.0000x reference)
import numpy as np
import ml_dtypes
from contextlib import ExitStack
from collections import defaultdict
import concourse.bass as bass
import concourse.mybir as mybir
from concourse.bass_utils import run_bass_kernel_spmd

F32 = mybir.dt.float32
BF16 = mybir.dt.bfloat16
AF = mybir.ActivationFunctionType
ALU = mybir.AluOpType
AX = mybir.AxisListType
BF = ml_dtypes.bfloat16

NCORES = 8
B = 2
S = 16384
D = 1024
SL = S // NCORES
TL = B * SL
DI = 2048
H = 32
HD = 64
FF = 4096
NE = 14
FE = 2048
CH = 512
CPB = SL // CH
NCH = TL // CH
LSUB = 64
NSUB = SL // LSUB
JG = 8
CHAINED = True
EPS_LN = 1e-5
EPS_RMS = 1e-6
EPS_GN = 1e-5



def _plane_perm():
    new2old = np.zeros(DI, dtype=np.int64)
    for di in range(DI):
        h, hd = divmod(di, HD)
        new2old[(hd % 16) * 128 + h * 4 + hd // 16] = di
    return new2old


def _dup4(a):
    return np.repeat(np.asarray(a, np.float32), 4, axis=0)


def _ktile(vec):
    v = np.asarray(vec, np.float32)
    return np.ascontiguousarray(v.reshape(v.shape[0] // 128, 128).T)


def _cc(a, dt=np.float32):
    return np.ascontiguousarray(np.asarray(a, dtype=dt))


def stage_host(inputs):
    w = {}
    p = _plane_perm()
    w["wkT"] = _cc(np.asarray(inputs["L0_attn_Wk"]).T)
    w["wvT"] = _cc(np.asarray(inputs["L0_attn_Wv"]).T)
    w["wrT"] = _cc(np.asarray(inputs["L0_attn_Wr"]).T)
    w["woT"] = _cc(np.asarray(inputs["L0_attn_Wo"]).T)
    w["w1T"] = _cc(np.asarray(inputs["L0_ffn_W1"]).T)
    w["w2T"] = _cc(np.asarray(inputs["L0_ffn_W2"]).T)
    w["b1f"] = _ktile(inputs["L0_ffn_b1"])
    w["b2f"] = _ktile(inputs["L0_ffn_b2"])
    w["tdec"] = _ktile(inputs["L0_attn_time_decay"])
    w["tfirst"] = _ktile(inputs["L0_attn_time_first"])
    w["tmk"] = _ktile(inputs["L0_attn_time_mix_k"])
    w["tmv"] = _ktile(inputs["L0_attn_time_mix_v"])
    w["tmr"] = _ktile(inputs["L0_attn_time_mix_r"])

    ip = np.asarray(inputs["L1_attn_in_proj_W"])
    order = np.concatenate([p, 2048 + 2 * p, 2048 + 2 * p + 1, 6144 + p])
    w["ipT"] = _cc(ip[order, :].T)
    w["ipb"] = _cc(np.asarray(inputs["L1_attn_in_proj_b"])[order].reshape(64, 128).T)
    w["rmsw8"] = _ktile(inputs["L1_attn_rms_w"])

    def pl(vec):
        return _cc(np.asarray(vec, np.float32)[p].reshape(16, 128).T)
    w["cw"] = _cc(np.asarray(inputs["L1_attn_conv_w"])[p].reshape(16, 128, 4).transpose(1, 0, 2))
    w["cb"] = pl(inputs["L1_attn_conv_b"])
    w["Dpl"] = pl(inputs["L1_attn_D"])
    w["gnw"] = pl(inputs["L1_attn_gn_w"])
    w["gnb"] = pl(inputs["L1_attn_gn_b"])

    dyn = np.asarray(inputs["L1_attn_dyn_W"])
    SM = np.concatenate([
        _dup4(dyn[0:32]), _dup4(dyn[32:64]), _dup4(dyn[64:96]),
        _dup4(inputs["L1_attn_seldt_W"]), _dup4(inputs["L1_attn_beta_W"]),
        _dup4(inputs["L1_attn_rg_W"]), _dup4(inputs["L1_attn_ug_W"]),
        _dup4(np.asarray(inputs["L1_attn_selB_W"])[0::2]),
        _dup4(np.asarray(inputs["L1_attn_selB_W"])[1::2]),
        _dup4(np.asarray(inputs["L1_attn_selC_W"])[0::2]),
        _dup4(np.asarray(inputs["L1_attn_selC_W"])[1::2]),
    ], axis=0)
    w["smallT"] = _cc(SM[:, p].T)
    dynb = np.asarray(inputs["L1_attn_dyn_b"])
    small_b = np.concatenate([
        _dup4(dynb[0:32]), _dup4(dynb[32:64]), _dup4(dynb[64:96]),
        np.zeros(128, np.float32),
        _dup4(inputs["L1_attn_beta_b"]), _dup4(inputs["L1_attn_rg_b"]),
        _dup4(inputs["L1_attn_ug_b"]), np.zeros(512, np.float32),
    ]).astype(np.float32)
    w["smallb"] = _cc(small_b.reshape(11, 128).T)
    w["dtc"] = _cc(_dup4(inputs["L1_attn_dt_c"]).reshape(128, 1))
    rope = (1.0 / 10000.0 ** (np.arange(H, dtype=np.float64) / H)).astype(np.float32)
    w["ropec"] = _cc(_dup4(rope).reshape(128, 1))

    qw = np.asarray(inputs["L1_attn_Q_W"])
    qorder = np.concatenate([2 * p, 2 * p + 1])
    w["qwT"] = _cc(qw[qorder, :][:, p].T)
    w["owT"] = _cc(np.asarray(inputs["L1_attn_out_W"])[:, p].T)

    w["gwT"] = _cc(np.asarray(inputs["L1_ffn_gate_w"]).T)
    w["moew1"] = _cc(np.asarray(inputs["L1_ffn_w1"]))
    w["moew2"] = _cc(np.asarray(inputs["L1_ffn_w2"]))
    w["moeb1"] = _cc(np.asarray(inputs["L1_ffn_b1"]).reshape(NE, 16, 128).transpose(0, 2, 1))
    w["moeb2"] = _cc(np.asarray(inputs["L1_ffn_b2"]).reshape(NE, 8, 128).transpose(0, 2, 1))

    w["identf"] = np.eye(128, dtype=np.float32)
    w["tloc"] = np.arange(SL, dtype=np.float32).reshape(1, SL)
    gm = np.zeros((128, 32), np.float32)
    gm[np.arange(128), np.arange(128) // 4] = 1.0
    w["grpmask"] = _cc(gm)
    w["grpmaskT"] = _cc(gm.T)

    x = np.asarray(inputs["L0_attn_x"], dtype=np.float32)
    cores = []
    for c in range(NCORES):
        m = dict(w)
        xs = np.zeros((B, SL + 5, D), np.float32)
        lo = c * SL
        if c > 0:
            xs[:, 0:5, :] = x[:, lo - 5:lo, :]
        xs[:, 5:, :] = x[:, lo:lo + SL, :]
        m["x"] = _cc(xs)
        m["posrow"] = _cc(np.arange(lo, lo + SL, dtype=np.float32).reshape(1, SL))
        oh = np.zeros((1, NCORES), np.float32); oh[0, c] = 1.0
        m["ohself"] = oh
        ohp = np.zeros((1, NCORES), np.float32)
        if c > 0:
            ohp[0, c - 1] = 1.0
        m["ohprev"] = ohp
        m["halomask"] = np.full((1, 1), 1.0 if c > 0 else 0.0, np.float32)
        cores.append(m)
    return cores



class Sem:
    def __init__(self, nc, stack, name):
        self.h = stack.enter_context(nc.semaphore(name))
        self.v = 0

    def inc(self, instr, n=1):
        instr.then_inc(self.h, n)
        self.v += n
        return instr

    def dma(self, instr):
        return self.inc(instr, 16)

    def wait(self, eng, v=None):
        eng.wait_ge(self.h, self.v if v is None else v)


class G:
    def __init__(self, nc, mode="ladder", dev_out=()):
        self.nc = nc
        self.mode = mode
        self.dev_out = set(dev_out)
        self.stack = ExitStack()
        self.d = {}
        self._sn = 0

    def sem(self, name):
        self._sn += 1
        return Sem(self.nc, self.stack, f"{name}_{self._sn}")

    def din(self, name, shape, dtype=F32):
        if name not in self.d:
            self.d[name] = self.nc.dram_tensor(name, list(shape), dtype, kind="ExternalInput")
        return self.d[name]

    def dout(self, name, shape, dtype=F32):
        self.d[name] = self.nc.dram_tensor(name, list(shape), dtype, kind="ExternalOutput")
        return self.d[name]

    def dtmp(self, name, shape, dtype=F32):
        if name not in self.d:
            if name in self.dev_out:
                self.d[name] = self.nc.dram_tensor(name, list(shape), dtype,
                                                   kind="ExternalOutput")
            else:
                self.d[name] = self.nc.dram_tensor(name, list(shape), dtype)
        return self.d[name]

    def ap(self, name):
        return self.d[name].ap()

    def init_consts(self, values):
        n = len(values)
        self._ctile = self.stack.enter_context(self.nc.sbuf_tensor("consts", [128, n], F32))
        with self.nc.Block() as blk:
            @blk.vector
            def _(v):
                for i, val in enumerate(values):
                    v.memset(self._ctile[:, i:i + 1], float(val))
        for i, val in enumerate(values):
            self.nc.const_aps.register(F32, float(val), self._ctile[:, i:i + 1])


class EngProxy:

    _PASS = {"wait_ge", "sem_inc", "dma_start", "indirect_dma_start",
             "collective_compute", "nop", "engine_nop", "wait_op"}

    def __init__(self, eng, sem, chained=True):
        self._e = eng
        self._s = sem
        self._chained = chained
        self.last = None

    def __getattr__(self, name):
        fn = getattr(self._e, name)
        if name.startswith("_") or name in EngProxy._PASS or not callable(fn):
            return fn

        def wrapped(*a, **k):
            if self._chained and self._s.v > 0:
                self._e.wait_ge(self._s.h, self._s.v)
            ins = fn(*a, **k)
            self.last = ins
            if self._chained:
                self._s.inc(ins, 1)
            return ins
        return wrapped


class Phase:

    def __init__(self, g, name):
        self.g = g
        self.name = name
        self.items = []
        if not hasattr(g, "physems"):
            g.physems = dict(
                d=g.sem("d"), sd=g.sem("sd"), gd=g.sem("gd"),
                tensor=g.sem("ct"), vector=g.sem("cv"),
                scalar=g.sem("cs"), gpsimd=g.sem("cg"))
        self.dsem = g.physems["d"]
        self.sdsem = g.physems["sd"]
        self.gdsem = g.physems["gd"]
        self.chain = {e: g.physems[e] for e in
                      ("tensor", "vector", "scalar", "gpsimd")}
        self.groups = {None: None,
                       "a": {"vector", "d"},
                       "b": {"gpsimd", "sd"}}

    def st(self, engine, fn, lag=0, group=None):
        self.items.append((engine, fn, lag, group))

    def run(self):
        nc = self.g.nc
        seen = {e: {} for e in ("sync", "tensor", "vector", "scalar", "gpsimd")}
        hist = []
        with nc.Block() as blk:
            for en, fn, lag, grp in self.items:
                allsems = (list(self.chain.items())
                           + [("d", self.dsem), ("sd", self.sdsem), ("gd", self.gdsem)])
                hist.append({k: s.v for k, s in allsems})
                marks = hist[max(0, len(hist) - 1 - lag)]
                gkeys = self.groups.get(grp)

                def body(q, fn=fn, en=en, marks=marks, allsems=allsems, gkeys=gkeys):
                    for key, s in allsems:
                        if key == en:
                            continue
                        if gkeys is not None and key not in gkeys:
                            continue
                        tgt = marks[key]
                        if tgt > 0 and seen[en].get(key, 0) < tgt:
                            q.wait_ge(s.h, tgt)
                            seen[en][key] = tgt
                    if en == "sync":
                        fn(q)
                    elif CHAINED and en != "tensor":
                        px = EngProxy(q, self.chain[en], chained=True)
                        fn(px)
                    else:
                        px = EngProxy(q, self.chain[en], chained=False)
                        fn(px)
                        if px.last is not None:
                            self.chain[en].inc(px.last, 1)
                getattr(blk, en)(body)


def ld_cast(ph, stage_ap, dst_ap, src_ap):
    def ldf(q, ph=ph):
        dma_io(ph, q, [(stage_ap, src_ap)])
    ph.st("sync", ldf)

    def cst(v):
        v.tensor_copy(dst_ap, stage_ap)
    ph.st("vector", cst)


def dma_io(ph, q, pairs, sw=False, sem=None):
    s = sem if sem is not None else (ph.gdsem if sw else ph.dsem)
    for out, in_ in pairs:
        s.dma(q.dma_start(out=out, in_=in_))
    s.wait(q)


def mm_k(t, psum_ap, wsb, m_lo, m_n, rhs_slices):
    n = len(rhs_slices)
    r = None
    for kt, rh in enumerate(rhs_slices):
        r = t.matmul(psum_ap, wsb[:, kt, m_lo:m_lo + m_n], rh,
                     start=(kt == 0), stop=(kt == n - 1))
    return r


def _chunks(ch=CH):
    out = []
    for b in range(B):
        out.append((b, -4, 4, True))
        for i in range(SL // ch):
            out.append((b, i * ch, ch, False))
    return out



def ph0_ternary(g):
    nc = g.nc
    g.dtmp("w1t_bf", [D, FF], BF16)
    g.dtmp("w2t_bf", [FF, D], BF16)
    for (src, dst, rows, cols) in (("w1T", "w1t_bf", D, FF), ("w2T", "w2t_bf", FF, D)):
        nt = rows // 128
        with ExitStack() as ps:
            sb = lambda n, sh, dt=F32: ps.enter_context(nc.sbuf_tensor(f"t{src}_{n}", sh, dt))
            wt = sb("wt", [128, nt, cols])
            qb = sb("qb", [128, cols])
            mk = sb("mk", [128, cols])
            ob = sb("ob", [128, cols], BF16)
            acc = sb("acc", [128, 1])
            red = sb("red", [128, 1])
            ones = sb("ones", [128, 1])
            onesr = sb("onesr", [1, 128])
            s0 = sb("s0", [1, 1])
            rs = sb("rs", [128, 1])
            pt1 = ps.enter_context(nc.psum_tensor(f"t{src}_p1", [1, 1], F32))
            pt2 = ps.enter_context(nc.psum_tensor(f"t{src}_p2", [128, 1], F32))
            ph = Phase(g, f"t{src}")

            def in_fn(q, src=src, wt=wt, nt=nt, ph=ph):
                dma_io(ph, q, [(wt[:, i, :], g.ap(src)[i * 128:(i + 1) * 128, :])
                               for i in range(nt)])
            ph.st("sync", in_fn)

            def red_fn(v, wt=wt, nt=nt, acc=acc, red=red, ones=ones, onesr=onesr):
                v.memset(acc[:, :], 0.0)
                v.memset(ones[:, :], 1.0)
                v.memset(onesr[:, :], 1.0)
                for i in range(nt):
                    v.tensor_reduce(red[:, :], wt[:, i, :], AX.X, ALU.add,
                                    apply_absolute_value=True)
                    r = v.tensor_add(acc[:, :], acc[:, :], red[:, :])
                return r
            ph.st("vector", red_fn)

            ph.st("tensor", lambda t, pt1=pt1, ones=ones, acc=acc:
                  t.matmul(pt1[:, :], ones[:, :], acc[:, :], start=True, stop=True))

            def sc_fn(v, s0=s0, pt1=pt1, rows=rows, cols=cols):
                v.tensor_scalar(s0[:, :], pt1[:, :], 1.0 / (rows * cols), 1e-8,
                                ALU.mult, ALU.max)
                return v.reciprocal(s0[:, :], s0[:, :])
            ph.st("vector", sc_fn)

            ph.st("tensor", lambda t, pt2=pt2, onesr=onesr, s0=s0:
                  t.matmul(pt2[:, :], onesr[:, :], s0[:, :], start=True, stop=True))

            ph.st("vector", lambda v, rs=rs, pt2=pt2: v.tensor_copy(rs[:, :], pt2[:, :]))

            for i in range(nt):
                def tern_fn(v, i=i, wt=wt, qb=qb, mk=mk, ob=ob, rs=rs):
                    v.tensor_scalar_mul(qb[:, :], wt[:, i, :], rs[:, 0:1])
                    v.tensor_scalar_mul(mk[:, :], qb[:, :], -1.0)
                    v.tensor_max(mk[:, :], mk[:, :], qb[:, :])
                    v.tensor_single_scalar(mk[:, :], mk[:, :], 0.5, ALU.is_ge)
                    v.tensor_single_scalar(qb[:, :], qb[:, :], 0.0, ALU.is_gt)
                    v.tensor_scalar(qb[:, :], qb[:, :], 2.0, -1.0, ALU.mult, ALU.add)
                    return v.tensor_tensor(ob[:, :], qb[:, :], mk[:, :], ALU.mult)
                ph.st("vector", tern_fn)

                def out_fn(q, i=i, dst=dst, ob=ob, ph=ph):
                    dma_io(ph, q, [(g.ap(dst)[i * 128:(i + 1) * 128, :], ob[:, :])])
                ph.st("sync", out_fn)
            ph.run()



def ph1_l0a(g):
    nc = g.nc
    g.dtmp("xT", [128, 8, TL]); g.dtmp("xTh", [128, 8, B, 4])
    g.dtmp("xnbf", [128, 8, TL], BF16); g.dtmp("xnbfh", [128, 8, B, 4], BF16)
    for n in ("E", "EV", "srn"):
        g.dtmp(n, [128, 8, TL]); g.dtmp(n + "h", [128, 8, B, 4])
    with ExitStack() as ps:
        sb = lambda n, sh, dt=F32: ps.enter_context(nc.sbuf_tensor(f"p1_{n}", sh, dt))
        wk = sb("wk", [128, 8, D], BF16)
        wv = sb("wv", [128, 8, D], BF16)
        wr = sb("wr", [128, 8, D], BF16)
        tmix = sb("tmix", [128, 3, 8])
        xtok = sb("xtok", [128, 2, D])
        xctr = sb("xctr", [128, 2, D])
        stat = sb("stat", [128, 2, 4])
        CH1 = 256
        xnA = sb("xnA", [128, 8, CH1 + 1])
        xnB = sb("xnB", [128, 8, CH1 + 1])
        xtb = sb("xtb", [128, 8, CH1])
        dbuf = sb("dbuf", [128, 8, CH1])
        mixb = sb("mixb", [128, 3, 8, CH1], BF16)
        xnbf_b = sb("xnbfb", [128, 8, CH1], BF16)
        vsb = sb("vsb", [128, 8, CH1])
        kbuf = sb("kbuf", [128, 8, CH1])
        rbuf = sb("rbuf", [128, 8, CH1])
        ptA = ps.enter_context(nc.psum_tensor("p1_ptA", [128, 8, 128], F32))
        ptB = ps.enter_context(nc.psum_tensor("p1_ptB", [128, 8, 128], F32))
        pmV = ps.enter_context(nc.psum_tensor("p1_pmV", [128, CH1], F32))
        pmK = ps.enter_context(nc.psum_tensor("p1_pmK", [128, CH1], F32))
        pmR = ps.enter_context(nc.psum_tensor("p1_pmR", [128, CH1], F32))
        ident = sb("ident", [128, 128])
        ph = Phase(g, "p1")

        wstg = sb("wstg", [128, 1, D])
        for dst, nm in ((wk, "wkT"), (wv, "wvT"), (wr, "wrT")):
            wv_ = g.ap(nm).rearrange("(kt p) n -> p kt n", p=128)
            for q8 in range(8):
                ld_cast(ph, wstg[:, :, :], dst[:, q8:q8 + 1, :],
                        wv_[:, q8:q8 + 1, :])

        def c_load(q, ph=ph):
            dma_io(ph, q, [
                (tmix[:, 0, :], g.ap("tmk")),
                (tmix[:, 1, :], g.ap("tmv")),
                (tmix[:, 2, :], g.ap("tmr")),
                (ident[:, :], g.ap("identf")),
            ])
        ph.st("sync", c_load)

        for ci, (b, c0, cn, halo) in enumerate(_chunks(256)):
            xn_cur = xnA if (ci % 2 == 0) else xnB
            xn_prev = xnB if (ci % 2 == 0) else xnA
            ntt = 2 if not halo else 1
            tpn = 128 if not halo else 5
            tok0 = 0 if halo else 5 + c0
            coff = 0 if halo else 1

            def ld_x(q, b=b, tok0=tok0, ntt=ntt, tpn=tpn, ph=ph):
                pairs = []
                for tt in range(ntt):
                    pairs.append((xtok[0:tpn, tt, :],
                                  g.ap("x")[b, tok0 + tt * 128: tok0 + tt * 128 + tpn, :]))
                dma_io(ph, q, pairs)
            ph.st("sync", ld_x)

            def ln1(v, ntt=ntt, tpn=tpn):
                for tt in range(ntt):
                    v.tensor_reduce(stat[0:tpn, tt, 0:1], xtok[0:tpn, tt, :], AX.X, ALU.add)
                    v.tensor_scalar_mul(stat[0:tpn, tt, 1:2], stat[0:tpn, tt, 0:1], 1.0 / D)
                    v.tensor_scalar_sub(xctr[0:tpn, tt, :], xtok[0:tpn, tt, :],
                                        stat[0:tpn, tt, 1:2])
                    sqb = dbuf[:, 0:4, :].rearrange("p a b -> p (a b)")
                    v.tensor_tensor(sqb[0:tpn, :], xctr[0:tpn, tt, :], xctr[0:tpn, tt, :],
                                    ALU.mult)
                    v.tensor_reduce(stat[0:tpn, tt, 2:3], sqb[0:tpn, :], AX.X, ALU.add)
                    r = v.tensor_scalar_mul(stat[0:tpn, tt, 2:3], stat[0:tpn, tt, 2:3], 1.0 / D)
                return r
            ph.st("vector", ln1)

            def rstd_act(s, ntt=ntt, tpn=tpn):
                s.activation(stat[0:tpn, 0:ntt, 3], stat[0:tpn, 0:ntt, 2], AF.Ln,
                             bias=EPS_LN)
                return s.activation(stat[0:tpn, 0:ntt, 3], stat[0:tpn, 0:ntt, 3],
                                    AF.Exp, scale=-0.5)
            ph.st("scalar", rstd_act)

            def ln2(v, ntt=ntt, tpn=tpn):
                for tt in range(ntt):
                    r = v.tensor_scalar_mul(xctr[0:tpn, tt, :], xctr[0:tpn, tt, :],
                                            stat[0:tpn, tt, 3:4])
                return r
            ph.st("vector", ln2)

            for tt in range(ntt):
                def tp_xn(t, tt=tt, tpn=tpn):
                    for dt in range(8):
                        r = t.transpose(ptA[:, dt, 0:tpn],
                                        xctr[0:tpn, tt, dt * 128:(dt + 1) * 128],
                                        ident[0:tpn, 0:tpn])
                    return r
                ph.st("tensor", tp_xn)

                def ev_xn(v, tt=tt, tpn=tpn, xn_cur=xn_cur, coff=coff):
                    return v.tensor_copy(xn_cur[:, :, coff + tt * 128: coff + tt * 128 + tpn],
                                         ptA[:, :, 0:tpn])
                ph.st("vector", ev_xn)

                def tp_x(t, tt=tt, tpn=tpn):
                    for dt in range(8):
                        r = t.transpose(ptB[:, dt, 0:tpn],
                                        xtok[0:tpn, tt, dt * 128:(dt + 1) * 128],
                                        ident[0:tpn, 0:tpn])
                    return r
                ph.st("tensor", tp_x)

                def ev_x(v, tt=tt, tpn=tpn):
                    return v.tensor_copy(xtb[:, :, tt * 128: tt * 128 + tpn], ptB[:, :, 0:tpn])
                ph.st("vector", ev_x)

            def mixes(v, b=b, c0=c0, cn=cn, halo=halo, xn_cur=xn_cur, xn_prev=xn_prev):
                if halo:
                    lo, hi = 1, 5
                else:
                    prev_last = 4 if (c0 == 0) else CH1
                    v.tensor_copy(xn_cur[:, :, 0:1], xn_prev[:, :, prev_last:prev_last + 1])
                    lo, hi = 1, 1 + cn
                n = hi - lo
                v.tensor_sub(dbuf[:, :, 0:n], xn_cur[:, :, lo:hi], xn_cur[:, :, lo - 1:hi - 1])
                for qi in range(3):
                    for dt in range(8):
                        v.scalar_tensor_tensor(mixb[:, qi, dt, 0:n], dbuf[:, dt, 0:n],
                                               tmix[:, qi, dt:dt + 1],
                                               xn_cur[:, dt, lo - 1:hi - 1],
                                               ALU.mult, ALU.add)
                return v.tensor_copy(xnbf_b[:, :, 0:n], xn_cur[:, :, lo:hi])
            ph.st("vector", mixes)

            def st_xt(q, b=b, c0=c0, cn=cn, halo=halo, ph=ph):
                if halo:
                    dma_io(ph, q, [(g.ap("xTh")[:, :, b, :], xtb[:, :, 1:5]),
                                   (g.ap("xnbfh")[:, :, b, :], xnbf_b[:, :, 0:4])])
                else:
                    col = b * SL + c0
                    dma_io(ph, q, [(g.ap("xT")[:, :, col:col + cn], xtb[:, :, 0:cn]),
                                   (g.ap("xnbf")[:, :, col:col + cn], xnbf_b[:, :, 0:cn])])
            ph.st("sync", st_xt)

            for m in range(8):
                def mm3(t, m=m, cn=cn):
                    mm_k(t, pmV[:, 0:cn], wv, m * 128, 128, [mixb[:, 1, kt, 0:cn] for kt in range(8)])
                    mm_k(t, pmK[:, 0:cn], wk, m * 128, 128, [mixb[:, 0, kt, 0:cn] for kt in range(8)])
                    return mm_k(t, pmR[:, 0:cn], wr, m * 128, 128,
                                [mixb[:, 2, kt, 0:cn] for kt in range(8)])
                ph.st("tensor", mm3)

                def ev_vec(v, m=m, cn=cn):
                    v.tensor_copy(vsb[:, m, 0:cn], pmV[:, 0:cn])
                    v.tensor_copy(kbuf[:, m, 0:cn], pmK[:, 0:cn])
                    v.tensor_copy(rbuf[:, m, 0:cn], pmR[:, 0:cn])
                ph.st("vector", ev_vec)

            def bulk_act(s, cn=cn):
                s.activation(kbuf[:, :, 0:cn], kbuf[:, :, 0:cn], AF.Exp)
                return s.activation(rbuf[:, :, 0:cn], rbuf[:, :, 0:cn], AF.Sigmoid)
            ph.st("scalar", bulk_act)

            def mk_ev(v, cn=cn):
                v.tensor_tensor(vsb[:, :, 0:cn], kbuf[:, :, 0:cn], vsb[:, :, 0:cn],
                                ALU.mult)
            ph.st("vector", mk_ev)

            def st_m(q, b=b, c0=c0, cn=cn, halo=halo, ph=ph):
                if halo:
                    dma_io(ph, q, [(g.ap("Eh")[:, :, b, :], kbuf[:, :, 0:4]),
                                   (g.ap("EVh")[:, :, b, :], vsb[:, :, 0:4]),
                                   (g.ap("srnh")[:, :, b, :], rbuf[:, :, 0:4])])
                else:
                    col = b * SL + c0
                    dma_io(ph, q, [(g.ap("E")[:, :, col:col + cn], kbuf[:, :, 0:cn]),
                                   (g.ap("EV")[:, :, col:col + cn], vsb[:, :, 0:cn]),
                                   (g.ap("srn")[:, :, col:col + cn], rbuf[:, :, 0:cn])])
            ph.st("sync", st_m)
        ph.run()



def ph2_ffn0(g):
    nc = g.nc
    g.dtmp("g0", [128, 8, TL]); g.dtmp("g0h", [128, 8, B, 4])
    with ExitStack() as ps:
        sb = lambda n, sh, dt=F32: ps.enter_context(nc.sbuf_tensor(f"p2_{n}", sh, dt))
        w1 = sb("w1", [128, 8, FF], BF16)
        w2 = sb("w2", [128, 32, D], BF16)
        b1 = sb("b1", [128, 32])
        b2 = sb("b2", [128, 8])
        xb = sb("xb", [128, 8, CH], BF16)
        hb = sb("hb", [128, 32, CH], BF16)
        gb = sb("gb", [128, 8, CH])
        pmA = ps.enter_context(nc.psum_tensor("p2_pmA", [128, CH], F32))
        pmB = ps.enter_context(nc.psum_tensor("p2_pmB", [128, CH], F32))
        ph = Phase(g, "p2")

        ph.st("sync", lambda q, ph=None: None)
        def w_load2(q, ph=ph):
            dma_io(ph, q, [
                (w1[:, :, :], g.ap("w1t_bf").rearrange("(kt p) n -> p kt n", p=128)),
                (w2[:, :, :], g.ap("w2t_bf").rearrange("(kt p) n -> p kt n", p=128)),
                (b1[:, :], g.ap("b1f")),
                (b2[:, :], g.ap("b2f")),
            ])
        ph.items[-1] = ("sync", w_load2, 0, None)

        for (b, c0, cn, halo) in _chunks():
            col = b * SL + c0

            def ld_x(q, b=b, col=col, cn=cn, halo=halo, ph=ph):
                sr_ = g.ap("xnbfh")[:, :, b, :] if halo else g.ap("xnbf")[:, :, col:col + cn]
                dma_io(ph, q, [(xb[:, :, 0:cn], sr_)])
            ph.st("sync", ld_x)

            for fm in range(32):
                par = fm % 2
                def mm1(t, fm=fm, cn=cn, pm=(pmA if par == 0 else pmB)):
                    return mm_k(t, pm[:, 0:cn], w1, fm * 128, 128,
                                [xb[:, kt, 0:cn] for kt in range(8)])
                ph.st("tensor", mm1, lag=(1 if fm > 0 else 0))

                def ev1(s, fm=fm, cn=cn, pm=(pmA if par == 0 else pmB)):
                    return s.activation(hb[:, fm, 0:cn], pm[:, 0:cn], AF.Gelu,
                                        bias=b1[:, fm:fm + 1])
                ph.st("scalar", ev1)

            for dm in range(8):
                par = dm % 2
                def mm2(t, dm=dm, cn=cn, pm=(pmA if par == 0 else pmB)):
                    return mm_k(t, pm[:, 0:cn], w2, dm * 128, 128,
                                [hb[:, kt, 0:cn] for kt in range(32)])
                ph.st("tensor", mm2, lag=(1 if dm > 0 else 0))

                def ev2(s, dm=dm, cn=cn, pm=(pmA if par == 0 else pmB)):
                    return s.activation(gb[:, dm, 0:cn], pm[:, 0:cn], AF.Silu,
                                        bias=b2[:, dm:dm + 1])
                ph.st("scalar", ev2)

            def st_g(q, b=b, col=col, cn=cn, halo=halo, ph=ph):
                dst = g.ap("g0h")[:, :, b, :] if halo else g.ap("g0")[:, :, col:col + cn]
                dma_io(ph, q, [(dst, gb[:, :, 0:cn])])
            ph.st("sync", st_g)
        ph.run()



def ph3_l0scan(g):
    nc = g.nc
    g.dtmp("Sloc", [128, 8, TL]); g.dtmp("Tloc", [128, 8, TL])
    if g.mode == "ladder":
        g.dout("l0sum", [128, 8, B, 10])
    else:
        g.dtmp("l0sum", [128, 8, B, 10])
    with ExitStack() as ps:
        sb = lambda n, sh, dt=F32: ps.enter_context(nc.sbuf_tensor(f"p3_{n}", sh, dt))
        td = sb("td", [128, 8])
        etd = sb("etd", [128, 8])
        ew = sb("ew", [128, 8])
        ewp = sb("ewp", [128, SL])
        Eb = sb("Eb", [128, SL])
        EVb = sb("EVb", [128, SL])
        Sb = sb("Sb", [128, SL])
        Tb = sb("Tb", [128, SL])
        summ = sb("summ", [128, 8, B, 10])
        ph = Phase(g, "p3")

        def td_ld(q, ph=ph):
            dma_io(ph, q, [(td[:, :], g.ap("tdec"))])
        ph.st("sync", td_ld)
        ph.st("scalar", lambda s: s.activation(etd[:, :], td[:, :], AF.Exp))
        ph.st("scalar", lambda s: s.activation(ew[:, :], etd[:, :], AF.Exp, scale=-1.0))

        for dt in range(8):
            for b in range(B):
                col = b * SL

                def ld(q, dt=dt, col=col, ph=ph):
                    dma_io(ph, q, [(Eb[:, :], g.ap("E")[:, dt, col:col + SL]),
                                   (EVb[:, :], g.ap("EV")[:, dt, col:col + SL])])
                ph.st("sync", ld)

                def mkew(v, dt=dt):
                    v.memset(ewp[:, :], 0.0)
                    return v.tensor_scalar_add(ewp[:, :], ewp[:, :], ew[:, dt:dt + 1])
                ph.st("vector", mkew)

                def scans(v, dt=dt, b=b):
                    v.tensor_tensor_scan(Sb[:, :], ewp[:, :], EVb[:, :], 0.0,
                                         ALU.mult, ALU.add)
                    v.tensor_tensor_scan(Tb[:, :], ewp[:, :], Eb[:, :], 0.0,
                                         ALU.mult, ALU.add)
                    v.tensor_copy(summ[:, dt, b, 0:1], Sb[:, SL - 1:SL])
                    v.tensor_copy(summ[:, dt, b, 1:2], Tb[:, SL - 1:SL])
                    v.tensor_copy(summ[:, dt, b, 2:6], Sb[:, SL - 4:SL])
                    return v.tensor_copy(summ[:, dt, b, 6:10], Tb[:, SL - 4:SL])
                ph.st("vector", scans)

                def stout(q, dt=dt, col=col, ph=ph):
                    dma_io(ph, q, [(g.ap("Sloc")[:, dt, col:col + SL], Sb[:, :]),
                                   (g.ap("Tloc")[:, dt, col:col + SL], Tb[:, :])])
                ph.st("sync", stout)

        def st_sum(q, ph=ph):
            dma_io(ph, q, [(g.ap("l0sum")[:, :, :, :], summ[:, :, :, :])])
        ph.st("sync", st_sum)
        ph.run()



def ph4_l0fix(g):
    nc = g.nc
    if g.mode == "ladder":
        g.din("l0sum_all", [NCORES, 128, 8, B, 10])
    g.dtmp("x1T", [128, 8, TL]); g.dtmp("xn1T", [128, 8, TL])
    g.dtmp("xn1bf", [128, 8, TL], BF16); g.dtmp("xnr1bf", [128, 8, TL], BF16)
    g.dtmp("xnr1hbf", [128, 8, B, 3], BF16)
    g.dtmp("statrow", [1, 3 * CH])
    with ExitStack() as po:
        so = lambda n, sh, dt=F32: po.enter_context(nc.sbuf_tensor(f"p4_{n}", sh, dt))
        rwkv = so("rwkv", [128, 8, TL], BF16)
        rwkhb = so("rwkhb", [128, 8, B, 3], BF16)
        wo = so("wo", [128, 8, D], BF16)
        etd = so("etd", [128, 8]); negetd = so("negetd", [128, 8])
        eu = so("eu", [128, 8])

        with ExitStack() as ps:
            sb = lambda n, sh, dt=F32: ps.enter_context(nc.sbuf_tensor(f"p4a_{n}", sh, dt))
            td = sb("td", [128, 8]); tf = sb("tf", [128, 8])
            d2k = sb("d2k", [128, 8])
            ew3 = sb("ew3", [128, 8, 3])
            sall = sb("sall", [128, NCORES, 8, B, 10])
            ohbs = sb("ohbs", [128, NCORES]); ohbp = sb("ohbp", [128, NCORES])
            candS = sb("candS", [128, 8, B, NCORES]); candT = sb("candT", [128, 8, B, NCORES])
            CS = sb("CS", [128, 8, B]); CT = sb("CT", [128, 8, B])
            CSp = sb("CSp", [128, 8, B]); CTp = sb("CTp", [128, 8, B])
            SlocP = sb("SlocP", [128, 8, B, 4]); TlocP = sb("TlocP", [128, 8, B, 4])
            Eh = sb("Eh", [128, 8, B, 4]); EVh = sb("EVh", [128, 8, B, 4])
            srh = sb("srh", [128, 8, B, 4])
            numh = sb("numh", [128, 8, B, 3]); denh = sb("denh", [128, 8, B, 3])
            tmph = sb("tmph", [128, 8, B, 3])
            tlocb = sb("tlocb", [128, SL])
            Pw = sb("Pw", [128, SL])
            Eb = sb("Eb", [128, SL]); EVb = sb("EVb", [128, SL])
            Sb = sb("Sb", [128, SL]); Tb = sb("Tb", [128, SL]); srb = sb("srb", [128, SL])
            numb = sb("numb", [128, SL]); denb = sb("denb", [128, SL])
            ph = Phase(g, "p4a")

            def ld0(q, ph=ph):
                dma_io(ph, q, [
                    (td[:, :], g.ap("tdec")), (tf[:, :], g.ap("tfirst")),
                    (sall[:, :, :, :, :],
                     g.ap("l0sum_all").rearrange("c p d b q -> p c d b q")),
                    (ohbs[:, :], g.ap("ohself").partition_broadcast(128)),
                    (ohbp[:, :], g.ap("ohprev").partition_broadcast(128)),
                    (tlocb[:, :], g.ap("tloc").partition_broadcast(128)),
                    (Eh[:, :, :, :], g.ap("Eh")), (EVh[:, :, :, :], g.ap("EVh")),
                    (srh[:, :, :, :], g.ap("srnh")),
                ])
            ph.st("sync", ld0)

            wstg4 = sb("wstg4", [128, 8, D])
            ld_cast(ph, wstg4[:, :, :], wo[:, :, :],
                    g.ap("woT").rearrange("(kt p) n -> p kt n", p=128))

            def acts0(s):
                s.activation(etd[:, :], td[:, :], AF.Exp)
                s.activation(eu[:, :], tf[:, :], AF.Exp)
                s.activation(d2k[:, :], etd[:, :], AF.Exp, scale=-float(SL))
                for i in range(3):
                    s.activation(ew3[:, :, i], etd[:, :], AF.Exp, scale=-float(SL - 3 + i))
            ph.st("scalar", acts0)

            def chain_fn(v):
                v.tensor_scalar_mul(negetd[:, :], etd[:, :], -1.0)
                v.memset(candS[:, :, :, 0], 0.0)
                v.memset(candT[:, :, :, 0], 0.0)
                d2b = d2k[:, :].unsqueeze(2).broadcast_to([128, 8, B])
                for c in range(NCORES - 1):
                    v.tensor_tensor(candS[:, :, :, c + 1], candS[:, :, :, c], d2b, ALU.mult)
                    v.tensor_add(candS[:, :, :, c + 1], candS[:, :, :, c + 1],
                                 sall[:, c, :, :, 0])
                    v.tensor_tensor(candT[:, :, :, c + 1], candT[:, :, :, c], d2b, ALU.mult)
                    v.tensor_add(candT[:, :, :, c + 1], candT[:, :, :, c + 1],
                                 sall[:, c, :, :, 1])
                for dst, cand, oh in ((CS, candS, ohbs), (CT, candT, ohbs),
                                      (CSp, candS, ohbp), (CTp, candT, ohbp)):
                    v.memset(dst[:, :, :], 0.0)
                    for c in range(NCORES):
                        v.scalar_tensor_tensor(dst[:, :, :], cand[:, :, :, c],
                                               oh[:, c:c + 1], dst[:, :, :],
                                               ALU.mult, ALU.add)
                for dst, qi in ((SlocP, 2), (TlocP, 6)):
                    v.memset(dst[:, :, :, :], 0.0)
                    for c in range(NCORES):
                        v.scalar_tensor_tensor(dst[:, :, :, :], sall[:, c, :, :, qi:qi + 4],
                                               ohbp[:, c:c + 1], dst[:, :, :, :],
                                               ALU.mult, ALU.add)
            ph.st("vector", chain_fn)

            def halo_y(v):
                ew3b = ew3[:, :, :].unsqueeze(2).broadcast_to([128, 8, B, 3])
                cspb = CSp[:, :, :].unsqueeze(3).broadcast_to([128, 8, B, 3])
                eub = eu[:, :].unsqueeze(2).unsqueeze(3).broadcast_to([128, 8, B, 3])
                v.tensor_tensor(numh[:, :, :, :], ew3b, cspb, ALU.mult)
                v.tensor_add(numh[:, :, :, :], numh[:, :, :, :], SlocP[:, :, :, 0:3])
                v.tensor_tensor(tmph[:, :, :, :], EVh[:, :, :, 1:4], eub, ALU.mult)
                v.tensor_add(numh[:, :, :, :], numh[:, :, :, :], tmph[:, :, :, :])
                ctpb = CTp[:, :, :].unsqueeze(3).broadcast_to([128, 8, B, 3])
                v.tensor_tensor(denh[:, :, :, :], ew3b, ctpb, ALU.mult)
                v.tensor_add(denh[:, :, :, :], denh[:, :, :, :], TlocP[:, :, :, 0:3])
                v.tensor_tensor(tmph[:, :, :, :], Eh[:, :, :, 1:4], eub, ALU.mult)
                v.tensor_add(denh[:, :, :, :], denh[:, :, :, :], tmph[:, :, :, :])
                v.reciprocal(denh[:, :, :, :], denh[:, :, :, :])
                v.tensor_tensor(numh[:, :, :, :], numh[:, :, :, :], denh[:, :, :, :],
                                ALU.mult)
                v.tensor_tensor(rwkhb[:, :, :, :], numh[:, :, :, :], srh[:, :, :, 1:4],
                                ALU.mult)
            ph.st("vector", halo_y)

            for dt in range(8):
                for b in range(B):
                    col = b * SL

                    def ld(q, dt=dt, col=col, ph=ph):
                        dma_io(ph, q, [(Sb[:, :], g.ap("Sloc")[:, dt, col:col + SL]),
                                       (Tb[:, :], g.ap("Tloc")[:, dt, col:col + SL]),
                                       (Eb[:, :], g.ap("E")[:, dt, col:col + SL]),
                                       (EVb[:, :], g.ap("EV")[:, dt, col:col + SL]),
                                       (srb[:, :], g.ap("srn")[:, dt, col:col + SL])])
                    ph.st("sync", ld)

                    def mk_pw(v, dt=dt):
                        v.tensor_scalar_mul(Pw[:, :], tlocb[:, :], negetd[:, dt:dt + 1])
                    ph.st("vector", mk_pw)

                    ph.st("scalar", lambda s: s.activation(Pw[:, :], Pw[:, :], AF.Exp))

                    def fix(v, dt=dt, b=b, col=col):
                        v.tensor_scalar_mul(numb[:, :], Pw[:, :], CS[:, dt, b:b + 1])
                        v.scalar_tensor_tensor(numb[:, :], EVb[:, :], eu[:, dt:dt + 1],
                                               numb[:, :], ALU.mult, ALU.add)
                        v.tensor_add(numb[:, 1:SL], numb[:, 1:SL], Sb[:, 0:SL - 1])
                        v.tensor_scalar_mul(denb[:, :], Pw[:, :], CT[:, dt, b:b + 1])
                        v.scalar_tensor_tensor(denb[:, :], Eb[:, :], eu[:, dt:dt + 1],
                                               denb[:, :], ALU.mult, ALU.add)
                        v.tensor_add(denb[:, 1:SL], denb[:, 1:SL], Tb[:, 0:SL - 1])
                        v.reciprocal(denb[:, :], denb[:, :])
                        v.tensor_tensor(numb[:, :], numb[:, :], denb[:, :], ALU.mult)
                        v.tensor_tensor(rwkv[:, dt, col:col + SL], numb[:, :], srb[:, :],
                                        ALU.mult)
                    ph.st("vector", fix)
            ph.run()

        with ExitStack() as ps:
            sb = lambda n, sh, dt=F32: ps.enter_context(nc.sbuf_tensor(f"p4b_{n}", sh, dt))
            onecol = sb("onecol", [128, 1])
            s_sb = sb("s_sb", [128, 8, CH])
            gch = sb("gch", [128, 8, CH]); xtc = sb("xtc", [128, 8, CH])
            x1b = sb("x1b", [128, 8, CH])
            xn1b = sb("xn1b", [128, 8, CH])
            obf = sb("obf", [128, 8, CH], BF16); obf2 = sb("obf2", [128, 8, CH], BF16)
            strow = sb("strow", [1, 3, CH]); stbc = sb("stbc", [128, 3, CH])
            svar = sb("svar", [1, CH]); stmp = sb("stmp", [1, CH]); stmp2 = sb("stmp2", [1, CH])
            pm0 = ps.enter_context(nc.psum_tensor("p4_pm0", [128, CH], F32))
            pm1 = ps.enter_context(nc.psum_tensor("p4_pm1", [128, CH], F32))
            pmu = ps.enter_context(nc.psum_tensor("p4_pmu", [1, CH], F32))
            ps2 = ps.enter_context(nc.psum_tensor("p4_ps2", [1, CH], F32))
            ph = Phase(g, "p4b")

            def initb(v):
                v.memset(onecol[:, :], 1.0)
                v.memset(strow[:, :, :], 0.0)
            ph.st("vector", initb)

            for b, c0, cn, halo in _chunks():
                col = b * SL + c0
                nmm = cn if not halo else 3

                if not halo:
                    def ldc(q, col=col, cn=cn, ph=ph):
                        dma_io(ph, q, [(gch[:, :, 0:cn], g.ap("g0")[:, :, col:col + cn]),
                                       (xtc[:, :, 0:cn], g.ap("xT")[:, :, col:col + cn])])
                else:
                    def ldc(q, b=b, ph=ph):
                        dma_io(ph, q, [(gch[:, :, 0:3], g.ap("g0h")[:, :, b, 1:4]),
                                       (xtc[:, :, 0:3], g.ap("xTh")[:, :, b, 1:4])])
                ph.st("sync", ldc)

                for m in range(8):
                    def mmwo(t, m=m, b=b, col=col, halo=halo, nmm=nmm,
                             pm=(pm0 if m % 2 == 0 else pm1)):
                        if halo:
                            rhs = [rwkhb[:, kt, b, :] for kt in range(8)]
                        else:
                            rhs = [rwkv[:, kt, col:col + nmm] for kt in range(8)]
                        return mm_k(t, pm[:, 0:nmm], wo, m * 128, 128, rhs)
                    ph.st("tensor", mmwo, lag=(1 if m > 0 else 0))

                    def evwo(v, m=m, nmm=nmm, pm=(pm0 if m % 2 == 0 else pm1)):
                        v.tensor_copy(s_sb[:, m, 0:nmm], pm[:, 0:nmm])
                    ph.st("vector", evwo)

                def mkx1(v, nmm=nmm):
                    v.tensor_tensor(x1b[:, :, 0:nmm], s_sb[:, :, 0:nmm], gch[:, :, 0:nmm],
                                    ALU.mult)
                    v.tensor_add(x1b[:, :, 0:nmm], x1b[:, :, 0:nmm], xtc[:, :, 0:nmm])
                    v.tensor_tensor(s_sb[:, :, 0:nmm], x1b[:, :, 0:nmm], x1b[:, :, 0:nmm],
                                    ALU.mult)
                ph.st("vector", mkx1)

                def mmstat(t, nmm=nmm):
                    for kt in range(8):
                        t.matmul(pmu[:, 0:nmm], onecol[:, :], x1b[:, kt, 0:nmm],
                                 start=(kt == 0), stop=(kt == 7))
                    r = None
                    for kt in range(8):
                        r = t.matmul(ps2[:, 0:nmm], onecol[:, :], s_sb[:, kt, 0:nmm],
                                     start=(kt == 0), stop=(kt == 7))
                    return r
                ph.st("tensor", mmstat)

                def stat1(v, nmm=nmm):
                    v.tensor_scalar_mul(strow[:, 0, 0:nmm], pmu[:, 0:nmm], 1.0 / D)
                    v.tensor_scalar_mul(svar[:, 0:nmm], ps2[:, 0:nmm], 1.0 / D)
                    v.tensor_tensor(stmp[:, 0:nmm], strow[:, 0, 0:nmm], strow[:, 0, 0:nmm],
                                    ALU.mult)
                    v.tensor_sub(svar[:, 0:nmm], svar[:, 0:nmm], stmp[:, 0:nmm])
                ph.st("vector", stat1)

                def rstd_act(s, nmm=nmm):
                    s.activation(stmp[:, 0:nmm], svar[:, 0:nmm], AF.Ln, bias=EPS_LN)
                    return s.activation(strow[:, 1, 0:nmm], stmp[:, 0:nmm], AF.Exp,
                                        scale=-0.5)
                ph.st("scalar", rstd_act)

                def stat2(v, nmm=nmm):
                    v.tensor_tensor(stmp[:, 0:nmm], strow[:, 1, 0:nmm],
                                    strow[:, 1, 0:nmm], ALU.mult)
                    v.tensor_tensor(stmp2[:, 0:nmm], svar[:, 0:nmm], stmp[:, 0:nmm],
                                    ALU.mult)
                ph.st("vector", stat2)

                def rstd2_act(s, nmm=nmm):
                    s.activation(stmp2[:, 0:nmm], stmp2[:, 0:nmm], AF.Ln, bias=EPS_RMS)
                    return s.activation(strow[:, 2, 0:nmm], stmp2[:, 0:nmm], AF.Exp,
                                        scale=-0.5)
                ph.st("scalar", rstd2_act)

                def strd(q, ph=ph):
                    dma_io(ph, q, [(g.ap("statrow")[:, 0:3 * CH],
                                    strow[:, :, :].rearrange("a b c -> a (b c)"))])
                    dma_io(ph, q, [(stbc[:, :, :].rearrange("a b c -> a (b c)"),
                                    g.ap("statrow").partition_broadcast(128))])
                ph.st("sync", strd)

                def norm(v, nmm=nmm):
                    for kt in range(8):
                        v.tensor_sub(xn1b[:, kt, 0:nmm], x1b[:, kt, 0:nmm],
                                     stbc[:, 0, 0:nmm])
                        v.tensor_tensor(xn1b[:, kt, 0:nmm], xn1b[:, kt, 0:nmm],
                                        stbc[:, 1, 0:nmm], ALU.mult)
                    v.tensor_copy(obf[:, :, 0:nmm], xn1b[:, :, 0:nmm])
                    for kt in range(8):
                        v.tensor_tensor(obf2[:, kt, 0:nmm], xn1b[:, kt, 0:nmm],
                                        stbc[:, 2, 0:nmm], ALU.mult)
                ph.st("vector", norm)

                if not halo:
                    def stc(q, col=col, cn=cn, ph=ph):
                        dma_io(ph, q, [
                            (g.ap("x1T")[:, :, col:col + cn], x1b[:, :, 0:cn]),
                            (g.ap("xn1T")[:, :, col:col + cn], xn1b[:, :, 0:cn]),
                            (g.ap("xn1bf")[:, :, col:col + cn], obf[:, :, 0:cn]),
                            (g.ap("xnr1bf")[:, :, col:col + cn], obf2[:, :, 0:cn]),
                        ])
                else:
                    def stc(q, b=b, ph=ph):
                        dma_io(ph, q, [(g.ap("xnr1hbf")[:, :, b, :], obf2[:, :, 0:3])])
                ph.st("sync", stc)
            ph.run()



def ph5_inproj(g):
    nc = g.nc
    CH5 = 256
    g.dtmp("siluz", [128, 16, TL], BF16)
    g.dtmp("K0", [128, 16, TL], BF16); g.dtmp("K1", [128, 16, TL], BF16)
    g.dtmp("u", [128, 16, TL]); g.dtmp("ubf", [128, 16, TL], BF16)
    with ExitStack() as ps:
        sb = lambda n, sh, dt=F32: ps.enter_context(nc.sbuf_tensor(f"p5_{n}", sh, dt))
        wip = sb("wip", [128, 8, 8192], BF16)
        rms8 = sb("rms8", [128, 8])
        ipb = sb("ipb", [128, 64])
        cwp = sb("cwp", [128, 16, 4]); cbp = sb("cbp", [128, 16])
        hmsk = sb("hmsk", [128, 1])
        xrh = sb("xrh", [128, 8, B, 3], BF16)
        Vhalo = sb("Vhalo", [128, 16, B, 3])
        xrb = sb("xrb", [128, 8, CH5], BF16)
        Vb = sb("Vb", [128, 16, 3 + CH5])
        ub = sb("ub", [128, 16, CH5]); ubfb = sb("ubfb", [128, 16, CH5], BF16)
        cvt = sb("cvt", [128, 16, CH5])
        stg0 = sb("stg0", [128, CH5], BF16); stg1 = sb("stg1", [128, CH5], BF16)
        zstg0 = sb("zstg0", [128, CH5], BF16); zstg1 = sb("zstg1", [128, CH5], BF16)
        pmA = ps.enter_context(nc.psum_tensor("p5_pmA", [128, CH5], F32))
        pmB = ps.enter_context(nc.psum_tensor("p5_pmB", [128, CH5], F32))
        ph = Phase(g, "p5")

        def ld0(q, ph=ph):
            dma_io(ph, q, [(rms8[:, :], g.ap("rmsw8")), (ipb[:, :], g.ap("ipb")),
                           (cwp[:, :, :], g.ap("cw")), (cbp[:, :], g.ap("cb")),
                           (hmsk[:, :], g.ap("halomask").partition_broadcast(128)),
                           (xrh[:, :, :, :], g.ap("xnr1hbf"))])
        ph.st("sync", ld0)

        for kt in range(8):
            for half in range(2):
                lo = half * 4096

                def ldw(q, kt=kt, lo=lo, ph=ph):
                    dma_io(ph, q, [(ub[:, :, :].rearrange("p a b -> p (a b)")[:, 0:4096],
                                    g.ap("ipT")[kt * 128:(kt + 1) * 128, lo:lo + 4096])])
                ph.st("sync", ldw)

                def fold(v, kt=kt, lo=lo):
                    slab = ub[:, :, :].rearrange("p a b -> p (a b)")[:, 0:4096]
                    v.tensor_scalar_mul(slab, slab, rms8[:, kt:kt + 1])
                    v.tensor_copy(wip[:, kt, lo:lo + 4096], slab)
                ph.st("vector", fold)

        for b in range(B):
            for i, m in enumerate(range(48, 64)):
                def mmh(t, m=m, b=b):
                    return mm_k(t, pmA[:, 0:3], wip, m * 128, 128,
                                [xrh[:, kt, b, :] for kt in range(8)])
                ph.st("tensor", mmh)

                def evh(s, m=m, b=b, i=i):
                    return s.activation(Vhalo[:, i, b, :], pmA[:, 0:3], AF.Identity,
                                        bias=ipb[:, m:m + 1])
                ph.st("scalar", evh)

        def maskh(v):
            v.tensor_scalar_mul(Vhalo[:, :, :, :], Vhalo[:, :, :, :], hmsk[:, 0:1])
        ph.st("vector", maskh)

        for b in range(B):
            for c0 in range(0, SL, CH5):
                col = b * SL + c0
                cn = CH5

                def ldx(q, col=col, cn=cn, ph=ph):
                    dma_io(ph, q, [(xrb[:, :, 0:cn], g.ap("xnr1bf")[:, :, col:col + cn])])
                ph.st("sync", ldx)

                if c0 == 0:
                    def carry0(v, b=b):
                        v.tensor_copy(Vb[:, :, 0:3], Vhalo[:, :, b, :])
                    ph.st("vector", carry0)

                for m in range(64):
                    pm = pmA if m % 2 == 0 else pmB
                    def mm1(t, m=m, cn=cn, pm=pm):
                        return mm_k(t, pm[:, 0:cn], wip, m * 128, 128,
                                    [xrb[:, kt, 0:cn] for kt in range(8)])
                    ph.st("tensor", mm1, lag=(1 if m > 0 else 0))

                    if m < 16:
                        zs = zstg0 if (m // 2) % 2 == 0 else zstg1
                        def ev(v, m=m, cn=cn, pm=pm, zs=zs):
                            v.tensor_scalar_add(cvt[:, m, 0:cn], pm[:, 0:cn],
                                                ipb[:, m:m + 1])
                        ph.st("vector", ev)
                    elif m < 48:
                        ss = stg0 if (m // 2) % 2 == 0 else stg1
                        def ev(v, m=m, cn=cn, pm=pm, ss=ss):
                            v.tensor_scalar_add(ss[:, 0:cn], pm[:, 0:cn], ipb[:, m:m + 1])
                        ph.st("vector", ev)

                        def stv(q, m=m, col=col, cn=cn, ss=ss, ph=ph):
                            if m < 32:
                                dst = g.ap("K0")[:, m - 16, col:col + cn]
                            else:
                                dst = g.ap("K1")[:, m - 32, col:col + cn]
                            dma_io(ph, q, [(dst, ss[:, 0:cn])])
                        ph.st("sync", stv)
                    else:
                        def ev(v, m=m, cn=cn, pm=pm):
                            v.tensor_scalar_add(Vb[:, m - 48, 3:3 + cn], pm[:, 0:cn],
                                                ipb[:, m:m + 1])
                        ph.st("vector", ev)

                def zsilu(s, cn=cn):
                    return s.activation(cvt[:, :, 0:cn], cvt[:, :, 0:cn], AF.Silu)
                ph.st("scalar", zsilu)

                def zst(v, cn=cn):
                    v.tensor_copy(ubfb[:, :, 0:cn], cvt[:, :, 0:cn])
                ph.st("vector", zst)

                def stz(q, col=col, cn=cn, ph=ph):
                    dma_io(ph, q, [(g.ap("siluz")[:, :, col:col + cn], ubfb[:, :, 0:cn])])
                ph.st("sync", stz)

                def conv(v, cn=cn):
                    cw0 = cwp[:, :, 0:1].broadcast_to([128, 16, cn])
                    v.tensor_tensor(cvt[:, :, 0:cn], Vb[:, :, 0:cn], cw0, ALU.mult)
                    for tau in range(1, 4):
                        cwt = cwp[:, :, tau:tau + 1].broadcast_to([128, 16, cn])
                        v.tensor_tensor(ub[:, :, 0:cn], Vb[:, :, tau:tau + cn], cwt,
                                        ALU.mult)
                        v.tensor_add(cvt[:, :, 0:cn], cvt[:, :, 0:cn], ub[:, :, 0:cn])
                    cbb = cbp[:, :].unsqueeze(2).broadcast_to([128, 16, cn])
                    v.tensor_add(cvt[:, :, 0:cn], cvt[:, :, 0:cn], cbb)
                ph.st("vector", conv)

                ph.st("scalar", lambda s, cn=cn: s.activation(ub[:, :, 0:cn],
                                                              cvt[:, :, 0:cn], AF.Silu))

                def cast_u(v, cn=cn):
                    v.tensor_copy(ubfb[:, :, 0:cn], ub[:, :, 0:cn])
                    v.tensor_copy(Vb[:, :, 0:3], Vb[:, :, cn:cn + 3])
                ph.st("vector", cast_u)

                def stc(q, col=col, cn=cn, ph=ph):
                    dma_io(ph, q, [
                        (g.ap("u")[:, :, col:col + cn], ub[:, :, 0:cn]),
                        (g.ap("ubf")[:, :, col:col + cn], ubfb[:, :, 0:cn]),
                    ])
                ph.st("sync", stc)
        ph.run()



def ph6_chains(g):
    nc = g.nc
    for n in ("a11d", "a12d", "betad", "vpugd", "sb0d", "sb1d", "sc0d", "sc1d"):
        g.dtmp(n, [128, TL])
    g.dtmp("Kh0", [128, 16, TL], BF16); g.dtmp("Kh1", [128, 16, TL], BF16)
    g.dtmp("Vg", [128, 16, TL], BF16)
    dbg = "dbg1" in g.dev_out
    if dbg:
        for n in ("dbg1", "dbg2", "dbg3", "dbg4", "dbg5", "dbg6", "dbg7", "dbg8"):
            g.dtmp(n, [128, TL])
    with ExitStack() as ps:
        sb = lambda n, sh, dt=F32: ps.enter_context(nc.sbuf_tensor(f"p6_{n}", sh, dt))
        wsm = sb("wsm", [128, 16, 1408], BF16)
        smb = sb("smb", [128, 11])
        dtc = sb("dtc", [128, 1]); ropec = sb("ropec", [128, 1])
        spdtc = sb("spdtc", [128, 1]); ctmp = sb("ctmp", [128, 1]); ctmp2 = sb("ctmp2", [128, 1])
        posb = sb("posb", [128, SL])
        ubc = sb("ubc", [128, 16, CH], BF16)
        qt = sb("qt", [128, 11, CH])
        t1 = sb("t1", [128, CH]); t2 = sb("t2", [128, CH]); t3 = sb("t3", [128, CH])
        t4 = sb("t4", [128, CH]); t5 = sb("t5", [128, CH]); t6 = sb("t6", [128, CH])
        beto = sb("beto", [128, CH]); rgo = sb("rgo", [128, CH]); ugo = sb("ugo", [128, CH])
        a11o = sb("a11o", [128, CH]); a12o = sb("a12o", [128, CH]); vpo = sb("vpo", [128, CH])
        K0c = sb("K0c", [128, 16, CH], BF16); K1c = sb("K1c", [128, 16, CH], BF16)
        uc = sb("uc", [128, 16, CH], BF16)
        pmA = ps.enter_context(nc.psum_tensor("p6_pmA", [128, CH], F32))
        pmB = ps.enter_context(nc.psum_tensor("p6_pmB", [128, CH], F32))
        ph = Phase(g, "p6")

        def ld0(q, ph=ph):
            dma_io(ph, q, [(smb[:, :], g.ap("smallb")), (dtc[:, :], g.ap("dtc")),
                           (ropec[:, :], g.ap("ropec")),
                           (posb[:, :], g.ap("posrow").partition_broadcast(128))])
        ph.st("sync", ld0)

        wstg6 = sb("wstg6", [128, 8, 1408])
        smv = g.ap("smallT").rearrange("(kt p) n -> p kt n", p=128)
        for hf_ in range(2):
            ld_cast(ph, wstg6[:, :, :], wsm[:, hf_ * 8:hf_ * 8 + 8, :],
                    smv[:, hf_ * 8:hf_ * 8 + 8, :])

        def sp1(v):
            v.tensor_scalar_mul(ctmp[:, :], dtc[:, :], -1.0)
            v.tensor_max(ctmp[:, :], ctmp[:, :], dtc[:, :])
        ph.st("vector", sp1)

        def sp2(s):
            s.activation(ctmp[:, :], ctmp[:, :], AF.Exp, scale=-1.0)
            return s.activation(ctmp[:, :], ctmp[:, :], AF.Ln, bias=1.0)
        ph.st("scalar", sp2)

        def sp3(v):
            v.tensor_scalar_max(ctmp2[:, :], dtc[:, :], 0.0)
            v.tensor_add(spdtc[:, :], ctmp2[:, :], ctmp[:, :])
        ph.st("vector", sp3)

        for b in range(B):
            for c0 in range(0, SL, CH):
                col = b * SL + c0
                cn = CH

                def ldu(q, col=col, cn=cn, ph=ph):
                    dma_io(ph, q, [(ubc[:, :, 0:cn], g.ap("ubf")[:, :, col:col + cn])])
                ph.st("sync", ldu)

                for m in range(11):
                    pm = pmA if m % 2 == 0 else pmB
                    def mm(t, m=m, cn=cn, pm=pm):
                        return mm_k(t, pm[:, 0:cn], wsm, m * 128, 128,
                                    [ubc[:, kt, 0:cn] for kt in range(16)])
                    ph.st("tensor", mm, lag=(1 if m > 0 else 0))

                    def ev(v, m=m, cn=cn, pm=pm):
                        v.tensor_scalar_add(qt[:, m, 0:cn], pm[:, 0:cn], smb[:, m:m + 1])
                    ph.st("vector", ev)

                def ch0(s, cn=cn):
                    s.activation(beto[:, 0:cn], qt[:, 4, 0:cn], AF.Sigmoid)
                    s.activation(rgo[:, 0:cn], qt[:, 5, 0:cn], AF.Sigmoid)
                    return s.activation(ugo[:, 0:cn], qt[:, 6, 0:cn], AF.Sigmoid)
                ph.st("scalar", ch0)

                def ch1(v, cn=cn, c0=c0):
                    v.tensor_scalar_mul(t1[:, 0:cn], qt[:, 0, 0:cn], -1.0)
                    v.tensor_max(t1[:, 0:cn], t1[:, 0:cn], qt[:, 0, 0:cn])
                    v.tensor_scalar_max(t2[:, 0:cn], qt[:, 0, 0:cn], 0.0)
                    v.tensor_add(t3[:, 0:cn], qt[:, 1, 0:cn], qt[:, 2, 0:cn])
                    v.scalar_tensor_tensor(t3[:, 0:cn], posb[:, c0:c0 + cn], ropec[:, 0:1],
                                           t3[:, 0:cn], ALU.mult, ALU.add)
                    v.tensor_scalar_mul(t4[:, 0:cn], qt[:, 3, 0:cn], -1.0)
                    v.tensor_max(t4[:, 0:cn], t4[:, 0:cn], qt[:, 3, 0:cn])
                    v.tensor_scalar_max(t5[:, 0:cn], qt[:, 3, 0:cn], 0.0)
                ph.st("vector", ch1)

                def ch2(s, cn=cn):
                    s.activation(t1[:, 0:cn], t1[:, 0:cn], AF.Exp, scale=-1.0)
                    s.activation(t1[:, 0:cn], t1[:, 0:cn], AF.Ln, bias=1.0)
                    s.activation(t4[:, 0:cn], t4[:, 0:cn], AF.Exp, scale=-1.0)
                    return s.activation(t4[:, 0:cn], t4[:, 0:cn], AF.Ln, bias=1.0)
                ph.st("scalar", ch2)

                def ch3a(v, cn=cn):
                    v.tensor_add(t2[:, 0:cn], t2[:, 0:cn], t1[:, 0:cn])
                    v.tensor_add(t4[:, 0:cn], t4[:, 0:cn], t5[:, 0:cn])
                    v.tensor_scalar_mul(t5[:, 0:cn], t3[:, 0:cn], -1.0)
                    v.tensor_max(t5[:, 0:cn], t5[:, 0:cn], t3[:, 0:cn])
                    v.tensor_add(t5[:, 0:cn], t5[:, 0:cn], t2[:, 0:cn])
                    v.tensor_scalar_add(t5[:, 0:cn], t5[:, 0:cn], 1e-4)
                    v.reciprocal(t5[:, 0:cn], t5[:, 0:cn])
                    v.tensor_scalar_mul(t5[:, 0:cn], t5[:, 0:cn], spdtc[:, 0:1])
                ph.st("vector", ch3a)
                if dbg:
                    def dbg_st2(q, col=col, cn=cn, ph=ph):
                        dma_io(ph, q, [
                            (g.ap("dbg5")[:, col:col + cn], t5[:, 0:cn]),
                            (g.ap("dbg6")[:, col:col + cn], t4[:, 0:cn]),
                            (g.ap("dbg7")[:, col:col + cn], t2[:, 0:cn]),
                            (g.ap("dbg8")[:, col:col + cn], t3[:, 0:cn]),
                        ])
                    ph.st("sync", dbg_st2)

                def ch3(v, cn=cn):
                    v.tensor_add(t4[:, 0:cn], t4[:, 0:cn], t5[:, 0:cn])
                    v.tensor_scalar_mul(t4[:, 0:cn], t4[:, 0:cn], 0.5)
                    v.tensor_tensor(t2[:, 0:cn], t4[:, 0:cn], t2[:, 0:cn], ALU.mult)
                    v.tensor_tensor(t3[:, 0:cn], t4[:, 0:cn], t3[:, 0:cn], ALU.mult)
                ph.st("vector", ch3)

                if dbg:
                    def dbg_st(q, col=col, cn=cn, ph=ph):
                        dma_io(ph, q, [
                            (g.ap("dbg1")[:, col:col + cn], t2[:, 0:cn]),
                            (g.ap("dbg2")[:, col:col + cn], t3[:, 0:cn]),
                            (g.ap("dbg3")[:, col:col + cn], qt[:, 0, 0:cn]),
                            (g.ap("dbg4")[:, col:col + cn], t4[:, 0:cn]),
                        ])
                    ph.st("sync", dbg_st)

                def ch4(v, cn=cn):
                    v.tensor_scalar_add(t4[:, 0:cn], t2[:, 0:cn], 1.0)
                    v.tensor_tensor(t4[:, 0:cn], t4[:, 0:cn], t4[:, 0:cn], ALU.mult)
                    v.tensor_tensor(t5[:, 0:cn], t3[:, 0:cn], t3[:, 0:cn], ALU.mult)
                    v.tensor_scalar(t1[:, 0:cn], t2[:, 0:cn], -1.0, 1.0, ALU.mult, ALU.add)
                    v.tensor_tensor(t1[:, 0:cn], t1[:, 0:cn], t1[:, 0:cn], ALU.mult)
                    v.tensor_tensor(t6[:, 0:cn], t2[:, 0:cn], t2[:, 0:cn], ALU.mult)
                ph.st("vector", ch4)

                def ch5(v, cn=cn):
                    v.tensor_add(t4[:, 0:cn], t4[:, 0:cn], t5[:, 0:cn])
                    v.reciprocal(t4[:, 0:cn], t4[:, 0:cn])
                    v.tensor_add(t1[:, 0:cn], t1[:, 0:cn], t5[:, 0:cn])
                    v.tensor_tensor(t1[:, 0:cn], t1[:, 0:cn], t4[:, 0:cn], ALU.mult)
                    v.tensor_scalar(t6[:, 0:cn], t6[:, 0:cn], -1.0, 1.0, ALU.mult, ALU.add)
                    v.tensor_sub(t6[:, 0:cn], t6[:, 0:cn], t5[:, 0:cn])
                    v.tensor_tensor(a11o[:, 0:cn], t6[:, 0:cn], t4[:, 0:cn], ALU.mult)
                    v.scalar_tensor_tensor(a12o[:, 0:cn], t3[:, 0:cn], 2.0, t4[:, 0:cn],
                                           ALU.mult, ALU.mult)
                ph.st("vector", ch5)

                def ch6(s, cn=cn):
                    return s.activation(t1[:, 0:cn], t1[:, 0:cn], AF.Ln)
                ph.st("scalar", ch6)

                ph.st("vector", lambda v, cn=cn: v.tensor_tensor(
                    t1[:, 0:cn], rgo[:, 0:cn], t1[:, 0:cn], ALU.mult))

                ph.st("scalar", lambda s, cn=cn: s.activation(t1[:, 0:cn], t1[:, 0:cn],
                                                              AF.Exp))

                def ch7(v, cn=cn):
                    v.tensor_scalar(t1[:, 0:cn], t1[:, 0:cn], -1.0, 1.0, ALU.mult, ALU.add)
                    v.tensor_scalar_max(t1[:, 0:cn], t1[:, 0:cn], 1e-6)
                ph.st("vector", ch7)

                def sqr_act(s, cn=cn):
                    s.activation(t1[:, 0:cn], t1[:, 0:cn], AF.Ln)
                    return s.activation(t1[:, 0:cn], t1[:, 0:cn], AF.Exp, scale=0.5)
                ph.st("scalar", sqr_act)

                ph.st("vector", lambda v, cn=cn: v.tensor_tensor(
                    vpo[:, 0:cn], t1[:, 0:cn], ugo[:, 0:cn], ALU.mult))

                def ldk(q, col=col, cn=cn, ph=ph):
                    dma_io(ph, q, [(K0c[:, :, 0:cn], g.ap("K0")[:, :, col:col + cn]),
                                   (K1c[:, :, 0:cn], g.ap("K1")[:, :, col:col + cn]),
                                   (uc[:, :, 0:cn], g.ap("ubf")[:, :, col:col + cn])])
                ph.st("sync", ldk)

                def khvg(v, cn=cn):
                    sb0b = qt[:, 7, 0:cn].unsqueeze(1).broadcast_to([128, 16, cn])
                    sb1b = qt[:, 8, 0:cn].unsqueeze(1).broadcast_to([128, 16, cn])
                    vpb = vpo[:, 0:cn].unsqueeze(1).broadcast_to([128, 16, cn])
                    v.tensor_tensor(K0c[:, :, 0:cn], K0c[:, :, 0:cn], sb0b, ALU.mult)
                    v.tensor_tensor(K1c[:, :, 0:cn], K1c[:, :, 0:cn], sb1b, ALU.mult)
                    v.tensor_tensor(uc[:, :, 0:cn], uc[:, :, 0:cn], vpb, ALU.mult)
                ph.st("vector", khvg)

                def stc(q, col=col, cn=cn, ph=ph):
                    dma_io(ph, q, [
                        (g.ap("a11d")[:, col:col + cn], a11o[:, 0:cn]),
                        (g.ap("a12d")[:, col:col + cn], a12o[:, 0:cn]),
                        (g.ap("betad")[:, col:col + cn], beto[:, 0:cn]),
                        (g.ap("vpugd")[:, col:col + cn], vpo[:, 0:cn]),
                        (g.ap("sb0d")[:, col:col + cn], qt[:, 7, 0:cn]),
                        (g.ap("sb1d")[:, col:col + cn], qt[:, 8, 0:cn]),
                        (g.ap("sc0d")[:, col:col + cn], qt[:, 9, 0:cn]),
                        (g.ap("sc1d")[:, col:col + cn], qt[:, 10, 0:cn]),
                        (g.ap("Kh0")[:, :, col:col + cn], K0c[:, :, 0:cn]),
                        (g.ap("Kh1")[:, :, col:col + cn], K1c[:, :, 0:cn]),
                        (g.ap("Vg")[:, :, col:col + cn], uc[:, :, 0:cn]),
                    ])
                ph.st("sync", stc)
        ph.run()



def ph7_q(g):
    nc = g.nc
    g.dtmp("Q0", [128, 16, TL]); g.dtmp("Q1", [128, 16, TL])
    with ExitStack() as ps:
        sb = lambda n, sh, dt=F32: ps.enter_context(nc.sbuf_tensor(f"p7_{n}", sh, dt))
        wq = sb("wq", [128, 16, FF], BF16)
        ubc = sb("ubc", [128, 16, CH], BF16)
        scc = sb("scc", [128, 2, CH])
        stg0 = sb("stg0", [128, CH]); stg1 = sb("stg1", [128, CH])
        pmA = ps.enter_context(nc.psum_tensor("p7_pmA", [128, CH], F32))
        pmB = ps.enter_context(nc.psum_tensor("p7_pmB", [128, CH], F32))
        ph = Phase(g, "p7")

        wstg7 = sb("wstg7", [128, 2, FF])
        qv = g.ap("qwT").rearrange("(kt p) n -> p kt n", p=128)
        for i8 in range(8):
            ld_cast(ph, wstg7[:, :, :], wq[:, i8 * 2:i8 * 2 + 2, :],
                    qv[:, i8 * 2:i8 * 2 + 2, :])

        for b in range(B):
            for c0 in range(0, SL, CH):
                col = b * SL + c0
                cn = CH

                def ldu(q, col=col, cn=cn, ph=ph):
                    dma_io(ph, q, [(ubc[:, :, 0:cn], g.ap("ubf")[:, :, col:col + cn]),
                                   (scc[:, 0, 0:cn], g.ap("sc0d")[:, col:col + cn]),
                                   (scc[:, 1, 0:cn], g.ap("sc1d")[:, col:col + cn])])
                ph.st("sync", ldu)

                for m in range(32):
                    comp, l = m // 16, m % 16
                    pm = pmA if m % 2 == 0 else pmB
                    ss = stg0 if (m // 2) % 2 == 0 else stg1

                    def mm(t, m=m, cn=cn, pm=pm):
                        return mm_k(t, pm[:, 0:cn], wq, m * 128, 128,
                                    [ubc[:, kt, 0:cn] for kt in range(16)])
                    ph.st("tensor", mm, lag=(1 if m > 0 else 0))

                    def ev(v, comp=comp, cn=cn, pm=pm, ss=ss):
                        v.tensor_tensor(ss[:, 0:cn], pm[:, 0:cn], scc[:, comp, 0:cn],
                                        ALU.mult)
                    ph.st("vector", ev)

                    def stv(q, comp=comp, l=l, col=col, cn=cn, ss=ss, ph=ph):
                        dst = g.ap("Q0" if comp == 0 else "Q1")[:, l, col:col + cn]
                        dma_io(ph, q, [(dst, ss[:, 0:cn])])
                    ph.st("sync", stv)
        ph.run()



def ph8_scan1(g):
    nc = g.nc
    g.dtmp("pc", [B, 2, 8, 128, 8, NSUB, JG, 6])
    g.dtmp("l1subs", [B, 2, 128, 8, NSUB, 6])
    if g.mode == "ladder":
        g.dout("l1sum", [128, 16, B, 6])
    else:
        g.dtmp("l1sum", [128, 16, B, 6])
    with ExitStack() as ps:
        sb = lambda n, sh, dt=F32: ps.enter_context(nc.sbuf_tensor(f"p8_{n}", sh, dt))
        l1s = sb("l1s", [128, 16, B, 6])
        bufs = {}
        for gr in ("a", "b"):
            bufs[gr] = dict(
                kh0=sb(f"kh0{gr}", [128, 8, SL // 2], BF16),
                kh1=sb(f"kh1{gr}", [128, 8, SL // 2], BF16),
                vgg=sb(f"vgg{gr}", [128, 8, SL // 2], BF16),
                a11=sb(f"a11{gr}", [128, SL // 2]), a12=sb(f"a12{gr}", [128, SL // 2]),
                bet=sb(f"bet{gr}", [128, SL // 2]),
                fb=sb(f"fb{gr}", [128, 8, NSUB // 2, JG, 6]),
                tr0=sb(f"tr0{gr}", [128, 8, NSUB // 2]), tr1=sb(f"tr1{gr}", [128, 8, NSUB // 2]),
                tkd=sb(f"tkd{gr}", [128, 8, NSUB // 2]), tx=sb(f"tx{gr}", [128, 8, NSUB // 2]),
                ty=sb(f"ty{gr}", [128, 8, NSUB // 2]),
                Pl=sb(f"Pl{gr}", [128, 8, 4]), Slv=sb(f"Slv{gr}", [128, 8, 2]),
                cp0=sb(f"cp0{gr}", [128, 8]), cp1=sb(f"cp1{gr}", [128, 8]),
                cp2=sb(f"cp2{gr}", [128, 8]), cp3=sb(f"cp3{gr}", [128, 8]),
                subb=sb(f"subb{gr}", [128, 8, NSUB, 6]),
            )
        ph = Phase(g, "p8")

        def emit_iter(b_, hf, gr):
            bf = bufs[gr]
            eng = "vector" if gr == "a" else "gpsimd"
            dq = "sync" if gr == "a" else "scalar"
            dsem = ph.dsem if gr == "a" else ph.sdsem
            col0 = b_ * SL
            lsl = slice(hf * 8, hf * 8 + 8)
            HS = SL // 2
            HN = NSUB // 2
            for sg in range(2):
                col = col0 + sg * HS

                def ld(q, col=col, lsl=lsl, bf=bf, dsem=dsem):
                    dma_io(ph, q, [(bf["kh0"][:, :, :], g.ap("Kh0")[:, lsl, col:col + HS]),
                                   (bf["kh1"][:, :, :], g.ap("Kh1")[:, lsl, col:col + HS]),
                                   (bf["vgg"][:, :, :], g.ap("Vg")[:, lsl, col:col + HS]),
                                   (bf["a11"][:, :], g.ap("a11d")[:, col:col + HS]),
                                   (bf["a12"][:, :], g.ap("a12d")[:, col:col + HS]),
                                   (bf["bet"][:, :], g.ap("betad")[:, col:col + HS])],
                           sem=dsem)
                ph.st(dq, ld, group=gr)

                kv0 = bf["kh0"][:, :, :].rearrange("p l (s j) -> p l s j", j=LSUB)
                kv1 = bf["kh1"][:, :, :].rearrange("p l (s j) -> p l s j", j=LSUB)
                vgv = bf["vgg"][:, :, :].rearrange("p l (s j) -> p l s j", j=LSUB)
                a11v = bf["a11"][:, :].rearrange("p (s j) -> p s j", j=LSUB)
                a12v = bf["a12"][:, :].rearrange("p (s j) -> p s j", j=LSUB)
                betv = bf["bet"][:, :].rearrange("p (s j) -> p s j", j=LSUB)
                SHP = [128, 8, HN]
                fb = bf["fb"]

                for j in range(LSUB):
                    def step(v, j=j, bf=bf, kv0=kv0, kv1=kv1, vgv=vgv,
                             a11v=a11v, a12v=a12v, betv=betv, SHP=SHP, fb=fb):
                        tr0, tr1, tkd = bf["tr0"], bf["tr1"], bf["tkd"]
                        tx, ty = bf["tx"], bf["ty"]
                        k0j = kv0[:, :, :, j]
                        k1j = kv1[:, :, :, j]
                        vgj = vgv[:, :, :, j]
                        a11j = a11v[:, :, j].unsqueeze(1).broadcast_to(SHP)
                        a12j = a12v[:, :, j].unsqueeze(1).broadcast_to(SHP)
                        betj = betv[:, :, j].unsqueeze(1).broadcast_to(SHP)
                        sl_new = fb[:, :, :, j % JG, :]
                        sl_old = fb[:, :, :, (j - 1) % JG, :] if j > 0 else None
                        for pi, (i0, i1, with_in) in enumerate(((0, 2, False),
                                                               (1, 3, False),
                                                               (4, 5, True))):
                            if j == 0:
                                if pi == 0:
                                    v.tensor_copy(tr0[:, :, :], a11j)
                                    v.tensor_scalar_mul(tr1[:, :, :], a12j, -1.0)
                                elif pi == 1:
                                    v.tensor_copy(tr0[:, :, :], a12j)
                                    v.tensor_copy(tr1[:, :, :], a11j)
                                else:
                                    v.memset(tr0[:, :, :], 0.0)
                                    v.memset(tr1[:, :, :], 0.0)
                            else:
                                x0 = sl_old[:, :, :, i0]
                                x1 = sl_old[:, :, :, i1]
                                v.tensor_tensor(tx[:, :, :], a11j, x0, ALU.mult)
                                v.tensor_tensor(ty[:, :, :], a12j, x1, ALU.mult)
                                v.tensor_add(tr0[:, :, :], tx[:, :, :], ty[:, :, :])
                                v.tensor_tensor(tx[:, :, :], a11j, x1, ALU.mult)
                                v.tensor_tensor(ty[:, :, :], a12j, x0, ALU.mult)
                                v.tensor_sub(tr1[:, :, :], tx[:, :, :], ty[:, :, :])
                            v.tensor_tensor(tx[:, :, :], k0j, tr0[:, :, :], ALU.mult)
                            v.tensor_tensor(ty[:, :, :], k1j, tr1[:, :, :], ALU.mult)
                            v.tensor_add(tkd[:, :, :], tx[:, :, :], ty[:, :, :])
                            if with_in:
                                v.tensor_sub(tkd[:, :, :], vgj, tkd[:, :, :])
                            v.tensor_tensor(tkd[:, :, :], betj, tkd[:, :, :], ALU.mult)
                            op = ALU.add if with_in else ALU.subtract
                            v.tensor_tensor(tx[:, :, :], tkd[:, :, :], k0j, ALU.mult)
                            v.tensor_tensor(sl_new[:, :, :, i0], tr0[:, :, :],
                                            tx[:, :, :], op)
                            v.tensor_tensor(ty[:, :, :], tkd[:, :, :], k1j, ALU.mult)
                            v.tensor_tensor(sl_new[:, :, :, i1], tr1[:, :, :],
                                            ty[:, :, :], op)
                    ph.st(eng, step, group=gr)

                    if j % JG == JG - 1:
                        def flush(q, b_=b_, hf=hf, jg=j // JG, sg=sg, bf=bf, dsem=dsem):
                            s0 = sg * HN
                            dma_io(ph, q, [(g.ap("pc")[b_, hf, jg, :, :, s0:s0 + HN, :, :],
                                            bf["fb"][:, :, :, :, :])], sem=dsem)
                        ph.st(dq, flush, group=gr)

                def chainseg(v, b_=b_, hf=hf, sg=sg, bf=bf):
                    Pl, Slv, subb = bf["Pl"], bf["Slv"], bf["subb"]
                    cp0, cp1, cp2, cp3 = bf["cp0"], bf["cp1"], bf["cp2"], bf["cp3"]
                    st6v = bf["fb"][:, :, :, (LSUB - 1) % JG, :]
                    if sg == 0:
                        v.memset(Pl[:, :, 0], 1.0); v.memset(Pl[:, :, 1], 0.0)
                        v.memset(Pl[:, :, 2], 0.0); v.memset(Pl[:, :, 3], 1.0)
                        v.memset(Slv[:, :, :], 0.0)
                    for s in range(HN):
                        sglob = sg * HN + s
                        for i in range(4):
                            v.tensor_copy(subb[:, :, sglob, i], Pl[:, :, i])
                        for i in range(2):
                            v.tensor_copy(subb[:, :, sglob, 4 + i], Slv[:, :, i])
                        P00, P01 = st6v[:, :, s, 0], st6v[:, :, s, 1]
                        P10, P11 = st6v[:, :, s, 2], st6v[:, :, s, 3]
                        c0_, c1_ = st6v[:, :, s, 4], st6v[:, :, s, 5]
                        v.tensor_tensor(cp0[:, :], P00, Pl[:, :, 0], ALU.mult)
                        v.tensor_tensor(cp1[:, :], P01, Pl[:, :, 2], ALU.mult)
                        v.tensor_tensor(cp2[:, :], P00, Pl[:, :, 1], ALU.mult)
                        v.tensor_tensor(cp3[:, :], P01, Pl[:, :, 3], ALU.mult)
                        v.tensor_add(cp0[:, :], cp0[:, :], cp1[:, :])
                        v.tensor_add(cp2[:, :], cp2[:, :], cp3[:, :])
                        v.tensor_tensor(cp1[:, :], P10, Pl[:, :, 0], ALU.mult)
                        v.tensor_tensor(cp3[:, :], P11, Pl[:, :, 2], ALU.mult)
                        v.tensor_add(cp1[:, :], cp1[:, :], cp3[:, :])
                        v.tensor_tensor(cp3[:, :], P10, Pl[:, :, 1], ALU.mult)
                        v.tensor_copy(Pl[:, :, 0], cp0[:, :])
                        v.tensor_tensor(cp0[:, :], P11, Pl[:, :, 3], ALU.mult)
                        v.tensor_add(cp3[:, :], cp3[:, :], cp0[:, :])
                        v.tensor_copy(Pl[:, :, 1], cp2[:, :])
                        v.tensor_copy(Pl[:, :, 2], cp1[:, :])
                        v.tensor_copy(Pl[:, :, 3], cp3[:, :])
                        v.tensor_tensor(cp0[:, :], P00, Slv[:, :, 0], ALU.mult)
                        v.tensor_tensor(cp1[:, :], P01, Slv[:, :, 1], ALU.mult)
                        v.tensor_add(cp0[:, :], cp0[:, :], cp1[:, :])
                        v.tensor_add(cp0[:, :], cp0[:, :], c0_)
                        v.tensor_tensor(cp1[:, :], P10, Slv[:, :, 0], ALU.mult)
                        v.tensor_tensor(cp2[:, :], P11, Slv[:, :, 1], ALU.mult)
                        v.tensor_add(cp1[:, :], cp1[:, :], cp2[:, :])
                        v.tensor_add(cp1[:, :], cp1[:, :], c1_)
                        v.tensor_copy(Slv[:, :, 0], cp0[:, :])
                        v.tensor_copy(Slv[:, :, 1], cp1[:, :])
                ph.st(eng, chainseg, group=gr)

            def st_l1s(v, b_=b_, hf=hf, bf=bf):
                for i in range(4):
                    v.tensor_copy(l1s[:, hf * 8:hf * 8 + 8, b_, i], bf["Pl"][:, :, i])
                for i in range(2):
                    v.tensor_copy(l1s[:, hf * 8:hf * 8 + 8, b_, 4 + i], bf["Slv"][:, :, i])
            ph.st(eng, st_l1s, group=gr)

            def stsub(q, b_=b_, hf=hf, bf=bf, dsem=dsem):
                dma_io(ph, q, [(g.ap("l1subs")[b_, hf, :, :, :, :], bf["subb"][:, :, :, :])],
                       sem=dsem)
            ph.st(dq, stsub, group=gr)

        emit_iter(0, 0, "a")
        emit_iter(1, 0, "b")
        emit_iter(0, 1, "a")
        emit_iter(1, 1, "b")

        def stsum(q, ph=ph):
            dma_io(ph, q, [(g.ap("l1sum")[:, :, :, :], l1s[:, :, :, :])])
        ph.st("sync", stsum)
        ph.run()



def ph9_l1fix(g):
    nc = g.nc
    if g.mode == "ladder":
        g.din("l1sum_all", [NCORES, 128, 16, B, 6])
        g.dout("l1stats", [128, B, 2])
    else:
        g.dtmp("l1stats", [128, B, 2])
    g.dtmp("yq", [128, 16, TL])
    HSL = SL // 2
    HSUB = NSUB // 2
    with ExitStack() as ps:
        sb = lambda n, sh, dt=F32: ps.enter_context(nc.sbuf_tensor(f"p9_{n}", sh, dt))
        sall = sb("sall", [128, NCORES, 16, B, 6])
        ohbs = sb("ohbs", [128, NCORES])
        cand = sb("cand", [128, 16, B, 2, NCORES])
        ci = sb("ci", [128, 16, B, 2])
        t0 = sb("t0", [128, 16, B]); t1 = sb("t1", [128, 16, B])
        subl = sb("subl", [128, 8, NSUB, 6])
        sini = sb("sini", [128, 8, NSUB, 2])
        ta = sb("ta", [128, 8, NSUB]); tb = sb("tb", [128, 8, NSUB])
        slab = sb("slab", [128, 8, HSUB, JG, 6])
        Y0 = sb("Y0", [128, 8, HSL]); Y1 = sb("Y1", [128, 8, HSL])
        Q0b = sb("Q0b", [128, 8, HSL]); Q1b = sb("Q1b", [128, 8, HSL])
        yb = sb("yb", [128, 8, HSL])
        stt = sb("stt", [128, B, 2])
        red = sb("red", [128, 1])
        ph = Phase(g, "p9")

        def ld0(q, ph=ph):
            dma_io(ph, q, [(sall[:, :, :, :, :],
                            g.ap("l1sum_all").rearrange("c p l b q -> p c l b q")),
                           (ohbs[:, :], g.ap("ohself").partition_broadcast(128))])
        ph.st("sync", ld0)

        def chain(v):
            v.memset(cand[:, :, :, :, 0], 0.0)
            v.memset(stt[:, :, :], 0.0)
            for c in range(NCORES - 1):
                P00 = sall[:, c, :, :, 0]; P01 = sall[:, c, :, :, 1]
                P10 = sall[:, c, :, :, 2]; P11 = sall[:, c, :, :, 3]
                i0 = cand[:, :, :, 0, c]; i1 = cand[:, :, :, 1, c]
                v.tensor_tensor(t0[:, :, :], P00, i0, ALU.mult)
                v.tensor_tensor(t1[:, :, :], P01, i1, ALU.mult)
                v.tensor_add(t0[:, :, :], t0[:, :, :], t1[:, :, :])
                v.tensor_add(cand[:, :, :, 0, c + 1], t0[:, :, :], sall[:, c, :, :, 4])
                v.tensor_tensor(t0[:, :, :], P10, i0, ALU.mult)
                v.tensor_tensor(t1[:, :, :], P11, i1, ALU.mult)
                v.tensor_add(t0[:, :, :], t0[:, :, :], t1[:, :, :])
                v.tensor_add(cand[:, :, :, 1, c + 1], t0[:, :, :], sall[:, c, :, :, 5])
            v.memset(ci[:, :, :, :], 0.0)
            for c in range(NCORES):
                v.scalar_tensor_tensor(ci[:, :, :, :], cand[:, :, :, :, c],
                                       ohbs[:, c:c + 1], ci[:, :, :, :],
                                       ALU.mult, ALU.add)
        ph.st("vector", chain)

        for b in range(B):
            for hf in range(2):
                lsl = slice(hf * 8, hf * 8 + 8)

                def ldsub(q, b=b, hf=hf, ph=ph):
                    dma_io(ph, q, [(subl[:, :, :, :], g.ap("l1subs")[b, hf, :, :, :, :])])
                ph.st("sync", ldsub)

                def mksini(v, b=b, hf=hf, lsl=lsl):
                    SHP = [128, 8, NSUB]
                    ci0 = ci[:, lsl, b, 0].unsqueeze(2).broadcast_to(SHP)
                    ci1 = ci[:, lsl, b, 1].unsqueeze(2).broadcast_to(SHP)
                    v.tensor_tensor(ta[:, :, :], subl[:, :, :, 0], ci0, ALU.mult)
                    v.tensor_tensor(tb[:, :, :], subl[:, :, :, 1], ci1, ALU.mult)
                    v.tensor_add(ta[:, :, :], ta[:, :, :], tb[:, :, :])
                    v.tensor_add(sini[:, :, :, 0], ta[:, :, :], subl[:, :, :, 4])
                    v.tensor_tensor(ta[:, :, :], subl[:, :, :, 2], ci0, ALU.mult)
                    v.tensor_tensor(tb[:, :, :], subl[:, :, :, 3], ci1, ALU.mult)
                    v.tensor_add(ta[:, :, :], ta[:, :, :], tb[:, :, :])
                    v.tensor_add(sini[:, :, :, 1], ta[:, :, :], subl[:, :, :, 5])
                ph.st("vector", mksini)

                for ch2 in range(2):
                    s0 = ch2 * HSUB
                    colh = b * SL + hf * 0 + ch2 * HSL

                    for jg in range(JG):
                        def ldsl(q, b=b, hf=hf, jg=jg, s0=s0, ph=ph):
                            dma_io(ph, q, [(slab[:, :, :, :, :],
                                            g.ap("pc")[b, hf, jg, :, :, s0:s0 + HSUB, :, :])])
                        ph.st("sync", ldsl)

                        def fixup(v, jg=jg, s0=s0):
                            SH = [128, 8, HSUB, JG]
                            I0 = sini[:, :, s0:s0 + HSUB, 0].unsqueeze(3).broadcast_to(SH)
                            I1 = sini[:, :, s0:s0 + HSUB, 1].unsqueeze(3).broadcast_to(SH)
                            y0v = Y0[:, :, :].rearrange("p l (s j) -> p l s j", j=LSUB)[
                                :, :, :, jg * JG:(jg + 1) * JG]
                            y1v = Y1[:, :, :].rearrange("p l (s j) -> p l s j", j=LSUB)[
                                :, :, :, jg * JG:(jg + 1) * JG]
                            v.tensor_tensor(y0v, slab[:, :, :, :, 0], I0, ALU.mult)
                            v.scalar_tensor_tensor(y0v, slab[:, :, :, :, 1], 1.0, I1,
                                                   ALU.mult, ALU.mult) if False else None
                            v.tensor_tensor(yb[:, :, 0:HSUB * JG].rearrange(
                                "p l (s j) -> p l s j", j=JG), slab[:, :, :, :, 1], I1,
                                ALU.mult)
                            v.tensor_add(y0v, y0v, yb[:, :, 0:HSUB * JG].rearrange(
                                "p l (s j) -> p l s j", j=JG))
                            v.tensor_add(y0v, y0v, slab[:, :, :, :, 4])
                            v.tensor_tensor(y1v, slab[:, :, :, :, 2], I0, ALU.mult)
                            v.tensor_tensor(yb[:, :, 0:HSUB * JG].rearrange(
                                "p l (s j) -> p l s j", j=JG), slab[:, :, :, :, 3], I1,
                                ALU.mult)
                            v.tensor_add(y1v, y1v, yb[:, :, 0:HSUB * JG].rearrange(
                                "p l (s j) -> p l s j", j=JG))
                            v.tensor_add(y1v, y1v, slab[:, :, :, :, 5])
                        ph.st("vector", fixup)

                    col = b * SL + ch2 * HSL

                    def ldq(q, col=col, lsl=lsl, ph=ph):
                        dma_io(ph, q, [(Q0b[:, :, :], g.ap("Q0")[:, lsl, col:col + HSL]),
                                       (Q1b[:, :, :], g.ap("Q1")[:, lsl, col:col + HSL])])
                    ph.st("sync", ldq)

                    def mky(v, b=b):
                        v.tensor_tensor(yb[:, :, :], Q0b[:, :, :], Y0[:, :, :], ALU.mult)
                        v.tensor_tensor(Q1b[:, :, :], Q1b[:, :, :], Y1[:, :, :], ALU.mult)
                        v.tensor_add(yb[:, :, :], yb[:, :, :], Q1b[:, :, :])
                        v.tensor_reduce(red[:, :], yb[:, :, :], AX.XY, ALU.add)
                        v.tensor_add(stt[:, b, 0:1], stt[:, b, 0:1], red[:, :])
                        v.tensor_tensor(Q0b[:, :, :], yb[:, :, :], yb[:, :, :], ALU.mult)
                        v.tensor_reduce(red[:, :], Q0b[:, :, :], AX.XY, ALU.add)
                        v.tensor_add(stt[:, b, 1:2], stt[:, b, 1:2], red[:, :])
                    ph.st("vector", mky)

                    def sty(q, col=col, lsl=lsl, ph=ph):
                        dma_io(ph, q, [(g.ap("yq")[:, lsl, col:col + HSL], yb[:, :, :])])
                    ph.st("sync", sty)

        def stst(q, ph=ph):
            dma_io(ph, q, [(g.ap("l1stats")[:, :, :], stt[:, :, :])])
        ph.st("sync", stst)
        ph.run()



def ph10_attnout(g):
    nc = g.nc
    if g.mode == "ladder":
        g.din("stats_all", [NCORES, 128, B, 2])
    g.dtmp("sattn", [128, 8, TL])
    NTOT = float(HD * S)
    with ExitStack() as ps:
        sb = lambda n, sh, dt=F32: ps.enter_context(nc.sbuf_tensor(f"p10_{n}", sh, dt))
        sall = sb("sall", [128, NCORES, B, 2])
        stt = sb("stt", [128, B, 2])
        gmk = sb("gmk", [128, 32]); gmkT = sb("gmkT", [32, 128])
        mus32 = sb("mus32", [32, B, 2]); tmp32 = sb("tmp32", [32, B])
        musp = sb("musp", [128, B, 2])
        gnw = sb("gnw", [128, 16]); gnb = sb("gnb", [128, 16]); Dpl = sb("Dpl", [128, 16])
        wow = sb("wow", [128, 16, D], BF16)
        yc = sb("yc", [128, 16, CH])
        zc = sb("zc", [128, 16, CH], BF16)
        uc = sb("uc", [128, 16, CH])
        ybf = sb("ybf", [128, 16, CH], BF16)
        xnc = sb("xnc", [128, 8, CH])
        sac = sb("sac", [128, 8, CH])
        pmA = ps.enter_context(nc.psum_tensor("p10_pmA", [128, CH], F32))
        pmB = ps.enter_context(nc.psum_tensor("p10_pmB", [128, CH], F32))
        pst = ps.enter_context(nc.psum_tensor("p10_pst", [32, B * 2], F32))
        psb = ps.enter_context(nc.psum_tensor("p10_psb", [128, B * 2], F32))
        ph = Phase(g, "p10")

        def ld0(q, ph=ph):
            dma_io(ph, q, [
                (sall[:, :, :, :], g.ap("stats_all").rearrange("c p b q -> p c b q")),
                (gmk[:, :], g.ap("grpmask")),
                (gmkT[:, :], g.ap("grpmaskT")),
                (gnw[:, :], g.ap("gnw")), (gnb[:, :], g.ap("gnb")),
                (Dpl[:, :], g.ap("Dpl")),
            ])
        ph.st("sync", ld0)

        wstg10 = sb("wstg10", [128, 8, D])
        ov = g.ap("owT").rearrange("(kt p) n -> p kt n", p=128)
        for hf_ in range(2):
            ld_cast(ph, wstg10[:, :, :], wow[:, hf_ * 8:hf_ * 8 + 8, :],
                    ov[:, hf_ * 8:hf_ * 8 + 8, :])

        def sum_cores(v):
            v.tensor_copy(stt[:, :, :], sall[:, 0, :, :])
            for c in range(1, NCORES):
                v.tensor_add(stt[:, :, :], stt[:, :, :], sall[:, c, :, :])
        ph.st("vector", sum_cores)

        ph.st("tensor", lambda t: t.matmul(
            pst[:, :], gmk[:, :], stt[:, :, :].rearrange("p b q -> p (b q)"),
            start=True, stop=True))

        def stats32(v):
            v.tensor_scalar_mul(mus32[:, :, :], pst[:, :].rearrange(
                "p (b q) -> p b q", q=2), 1.0 / NTOT)
            v.tensor_tensor(tmp32[:, :], mus32[:, :, 0], mus32[:, :, 0], ALU.mult)
            v.tensor_sub(mus32[:, :, 1], mus32[:, :, 1], tmp32[:, :])
        ph.st("vector", stats32)

        def gn_rstd(s):
            s.activation(mus32[:, :, 1], mus32[:, :, 1], AF.Ln, bias=EPS_GN)
            return s.activation(mus32[:, :, 1], mus32[:, :, 1], AF.Exp, scale=-0.5)
        ph.st("scalar", gn_rstd)

        ph.st("tensor", lambda t: t.matmul(
            psb[:, :], gmkT[:, :], mus32[:, :, :].rearrange("p b q -> p (b q)"),
            start=True, stop=True))

        ph.st("vector", lambda v: v.tensor_copy(
            musp[:, :, :], psb[:, :].rearrange("p (b q) -> p b q", q=2)))

        for b in range(B):
            for c0 in range(0, SL, CH):
                col = b * SL + c0
                cn = CH

                def ldc(q, col=col, cn=cn, ph=ph):
                    dma_io(ph, q, [(yc[:, :, 0:cn], g.ap("yq")[:, :, col:col + cn]),
                                   (zc[:, :, 0:cn], g.ap("siluz")[:, :, col:col + cn]),
                                   (uc[:, :, 0:cn], g.ap("u")[:, :, col:col + cn]),
                                   (xnc[:, :, 0:cn], g.ap("xn1T")[:, :, col:col + cn])])
                ph.st("sync", ldc)

                def gnorm(v, b=b, cn=cn):
                    v.tensor_scalar_sub(yc[:, :, 0:cn], yc[:, :, 0:cn], musp[:, b, 0:1])
                    v.tensor_scalar_mul(yc[:, :, 0:cn], yc[:, :, 0:cn], musp[:, b, 1:2])
                    gwb = gnw[:, :].unsqueeze(2).broadcast_to([128, 16, cn])
                    gbb = gnb[:, :].unsqueeze(2).broadcast_to([128, 16, cn])
                    v.tensor_tensor(yc[:, :, 0:cn], yc[:, :, 0:cn], gwb, ALU.mult)
                    v.tensor_add(yc[:, :, 0:cn], yc[:, :, 0:cn], gbb)
                    v.tensor_tensor(yc[:, :, 0:cn], yc[:, :, 0:cn], zc[:, :, 0:cn],
                                    ALU.mult)
                    dub = Dpl[:, :].unsqueeze(2).broadcast_to([128, 16, cn])
                    v.tensor_tensor(uc[:, :, 0:cn], uc[:, :, 0:cn], dub, ALU.mult)
                    v.tensor_add(yc[:, :, 0:cn], yc[:, :, 0:cn], uc[:, :, 0:cn])
                    v.tensor_copy(ybf[:, :, 0:cn], yc[:, :, 0:cn])
                ph.st("vector", gnorm)

                for m in range(8):
                    pm = pmA if m % 2 == 0 else pmB
                    def mmo(t, m=m, cn=cn, pm=pm):
                        return mm_k(t, pm[:, 0:cn], wow, m * 128, 128,
                                    [ybf[:, kt, 0:cn] for kt in range(16)])
                    ph.st("tensor", mmo, lag=(1 if m > 0 else 0))

                    def evo(v, m=m, cn=cn, pm=pm):
                        v.tensor_add(sac[:, m, 0:cn], pm[:, 0:cn], xnc[:, m, 0:cn])
                    ph.st("vector", evo)

                def stc(q, col=col, cn=cn, ph=ph):
                    dma_io(ph, q, [(g.ap("sattn")[:, :, col:col + cn], sac[:, :, 0:cn])])
                ph.st("sync", stc)
        ph.run()



def ph11_moe(g):
    nc = g.nc
    g.dtmp("cT", [NE, TL], BF16)
    g.dtmp("g1", [128, 8, TL])
    with ExitStack() as ps:
        sb = lambda n, sh, dt=F32: ps.enter_context(nc.sbuf_tensor(f"p11_{n}", sh, dt))
        accbf = sb("accbf", [128, 8, TL], BF16)
        bce = sb("bce", [128, TL], BF16)
        wstg11 = sb("wstg11", [128, 4096])
        wstg_a = wstg11.ap().rearrange("p (a n) -> p a n", n=2048)
        wstg_b = wstg11.ap().rearrange("p (a n) -> p a n", n=1024)
        wstg_c = wstg11.ap().rearrange("p (a n) -> p a n", n=512)
        wstg_f = wstg11.ap().rearrange("p (a n) -> p a n", n=CH)
        xbc = sb("xbc", [128, 2, 8, CH], BF16)
        tmpc = sb("tmpc", [128, CH])
        pmA = ps.enter_context(nc.psum_tensor("p11_pmA", [128, CH], F32))
        pmB = ps.enter_context(nc.psum_tensor("p11_pmB", [128, CH], F32))
        ph = Phase(g, "p11")

        with ExitStack() as gs:
            sg_ = lambda n, sh, dt=F32: gs.enter_context(nc.sbuf_tensor(f"p11g_{n}", sh, dt))
            gw = sg_("gw", [128, 8, NE], BF16)
            ident = sg_("ident", [128, 128])
            lgt = sg_("lgt", [NE, TL])
            ltok = sg_("ltok", [128, NE])
            m1n = sg_("m1n", [128, 1]); m2 = sg_("m2", [128, 1])
            eqv = sg_("eqv", [128, NE]); l2v = sg_("l2v", [128, NE])
            selv = sg_("selv", [128, NE]); ev = sg_("ev", [128, NE])
            ssum = sg_("ssum", [128, 1])
            ctok = sg_("ctok", [128, NE])
            ctt = sg_("ctt", [NE, TL], BF16)
            pmt = gs.enter_context(nc.psum_tensor("p11_pmt", [128, NE], F32))
            pmt2 = gs.enter_context(nc.psum_tensor("p11_pmt2", [128, 128], F32))

            def ld0(q, ph=ph):
                dma_io(ph, q, [(ident[:, :], g.ap("identf"))])
            ph.st("sync", ld0)

            ld_cast(ph, wstg_c[:, 0:8, 0:NE], gw[:, :, :],
                    g.ap("gwT").rearrange("(kt p) n -> p kt n", p=128))

            for ci in range(NCH):
                col = ci * CH

                def ldx(q, col=col, ph=ph):
                    dma_io(ph, q, [(xbc[:, 0, :, :], g.ap("xn1bf")[:, :, col:col + CH])])
                ph.st("sync", ldx)

                def mmg(t, col=col):
                    return mm_k(t, pmA[0:NE, 0:CH], gw, 0, NE,
                                [xbc[:, 0, kt, :] for kt in range(8)])
                ph.st("tensor", mmg)

                def evg(v, col=col):
                    v.tensor_copy(lgt[:, col:col + CH], pmA[0:NE, 0:CH])
                ph.st("vector", evg)

            for tt in range(TL // 128):
                def tpl(t, tt=tt):
                    return t.transpose(pmt[:, 0:NE], lgt[0:NE, tt * 128:(tt + 1) * 128],
                                       ident[0:NE, 0:NE])
                ph.st("tensor", tpl)

                def top2(v, tt=tt):
                    v.tensor_copy(ltok[:, :], pmt[:, 0:NE])
                    v.tensor_reduce(m1n[:, :], ltok[:, :], AX.X, ALU.max, negate=True)
                    v.tensor_scalar(eqv[:, :], ltok[:, :], m1n[:, 0:1], None, ALU.add)
                    v.tensor_single_scalar(eqv[:, :], eqv[:, :], 0.0, ALU.is_ge)
                    v.scalar_tensor_tensor(l2v[:, :], eqv[:, :], -1e30, ltok[:, :],
                                           ALU.mult, ALU.add)
                    v.tensor_reduce(m2[:, :], l2v[:, :], AX.X, ALU.max)
                    v.tensor_scalar(selv[:, :], ltok[:, :], m2[:, 0:1], None, ALU.is_ge)
                ph.st("vector", top2)

                def expn(s, tt=tt):
                    return s.activation(ev[:, :], ltok[:, :], AF.Exp, bias=m1n[:, 0:1])
                ph.st("scalar", expn)

                def wts(v, tt=tt):
                    v.tensor_tensor(ev[:, :], ev[:, :], selv[:, :], ALU.mult)
                    v.tensor_reduce(ssum[:, :], ev[:, :], AX.X, ALU.add)
                    v.reciprocal(ssum[:, :], ssum[:, :])
                    v.tensor_scalar_mul(ctok[:, :], ev[:, :], ssum[:, 0:1])
                ph.st("vector", wts)

                def tpc(t, tt=tt):
                    return t.transpose(pmt2[0:NE, :], ctok[:, :], ident[:, :])
                ph.st("tensor", tpc)

                def evc(v, tt=tt):
                    v.tensor_copy(ctt[:, tt * 128:(tt + 1) * 128], pmt2[0:NE, :])
                ph.st("vector", evc)

            def stct(q, ph=ph):
                dma_io(ph, q, [(g.ap("cT")[:, :], ctt[:, :])])
            ph.st("sync", stct)

        es = ps
        w1e = sb("w1e", [128, 8, FE], BF16)
        w2e = sb("w2e", [128, 16, D], BF16)
        b1e = sb("b1e", [128, 16]); b2e = sb("b2e", [128, 8])
        hh = sb("hh", [128, 16, CH], BF16)
        outb = sb("outb", [128, 2, 8, CH], BF16)
        gchunk = 0
        for e in range(NE):
            w1v = g.ap("moew1")[e].rearrange("(kt p) n -> p kt n", p=128)
            w2v = g.ap("moew2")[e].rearrange("(kt p) n -> p kt n", p=128)
            for hf_ in range(2):
                ld_cast(ph, wstg_a[:, 0:2, :], w1e[:, hf_ * 4:hf_ * 4 + 2, :],
                        w1v[:, hf_ * 4:hf_ * 4 + 2, :])
                ld_cast(ph, wstg_a[:, 0:2, :], w1e[:, hf_ * 4 + 2:hf_ * 4 + 4, :],
                        w1v[:, hf_ * 4 + 2:hf_ * 4 + 4, :])
                ld_cast(ph, wstg_b[:, 0:4, :], w2e[:, hf_ * 8:hf_ * 8 + 4, :],
                        w2v[:, hf_ * 8:hf_ * 8 + 4, :])
                ld_cast(ph, wstg_b[:, 0:4, :], w2e[:, hf_ * 8 + 4:hf_ * 8 + 8, :],
                        w2v[:, hf_ * 8 + 4:hf_ * 8 + 8, :])

            def ldbe(q, e=e, ph=ph):
                dma_io(ph, q, [(b1e[:, :], g.ap("moeb1")[e]), (b2e[:, :], g.ap("moeb2")[e]),
                               (bce[:, :], g.ap("cT")[e:e + 1, :].partition_broadcast(128))])
            ph.st("sync", ldbe)

            for ci in range(NCH):
                col = ci * CH
                par = gchunk % 2
                gchunk += 1
                first = (e == 0 and ci == 0)

                if ci == 0:
                    def ldx2(q, col=col, par=par, ph=ph):
                        dma_io(ph, q, [(xbc[:, par, :, :],
                                        g.ap("xn1bf")[:, :, col:col + CH])])
                    ph.st("sync", ldx2)

                for fm in range(16):
                    pm = pmA if fm % 2 == 0 else pmB
                    def mm1(t, fm=fm, pm=pm, par=par):
                        return mm_k(t, pm[:, :], w1e, fm * 128, 128,
                                    [xbc[:, par, kt, :] for kt in range(8)])
                    ph.st("tensor", mm1, lag=(1 if fm > 0 else (2 if not first else 0)))

                    def ev1(s, fm=fm, pm=pm):
                        return s.activation(hh[:, fm, :], pm[:, :], AF.Gelu,
                                            bias=b1e[:, fm:fm + 1])
                    ph.st("scalar", ev1)

                for dm in range(8):
                    pm = pmA if dm % 2 == 0 else pmB
                    def mm2(t, dm=dm, pm=pm):
                        return mm_k(t, pm[:, :], w2e, dm * 128, 128,
                                    [hh[:, kt, :] for kt in range(16)])
                    ph.st("tensor", mm2, lag=(2 if dm > 0 else 0))

                    def ev2(s, dm=dm, pm=pm, par=par):
                        return s.activation(outb[:, par, dm, :], pm[:, :], AF.Identity,
                                            bias=b2e[:, dm:dm + 1])
                    ph.st("scalar", ev2)

                    def combdm(v, e=e, dm=dm, col=col, par=par):
                        v.tensor_tensor(tmpc[:, :], outb[:, par, dm, :],
                                        bce[:, col:col + CH], ALU.mult)
                        if e == 0:
                            v.tensor_copy(accbf[:, dm, col:col + CH], tmpc[:, :])
                        else:
                            v.tensor_add(accbf[:, dm, col:col + CH],
                                         accbf[:, dm, col:col + CH], tmpc[:, :])
                    ph.st("vector", combdm)

                    if dm == 3 and ci < NCH - 1:
                        ncol = (ci + 1) * CH
                        npar = gchunk % 2
                        def ldxn(q, ncol=ncol, npar=npar, ph=ph):
                            dma_io(ph, q, [(xbc[:, npar, :, :],
                                            g.ap("xn1bf")[:, :, ncol:ncol + CH])])
                        ph.st("sync", ldxn, lag=8)

        for ci in range(NCH):
            col = ci * CH

            def ldf(q, col=col, ph=ph):
                dma_io(ph, q, [(wstg_f[:, :, :], g.ap("xn1T")[:, :, col:col + CH])])
            ph.st("sync", ldf)

            def fin(v, col=col):
                v.tensor_add(wstg_f[:, :, :], wstg_f[:, :, :], accbf[:, :, col:col + CH])
                v.tensor_scalar(wstg_f[:, :, :], wstg_f[:, :, :], 100.0, -100.0,
                                ALU.min, ALU.max)
            ph.st("vector", fin)

            ph.st("scalar", lambda s: s.activation(wstg_f[:, :, :], wstg_f[:, :, :],
                                                   AF.Silu))

            def stg1(q, col=col, ph=ph):
                dma_io(ph, q, [(g.ap("g1")[:, :, col:col + CH], wstg_f[:, :, :])])
            ph.st("sync", stg1)
        ph.run()



def ph12_out(g):
    nc = g.nc
    g.dout("out", [B, SL, D])
    with ExitStack() as ps:
        sb = lambda n, sh, dt=F32: ps.enter_context(nc.sbuf_tensor(f"p12_{n}", sh, dt))
        ident = sb("ident", [128, 128])
        x1c = sb("x1c", [128, 8, CH])
        sac = sb("sac", [128, 8, CH])
        g1c = sb("g1c", [128, 8, CH])
        otok = sb("otok", [128, 4, D])
        ptA = ps.enter_context(nc.psum_tensor("p12_ptA", [128, 8, 128], F32))
        ph = Phase(g, "p12")

        def ld0(q, ph=ph):
            dma_io(ph, q, [(ident[:, :], g.ap("identf"))])
        ph.st("sync", ld0)

        for b in range(B):
            for c0 in range(0, SL, CH):
                col = b * SL + c0

                def ldc(q, col=col, ph=ph):
                    dma_io(ph, q, [(x1c[:, :, :], g.ap("x1T")[:, :, col:col + CH]),
                                   (sac[:, :, :], g.ap("sattn")[:, :, col:col + CH]),
                                   (g1c[:, :, :], g.ap("g1")[:, :, col:col + CH])])
                ph.st("sync", ldc)

                def fin(v):
                    v.tensor_tensor(sac[:, :, :], sac[:, :, :], g1c[:, :, :], ALU.mult)
                    v.tensor_add(x1c[:, :, :], x1c[:, :, :], sac[:, :, :])
                ph.st("vector", fin)

                for tt in range(4):
                    def tpo(t, tt=tt):
                        r = None
                        for dt in range(8):
                            r = t.transpose(ptA[:, dt, :], x1c[:, dt, tt * 128:(tt + 1) * 128],
                                            ident[:, :])
                        return r
                    ph.st("tensor", tpo)

                    def evo(v, tt=tt):
                        v.tensor_copy(otok[:, tt, :],
                                      ptA[:, :, :].rearrange("a b c -> a (b c)")
                                      if False else ptA[:, :, :].rearrange("p d t -> p (d t)"))
                    ph.st("vector", evo)

                def sto(q, b=b, c0=c0, ph=ph):
                    dma_io(ph, q, [(g.ap("out")[b, c0:c0 + CH, :].rearrange(
                        "(t p) d -> p t d", p=128), otok[:, :, :])])
                ph.st("sync", sto)
        ph.run()



def _mk_cc(src_name, dst_name, shape):
    def cc(g):
        nc = g.nc
        g.dtmp(dst_name, [NCORES] + list(shape))
        sem = g.sem(f"cc_{src_name}")
        with nc.Block() as blk:
            @blk.gpsimd
            def _(q):
                q.collective_compute(
                    "AllGather", ALU.bypass,
                    replica_groups=[list(range(NCORES))],
                    ins=[g.ap(src_name).opt()],
                    outs=[g.ap(dst_name).opt()],
                ).then_inc(sem.h)
                q.wait_ge(sem.h, 1)
    cc.__name__ = f"cc_{src_name}"
    return cc


PHASES = [ph0_ternary, ph1_l0a, ph2_ffn0, ph3_l0scan, ph4_l0fix, ph5_inproj, ph6_chains, ph7_q, ph8_scan1, ph9_l1fix, ph10_attnout, ph11_moe, ph12_out]

FUSED_PHASES = [ph0_ternary, ph1_l0a, ph2_ffn0, ph3_l0scan,
                _mk_cc("l0sum", "l0sum_all", [128, 8, B, 10]),
                ph4_l0fix, ph5_inproj, ph6_chains, ph7_q, ph8_scan1,
                _mk_cc("l1sum", "l1sum_all", [128, 16, B, 6]),
                ph9_l1fix,
                _mk_cc("l1stats", "stats_all", [128, B, 2]),
                ph10_attnout, ph11_moe, ph12_out]


def declare_inputs(g):
    names_2d = {
        "wkT": [D, D], "wvT": [D, D], "wrT": [D, D], "woT": [D, D],
        "w1T": [D, FF], "w2T": [FF, D], "b1f": [128, 32], "b2f": [128, 8],
        "tdec": [128, 8], "tfirst": [128, 8], "tmk": [128, 8], "tmv": [128, 8],
        "tmr": [128, 8], "ipT": [D, 8192], "ipb": [128, 64], "rmsw8": [128, 8],
        "cb": [128, 16], "Dpl": [128, 16], "gnw": [128, 16], "gnb": [128, 16],
        "smallT": [DI, 1408], "smallb": [128, 11], "dtc": [128, 1], "ropec": [128, 1],
        "qwT": [DI, FF], "owT": [DI, D], "gwT": [D, NE],
        "identf": [128, 128], "tloc": [1, SL], "grpmaskT": [32, 128],
        "grpmask": [128, 32],
        "posrow": [1, SL], "ohself": [1, NCORES], "ohprev": [1, NCORES],
        "halomask": [1, 1],
    }
    for n, sh in names_2d.items():
        g.din(n, sh)
    g.din("x", [B, SL + 5, D])
    g.din("cw", [128, 16, 4])
    g.din("moew1", [NE, D, FE]); g.din("moew2", [NE, FE, D])
    g.din("moeb1", [NE, 128, 16]); g.din("moeb2", [NE, 128, 8])


def build(upto, mode="ladder", carries=(), dev_out=()):
    nc = bass.Bass("TRN2", target_bir_lowering=False)
    g = G(nc, mode, dev_out)
    declare_inputs(g)
    if "l0" in carries:
        g.din("l0sum_all", [NCORES, 128, 8, B, 10])
    if "l1" in carries:
        g.din("l1sum_all", [NCORES, 128, 16, B, 6])
    if "gs" in carries:
        g.din("stats_all", [NCORES, 128, B, 2])
    phl = FUSED_PHASES if mode == "fused" else PHASES
    with g.stack:
        g.init_consts([EPS_LN, EPS_RMS, EPS_GN, 1.0, 1e-4, 1e-6])
        for f in phl[:upto]:
            f(g)
    return g


def kernel(**inputs):
    cores = stage_host(inputs)
    g = build(len(FUSED_PHASES), mode="fused")
    res = run_bass_kernel_spmd(g.nc, [dict(c) for c in cores],
                               core_ids=list(range(NCORES)))
    out = np.zeros((B, S, D), np.float32)
    for c in range(NCORES):
        out[:, c * SL:(c + 1) * SL, :] = res.results[c]["out"]
    return out


# version 16 (final)
# speedup vs baseline: 1.1449x; 1.1449x over previous
import numpy as np
import ml_dtypes
from contextlib import ExitStack
from collections import defaultdict
import concourse.bass as bass
import concourse.mybir as mybir
from concourse.bass_utils import run_bass_kernel_spmd

F32 = mybir.dt.float32
BF16 = mybir.dt.bfloat16
AF = mybir.ActivationFunctionType
ALU = mybir.AluOpType
AX = mybir.AxisListType
BF = ml_dtypes.bfloat16

NCORES = 8
B = 2
S = 16384
D = 1024
SL = S // NCORES
TL = B * SL
DI = 2048
H = 32
HD = 64
FF = 4096
NE = 14
FE = 2048
CH = 512
CPB = SL // CH
NCH = TL // CH
LSUB = 64
NSUB = SL // LSUB
JG = 8
CHAINED = True
EPS_LN = 1e-5
EPS_RMS = 1e-6
EPS_GN = 1e-5



def _plane_perm():
    new2old = np.zeros(DI, dtype=np.int64)
    for di in range(DI):
        h, hd = divmod(di, HD)
        new2old[(hd % 16) * 128 + h * 4 + hd // 16] = di
    return new2old


def _dup4(a):
    return np.repeat(np.asarray(a, np.float32), 4, axis=0)


def _ktile(vec):
    v = np.asarray(vec, np.float32)
    return np.ascontiguousarray(v.reshape(v.shape[0] // 128, 128).T)


def _cc(a, dt=np.float32):
    return np.ascontiguousarray(np.asarray(a, dtype=dt))


def stage_host(inputs):
    w = {}
    p = _plane_perm()
    w["wkT"] = _cc(np.asarray(inputs["L0_attn_Wk"]).T)
    w["wvT"] = _cc(np.asarray(inputs["L0_attn_Wv"]).T)
    w["wrT"] = _cc(np.asarray(inputs["L0_attn_Wr"]).T)
    w["woT"] = _cc(np.asarray(inputs["L0_attn_Wo"]).T)
    w["w1T"] = _cc(np.asarray(inputs["L0_ffn_W1"]).T)
    w["w2T"] = _cc(np.asarray(inputs["L0_ffn_W2"]).T)
    w["b1f"] = _ktile(inputs["L0_ffn_b1"])
    w["b2f"] = _ktile(inputs["L0_ffn_b2"])
    w["tdec"] = _ktile(inputs["L0_attn_time_decay"])
    w["tfirst"] = _ktile(inputs["L0_attn_time_first"])
    w["tmk"] = _ktile(inputs["L0_attn_time_mix_k"])
    w["tmv"] = _ktile(inputs["L0_attn_time_mix_v"])
    w["tmr"] = _ktile(inputs["L0_attn_time_mix_r"])

    ip = np.asarray(inputs["L1_attn_in_proj_W"])
    order = np.concatenate([p, 2048 + 2 * p, 2048 + 2 * p + 1, 6144 + p])
    w["ipT"] = _cc(ip[order, :].T)
    w["ipb"] = _cc(np.asarray(inputs["L1_attn_in_proj_b"])[order].reshape(64, 128).T)
    w["rmsw8"] = _ktile(inputs["L1_attn_rms_w"])

    def pl(vec):
        return _cc(np.asarray(vec, np.float32)[p].reshape(16, 128).T)
    w["cw"] = _cc(np.asarray(inputs["L1_attn_conv_w"])[p].reshape(16, 128, 4).transpose(1, 0, 2))
    w["cb"] = pl(inputs["L1_attn_conv_b"])
    w["Dpl"] = pl(inputs["L1_attn_D"])
    w["gnw"] = pl(inputs["L1_attn_gn_w"])
    w["gnb"] = pl(inputs["L1_attn_gn_b"])

    dyn = np.asarray(inputs["L1_attn_dyn_W"])
    SM = np.concatenate([
        _dup4(dyn[0:32]), _dup4(dyn[32:64]), _dup4(dyn[64:96]),
        _dup4(inputs["L1_attn_seldt_W"]), _dup4(inputs["L1_attn_beta_W"]),
        _dup4(inputs["L1_attn_rg_W"]), _dup4(inputs["L1_attn_ug_W"]),
        _dup4(np.asarray(inputs["L1_attn_selB_W"])[0::2]),
        _dup4(np.asarray(inputs["L1_attn_selB_W"])[1::2]),
        _dup4(np.asarray(inputs["L1_attn_selC_W"])[0::2]),
        _dup4(np.asarray(inputs["L1_attn_selC_W"])[1::2]),
    ], axis=0)
    w["smallT"] = _cc(SM[:, p].T)
    dynb = np.asarray(inputs["L1_attn_dyn_b"])
    small_b = np.concatenate([
        _dup4(dynb[0:32]), _dup4(dynb[32:64]), _dup4(dynb[64:96]),
        np.zeros(128, np.float32),
        _dup4(inputs["L1_attn_beta_b"]), _dup4(inputs["L1_attn_rg_b"]),
        _dup4(inputs["L1_attn_ug_b"]), np.zeros(512, np.float32),
    ]).astype(np.float32)
    w["smallb"] = _cc(small_b.reshape(11, 128).T)
    w["dtc"] = _cc(_dup4(inputs["L1_attn_dt_c"]).reshape(128, 1))
    rope = (1.0 / 10000.0 ** (np.arange(H, dtype=np.float64) / H)).astype(np.float32)
    w["ropec"] = _cc(_dup4(rope).reshape(128, 1))

    qw = np.asarray(inputs["L1_attn_Q_W"])
    qorder = np.concatenate([2 * p, 2 * p + 1])
    w["qwT"] = _cc(qw[qorder, :][:, p].T)
    w["owT"] = _cc(np.asarray(inputs["L1_attn_out_W"])[:, p].T)

    w["gwT"] = _cc(np.asarray(inputs["L1_ffn_gate_w"]).T)
    w["moew1"] = _cc(np.asarray(inputs["L1_ffn_w1"]))
    w["moew2"] = _cc(np.asarray(inputs["L1_ffn_w2"]))
    w["moeb1"] = _cc(np.asarray(inputs["L1_ffn_b1"]).reshape(NE, 16, 128).transpose(0, 2, 1))
    w["moeb2"] = _cc(np.asarray(inputs["L1_ffn_b2"]).reshape(NE, 8, 128).transpose(0, 2, 1))

    w["identf"] = np.eye(128, dtype=np.float32)
    w["tloc"] = np.arange(SL, dtype=np.float32).reshape(1, SL)
    gm = np.zeros((128, 32), np.float32)
    gm[np.arange(128), np.arange(128) // 4] = 1.0
    w["grpmask"] = _cc(gm)
    w["grpmaskT"] = _cc(gm.T)

    x = np.asarray(inputs["L0_attn_x"], dtype=np.float32)
    cores = []
    for c in range(NCORES):
        m = dict(w)
        xs = np.zeros((B, SL + 5, D), np.float32)
        lo = c * SL
        if c > 0:
            xs[:, 0:5, :] = x[:, lo - 5:lo, :]
        xs[:, 5:, :] = x[:, lo:lo + SL, :]
        m["x"] = _cc(xs)
        m["posrow"] = _cc(np.arange(lo, lo + SL, dtype=np.float32).reshape(1, SL))
        oh = np.zeros((1, NCORES), np.float32); oh[0, c] = 1.0
        m["ohself"] = oh
        ohp = np.zeros((1, NCORES), np.float32)
        if c > 0:
            ohp[0, c - 1] = 1.0
        m["ohprev"] = ohp
        m["halomask"] = np.full((1, 1), 1.0 if c > 0 else 0.0, np.float32)
        cores.append(m)
    return cores



class Sem:
    def __init__(self, nc, stack, name):
        self.h = stack.enter_context(nc.semaphore(name))
        self.v = 0

    def inc(self, instr, n=1):
        instr.then_inc(self.h, n)
        self.v += n
        return instr

    def dma(self, instr):
        return self.inc(instr, 16)

    def wait(self, eng, v=None):
        eng.wait_ge(self.h, self.v if v is None else v)


class G:
    def __init__(self, nc, mode="ladder", dev_out=()):
        self.nc = nc
        self.mode = mode
        self.dev_out = set(dev_out)
        self.stack = ExitStack()
        self.d = {}
        self._sn = 0

    def sem(self, name):
        self._sn += 1
        return Sem(self.nc, self.stack, f"{name}_{self._sn}")

    def din(self, name, shape, dtype=F32):
        if name not in self.d:
            self.d[name] = self.nc.dram_tensor(name, list(shape), dtype, kind="ExternalInput")
        return self.d[name]

    def dout(self, name, shape, dtype=F32):
        self.d[name] = self.nc.dram_tensor(name, list(shape), dtype, kind="ExternalOutput")
        return self.d[name]

    def dtmp(self, name, shape, dtype=F32):
        if name not in self.d:
            if name in self.dev_out:
                self.d[name] = self.nc.dram_tensor(name, list(shape), dtype,
                                                   kind="ExternalOutput")
            else:
                self.d[name] = self.nc.dram_tensor(name, list(shape), dtype)
        return self.d[name]

    def ap(self, name):
        return self.d[name].ap()

    def init_consts(self, values):
        n = len(values)
        self._ctile = self.stack.enter_context(self.nc.sbuf_tensor("consts", [128, n], F32))
        with self.nc.Block() as blk:
            @blk.vector
            def _(v):
                for i, val in enumerate(values):
                    v.memset(self._ctile[:, i:i + 1], float(val))
        for i, val in enumerate(values):
            self.nc.const_aps.register(F32, float(val), self._ctile[:, i:i + 1])


class EngProxy:

    _PASS = {"wait_ge", "sem_inc", "dma_start", "indirect_dma_start",
             "collective_compute", "nop", "engine_nop", "wait_op"}

    def __init__(self, eng, sem, chained=True):
        self._e = eng
        self._s = sem
        self._chained = chained
        self.last = None

    def __getattr__(self, name):
        fn = getattr(self._e, name)
        if name.startswith("_") or name in EngProxy._PASS or not callable(fn):
            return fn

        def wrapped(*a, **k):
            if self._chained and self._s.v > 0:
                self._e.wait_ge(self._s.h, self._s.v)
            ins = fn(*a, **k)
            self.last = ins
            if self._chained:
                self._s.inc(ins, 1)
            return ins
        return wrapped


class Phase:

    def __init__(self, g, name):
        self.g = g
        self.name = name
        self.items = []
        if not hasattr(g, "physems"):
            g.physems = dict(
                d=g.sem("d"), sd=g.sem("sd"), gd=g.sem("gd"),
                tensor=g.sem("ct"), vector=g.sem("cv"),
                scalar=g.sem("cs"), gpsimd=g.sem("cg"))
        self.dsem = g.physems["d"]
        self.sdsem = g.physems["sd"]
        self.gdsem = g.physems["gd"]
        self.chain = {e: g.physems[e] for e in
                      ("tensor", "vector", "scalar", "gpsimd")}
        self.groups = {None: None,
                       "a": {"vector", "d"},
                       "b": {"gpsimd", "sd"}}

    def st(self, engine, fn, lag=0, group=None):
        self.items.append((engine, fn, lag, group))

    def run(self):
        nc = self.g.nc
        seen = {e: {} for e in ("sync", "tensor", "vector", "scalar", "gpsimd")}
        hist = []
        with nc.Block() as blk:
            for en, fn, lag, grp in self.items:
                allsems = (list(self.chain.items())
                           + [("d", self.dsem), ("sd", self.sdsem), ("gd", self.gdsem)])
                hist.append({k: s.v for k, s in allsems})
                marks = hist[max(0, len(hist) - 1 - lag)]
                gkeys = self.groups.get(grp)

                def body(q, fn=fn, en=en, marks=marks, allsems=allsems, gkeys=gkeys):
                    for key, s in allsems:
                        if key == en:
                            continue
                        if gkeys is not None and key not in gkeys:
                            continue
                        tgt = marks[key]
                        if tgt > 0 and seen[en].get(key, 0) < tgt:
                            q.wait_ge(s.h, tgt)
                            seen[en][key] = tgt
                    if en == "sync":
                        fn(q)
                    elif CHAINED and en != "tensor":
                        px = EngProxy(q, self.chain[en], chained=True)
                        fn(px)
                    else:
                        px = EngProxy(q, self.chain[en], chained=False)
                        fn(px)
                        if px.last is not None:
                            self.chain[en].inc(px.last, 1)
                getattr(blk, en)(body)


def ld_cast(ph, stage_ap, dst_ap, src_ap):
    def ldf(q, ph=ph):
        dma_io(ph, q, [(stage_ap, src_ap)])
    ph.st("sync", ldf)

    def cst(v):
        v.tensor_copy(dst_ap, stage_ap)
    ph.st("vector", cst)


def dma_io(ph, q, pairs, sw=False, sem=None):
    s = sem if sem is not None else (ph.gdsem if sw else ph.dsem)
    for out, in_ in pairs:
        s.dma(q.dma_start(out=out, in_=in_))
    s.wait(q)


def mm_k(t, psum_ap, wsb, m_lo, m_n, rhs_slices):
    n = len(rhs_slices)
    r = None
    for kt, rh in enumerate(rhs_slices):
        r = t.matmul(psum_ap, wsb[:, kt, m_lo:m_lo + m_n], rh,
                     start=(kt == 0), stop=(kt == n - 1))
    return r


def _chunks(ch=CH):
    out = []
    for b in range(B):
        out.append((b, -4, 4, True))
        for i in range(SL // ch):
            out.append((b, i * ch, ch, False))
    return out



def ph0_ternary(g):
    nc = g.nc
    g.dtmp("w1t_bf", [D, FF], BF16)
    g.dtmp("w2t_bf", [FF, D], BF16)
    for (src, dst, rows, cols) in (("w1T", "w1t_bf", D, FF), ("w2T", "w2t_bf", FF, D)):
        nt = rows // 128
        with ExitStack() as ps:
            sb = lambda n, sh, dt=F32: ps.enter_context(nc.sbuf_tensor(f"t{src}_{n}", sh, dt))
            wt = sb("wt", [128, nt, cols])
            qb = sb("qb", [128, cols])
            mk = sb("mk", [128, cols])
            ob = sb("ob", [128, cols], BF16)
            acc = sb("acc", [128, 1])
            red = sb("red", [128, 1])
            ones = sb("ones", [128, 1])
            onesr = sb("onesr", [1, 128])
            s0 = sb("s0", [1, 1])
            rs = sb("rs", [128, 1])
            pt1 = ps.enter_context(nc.psum_tensor(f"t{src}_p1", [1, 1], F32))
            pt2 = ps.enter_context(nc.psum_tensor(f"t{src}_p2", [128, 1], F32))
            ph = Phase(g, f"t{src}")

            def in_fn(q, src=src, wt=wt, nt=nt, ph=ph):
                dma_io(ph, q, [(wt[:, i, :], g.ap(src)[i * 128:(i + 1) * 128, :])
                               for i in range(nt)])
            ph.st("sync", in_fn)

            def red_fn(v, wt=wt, nt=nt, acc=acc, red=red, ones=ones, onesr=onesr):
                v.memset(acc[:, :], 0.0)
                v.memset(ones[:, :], 1.0)
                v.memset(onesr[:, :], 1.0)
                for i in range(nt):
                    v.tensor_reduce(red[:, :], wt[:, i, :], AX.X, ALU.add,
                                    apply_absolute_value=True)
                    r = v.tensor_add(acc[:, :], acc[:, :], red[:, :])
                return r
            ph.st("vector", red_fn)

            ph.st("tensor", lambda t, pt1=pt1, ones=ones, acc=acc:
                  t.matmul(pt1[:, :], ones[:, :], acc[:, :], start=True, stop=True))

            def sc_fn(v, s0=s0, pt1=pt1, rows=rows, cols=cols):
                v.tensor_scalar(s0[:, :], pt1[:, :], 1.0 / (rows * cols), 1e-8,
                                ALU.mult, ALU.max)
                return v.reciprocal(s0[:, :], s0[:, :])
            ph.st("vector", sc_fn)

            ph.st("tensor", lambda t, pt2=pt2, onesr=onesr, s0=s0:
                  t.matmul(pt2[:, :], onesr[:, :], s0[:, :], start=True, stop=True))

            ph.st("vector", lambda v, rs=rs, pt2=pt2: v.tensor_copy(rs[:, :], pt2[:, :]))

            for i in range(nt):
                def tern_fn(v, i=i, wt=wt, qb=qb, mk=mk, ob=ob, rs=rs):
                    v.tensor_scalar_mul(qb[:, :], wt[:, i, :], rs[:, 0:1])
                    v.tensor_scalar_mul(mk[:, :], qb[:, :], -1.0)
                    v.tensor_max(mk[:, :], mk[:, :], qb[:, :])
                    v.tensor_single_scalar(mk[:, :], mk[:, :], 0.5, ALU.is_ge)
                    v.tensor_single_scalar(qb[:, :], qb[:, :], 0.0, ALU.is_gt)
                    v.tensor_scalar(qb[:, :], qb[:, :], 2.0, -1.0, ALU.mult, ALU.add)
                    return v.tensor_tensor(ob[:, :], qb[:, :], mk[:, :], ALU.mult)
                ph.st("vector", tern_fn)

                def out_fn(q, i=i, dst=dst, ob=ob, ph=ph):
                    dma_io(ph, q, [(g.ap(dst)[i * 128:(i + 1) * 128, :], ob[:, :])])
                ph.st("sync", out_fn)
            ph.run()



def ph1_l0a(g):
    nc = g.nc
    g.dtmp("xT", [128, 8, TL]); g.dtmp("xTh", [128, 8, B, 4])
    g.dtmp("xnbf", [128, 8, TL], BF16); g.dtmp("xnbfh", [128, 8, B, 4], BF16)
    for n in ("E", "EV", "srn"):
        g.dtmp(n, [128, 8, TL]); g.dtmp(n + "h", [128, 8, B, 4])
    with ExitStack() as ps:
        sb = lambda n, sh, dt=F32: ps.enter_context(nc.sbuf_tensor(f"p1_{n}", sh, dt))
        wk = sb("wk", [128, 8, D], BF16)
        wv = sb("wv", [128, 8, D], BF16)
        wr = sb("wr", [128, 8, D], BF16)
        tmix = sb("tmix", [128, 3, 8])
        xtok = sb("xtok", [128, 2, D])
        xctr = sb("xctr", [128, 2, D])
        stat = sb("stat", [128, 2, 4])
        CH1 = 256
        xnA = sb("xnA", [128, 8, CH1 + 1])
        xnB = sb("xnB", [128, 8, CH1 + 1])
        xtb = sb("xtb", [128, 8, CH1])
        dbuf = sb("dbuf", [128, 8, CH1])
        mixb = sb("mixb", [128, 3, 8, CH1], BF16)
        xnbf_b = sb("xnbfb", [128, 8, CH1], BF16)
        vsb = sb("vsb", [128, 8, CH1])
        kbuf = sb("kbuf", [128, 8, CH1])
        rbuf = sb("rbuf", [128, 8, CH1])
        ptA = ps.enter_context(nc.psum_tensor("p1_ptA", [128, 8, 128], F32))
        ptB = ps.enter_context(nc.psum_tensor("p1_ptB", [128, 8, 128], F32))
        pmV = ps.enter_context(nc.psum_tensor("p1_pmV", [128, CH1], F32))
        pmK = ps.enter_context(nc.psum_tensor("p1_pmK", [128, CH1], F32))
        pmR = ps.enter_context(nc.psum_tensor("p1_pmR", [128, CH1], F32))
        ident = sb("ident", [128, 128])
        ph = Phase(g, "p1")

        wstg = sb("wstg", [128, 1, D])
        for dst, nm in ((wk, "wkT"), (wv, "wvT"), (wr, "wrT")):
            wv_ = g.ap(nm).rearrange("(kt p) n -> p kt n", p=128)
            for q8 in range(8):
                ld_cast(ph, wstg[:, :, :], dst[:, q8:q8 + 1, :],
                        wv_[:, q8:q8 + 1, :])

        def c_load(q, ph=ph):
            dma_io(ph, q, [
                (tmix[:, 0, :], g.ap("tmk")),
                (tmix[:, 1, :], g.ap("tmv")),
                (tmix[:, 2, :], g.ap("tmr")),
                (ident[:, :], g.ap("identf")),
            ])
        ph.st("sync", c_load)

        for ci, (b, c0, cn, halo) in enumerate(_chunks(256)):
            xn_cur = xnA if (ci % 2 == 0) else xnB
            xn_prev = xnB if (ci % 2 == 0) else xnA
            ntt = 2 if not halo else 1
            tpn = 128 if not halo else 5
            tok0 = 0 if halo else 5 + c0
            coff = 0 if halo else 1

            def ld_x(q, b=b, tok0=tok0, ntt=ntt, tpn=tpn, ph=ph):
                pairs = []
                for tt in range(ntt):
                    pairs.append((xtok[0:tpn, tt, :],
                                  g.ap("x")[b, tok0 + tt * 128: tok0 + tt * 128 + tpn, :]))
                dma_io(ph, q, pairs)
            ph.st("sync", ld_x)

            def ln1(v, ntt=ntt, tpn=tpn):
                for tt in range(ntt):
                    v.tensor_reduce(stat[0:tpn, tt, 0:1], xtok[0:tpn, tt, :], AX.X, ALU.add)
                    v.tensor_scalar_mul(stat[0:tpn, tt, 1:2], stat[0:tpn, tt, 0:1], 1.0 / D)
                    v.tensor_scalar_sub(xctr[0:tpn, tt, :], xtok[0:tpn, tt, :],
                                        stat[0:tpn, tt, 1:2])
                    sqb = dbuf[:, 0:4, :].rearrange("p a b -> p (a b)")
                    v.tensor_tensor(sqb[0:tpn, :], xctr[0:tpn, tt, :], xctr[0:tpn, tt, :],
                                    ALU.mult)
                    v.tensor_reduce(stat[0:tpn, tt, 2:3], sqb[0:tpn, :], AX.X, ALU.add)
                    r = v.tensor_scalar_mul(stat[0:tpn, tt, 2:3], stat[0:tpn, tt, 2:3], 1.0 / D)
                return r
            ph.st("vector", ln1)

            def rstd_act(s, ntt=ntt, tpn=tpn):
                s.activation(stat[0:tpn, 0:ntt, 3], stat[0:tpn, 0:ntt, 2], AF.Ln,
                             bias=EPS_LN)
                return s.activation(stat[0:tpn, 0:ntt, 3], stat[0:tpn, 0:ntt, 3],
                                    AF.Exp, scale=-0.5)
            ph.st("scalar", rstd_act)

            def ln2(v, ntt=ntt, tpn=tpn):
                for tt in range(ntt):
                    r = v.tensor_scalar_mul(xctr[0:tpn, tt, :], xctr[0:tpn, tt, :],
                                            stat[0:tpn, tt, 3:4])
                return r
            ph.st("vector", ln2)

            for tt in range(ntt):
                def tp_xn(t, tt=tt, tpn=tpn):
                    for dt in range(8):
                        r = t.transpose(ptA[:, dt, 0:tpn],
                                        xctr[0:tpn, tt, dt * 128:(dt + 1) * 128],
                                        ident[0:tpn, 0:tpn])
                    return r
                ph.st("tensor", tp_xn)

                def ev_xn(v, tt=tt, tpn=tpn, xn_cur=xn_cur, coff=coff):
                    return v.tensor_copy(xn_cur[:, :, coff + tt * 128: coff + tt * 128 + tpn],
                                         ptA[:, :, 0:tpn])
                ph.st("vector", ev_xn)

                def tp_x(t, tt=tt, tpn=tpn):
                    for dt in range(8):
                        r = t.transpose(ptB[:, dt, 0:tpn],
                                        xtok[0:tpn, tt, dt * 128:(dt + 1) * 128],
                                        ident[0:tpn, 0:tpn])
                    return r
                ph.st("tensor", tp_x)

                def ev_x(v, tt=tt, tpn=tpn):
                    return v.tensor_copy(xtb[:, :, tt * 128: tt * 128 + tpn], ptB[:, :, 0:tpn])
                ph.st("vector", ev_x)

            def mixes(v, b=b, c0=c0, cn=cn, halo=halo, xn_cur=xn_cur, xn_prev=xn_prev):
                if halo:
                    lo, hi = 1, 5
                else:
                    prev_last = 4 if (c0 == 0) else CH1
                    v.tensor_copy(xn_cur[:, :, 0:1], xn_prev[:, :, prev_last:prev_last + 1])
                    lo, hi = 1, 1 + cn
                n = hi - lo
                v.tensor_sub(dbuf[:, :, 0:n], xn_cur[:, :, lo:hi], xn_cur[:, :, lo - 1:hi - 1])
                for qi in range(3):
                    for dt in range(8):
                        v.scalar_tensor_tensor(mixb[:, qi, dt, 0:n], dbuf[:, dt, 0:n],
                                               tmix[:, qi, dt:dt + 1],
                                               xn_cur[:, dt, lo - 1:hi - 1],
                                               ALU.mult, ALU.add)
                return v.tensor_copy(xnbf_b[:, :, 0:n], xn_cur[:, :, lo:hi])
            ph.st("vector", mixes)

            def st_xt(q, b=b, c0=c0, cn=cn, halo=halo, ph=ph):
                if halo:
                    dma_io(ph, q, [(g.ap("xTh")[:, :, b, :], xtb[:, :, 1:5]),
                                   (g.ap("xnbfh")[:, :, b, :], xnbf_b[:, :, 0:4])])
                else:
                    col = b * SL + c0
                    dma_io(ph, q, [(g.ap("xT")[:, :, col:col + cn], xtb[:, :, 0:cn]),
                                   (g.ap("xnbf")[:, :, col:col + cn], xnbf_b[:, :, 0:cn])])
            ph.st("sync", st_xt)

            for m in range(8):
                def mm3(t, m=m, cn=cn):
                    mm_k(t, pmV[:, 0:cn], wv, m * 128, 128, [mixb[:, 1, kt, 0:cn] for kt in range(8)])
                    mm_k(t, pmK[:, 0:cn], wk, m * 128, 128, [mixb[:, 0, kt, 0:cn] for kt in range(8)])
                    return mm_k(t, pmR[:, 0:cn], wr, m * 128, 128,
                                [mixb[:, 2, kt, 0:cn] for kt in range(8)])
                ph.st("tensor", mm3)

                def ev_vec(v, m=m, cn=cn):
                    v.tensor_copy(vsb[:, m, 0:cn], pmV[:, 0:cn])
                    v.tensor_copy(kbuf[:, m, 0:cn], pmK[:, 0:cn])
                    v.tensor_copy(rbuf[:, m, 0:cn], pmR[:, 0:cn])
                ph.st("vector", ev_vec)

            def bulk_act(s, cn=cn):
                s.activation(kbuf[:, :, 0:cn], kbuf[:, :, 0:cn], AF.Exp)
                return s.activation(rbuf[:, :, 0:cn], rbuf[:, :, 0:cn], AF.Sigmoid)
            ph.st("scalar", bulk_act)

            def mk_ev(v, cn=cn):
                v.tensor_tensor(vsb[:, :, 0:cn], kbuf[:, :, 0:cn], vsb[:, :, 0:cn],
                                ALU.mult)
            ph.st("vector", mk_ev)

            def st_m(q, b=b, c0=c0, cn=cn, halo=halo, ph=ph):
                if halo:
                    dma_io(ph, q, [(g.ap("Eh")[:, :, b, :], kbuf[:, :, 0:4]),
                                   (g.ap("EVh")[:, :, b, :], vsb[:, :, 0:4]),
                                   (g.ap("srnh")[:, :, b, :], rbuf[:, :, 0:4])])
                else:
                    col = b * SL + c0
                    dma_io(ph, q, [(g.ap("E")[:, :, col:col + cn], kbuf[:, :, 0:cn]),
                                   (g.ap("EV")[:, :, col:col + cn], vsb[:, :, 0:cn]),
                                   (g.ap("srn")[:, :, col:col + cn], rbuf[:, :, 0:cn])])
            ph.st("sync", st_m)
        ph.run()



def ph2_ffn0(g):
    nc = g.nc
    g.dtmp("g0", [128, 8, TL]); g.dtmp("g0h", [128, 8, B, 4])
    with ExitStack() as ps:
        sb = lambda n, sh, dt=F32: ps.enter_context(nc.sbuf_tensor(f"p2_{n}", sh, dt))
        w1 = sb("w1", [128, 8, FF], BF16)
        w2 = sb("w2", [128, 32, D], BF16)
        b1 = sb("b1", [128, 32])
        b2 = sb("b2", [128, 8])
        xb = sb("xb", [128, 8, CH], BF16)
        hb = sb("hb", [128, 32, CH], BF16)
        gb = sb("gb", [128, 8, CH])
        pmA = ps.enter_context(nc.psum_tensor("p2_pmA", [128, CH], F32))
        pmB = ps.enter_context(nc.psum_tensor("p2_pmB", [128, CH], F32))
        ph = Phase(g, "p2")

        ph.st("sync", lambda q, ph=None: None)
        def w_load2(q, ph=ph):
            dma_io(ph, q, [
                (w1[:, :, :], g.ap("w1t_bf").rearrange("(kt p) n -> p kt n", p=128)),
                (w2[:, :, :], g.ap("w2t_bf").rearrange("(kt p) n -> p kt n", p=128)),
                (b1[:, :], g.ap("b1f")),
                (b2[:, :], g.ap("b2f")),
            ])
        ph.items[-1] = ("sync", w_load2, 0, None)

        for (b, c0, cn, halo) in _chunks():
            col = b * SL + c0

            def ld_x(q, b=b, col=col, cn=cn, halo=halo, ph=ph):
                sr_ = g.ap("xnbfh")[:, :, b, :] if halo else g.ap("xnbf")[:, :, col:col + cn]
                dma_io(ph, q, [(xb[:, :, 0:cn], sr_)])
            ph.st("sync", ld_x)

            for fm in range(32):
                par = fm % 2
                def mm1(t, fm=fm, cn=cn, pm=(pmA if par == 0 else pmB)):
                    return mm_k(t, pm[:, 0:cn], w1, fm * 128, 128,
                                [xb[:, kt, 0:cn] for kt in range(8)])
                ph.st("tensor", mm1, lag=(1 if fm > 0 else 0))

                def ev1(s, fm=fm, cn=cn, pm=(pmA if par == 0 else pmB)):
                    return s.activation(hb[:, fm, 0:cn], pm[:, 0:cn], AF.Gelu,
                                        bias=b1[:, fm:fm + 1])
                ph.st("scalar", ev1)

            for dm in range(8):
                par = dm % 2
                def mm2(t, dm=dm, cn=cn, pm=(pmA if par == 0 else pmB)):
                    return mm_k(t, pm[:, 0:cn], w2, dm * 128, 128,
                                [hb[:, kt, 0:cn] for kt in range(32)])
                ph.st("tensor", mm2, lag=(1 if dm > 0 else 0))

                def ev2(s, dm=dm, cn=cn, pm=(pmA if par == 0 else pmB)):
                    return s.activation(gb[:, dm, 0:cn], pm[:, 0:cn], AF.Silu,
                                        bias=b2[:, dm:dm + 1])
                ph.st("scalar", ev2)

            def st_g(q, b=b, col=col, cn=cn, halo=halo, ph=ph):
                dst = g.ap("g0h")[:, :, b, :] if halo else g.ap("g0")[:, :, col:col + cn]
                dma_io(ph, q, [(dst, gb[:, :, 0:cn])])
            ph.st("sync", st_g)
        ph.run()



def ph3_l0scan(g):
    nc = g.nc
    g.dtmp("Sloc", [128, 8, TL]); g.dtmp("Tloc", [128, 8, TL])
    if g.mode == "ladder":
        g.dout("l0sum", [128, 8, B, 10])
    else:
        g.dtmp("l0sum", [128, 8, B, 10])
    with ExitStack() as ps:
        sb = lambda n, sh, dt=F32: ps.enter_context(nc.sbuf_tensor(f"p3_{n}", sh, dt))
        td = sb("td", [128, 8])
        etd = sb("etd", [128, 8])
        ew = sb("ew", [128, 8])
        ewp = sb("ewp", [128, SL])
        Eb = sb("Eb", [128, SL])
        EVb = sb("EVb", [128, SL])
        Sb = sb("Sb", [128, SL])
        Tb = sb("Tb", [128, SL])
        summ = sb("summ", [128, 8, B, 10])
        ph = Phase(g, "p3")

        def td_ld(q, ph=ph):
            dma_io(ph, q, [(td[:, :], g.ap("tdec"))])
        ph.st("sync", td_ld)
        ph.st("scalar", lambda s: s.activation(etd[:, :], td[:, :], AF.Exp))
        ph.st("scalar", lambda s: s.activation(ew[:, :], etd[:, :], AF.Exp, scale=-1.0))

        for dt in range(8):
            for b in range(B):
                col = b * SL

                def ld(q, dt=dt, col=col, ph=ph):
                    dma_io(ph, q, [(Eb[:, :], g.ap("E")[:, dt, col:col + SL]),
                                   (EVb[:, :], g.ap("EV")[:, dt, col:col + SL])])
                ph.st("sync", ld)

                def mkew(v, dt=dt):
                    v.memset(ewp[:, :], 0.0)
                    return v.tensor_scalar_add(ewp[:, :], ewp[:, :], ew[:, dt:dt + 1])
                ph.st("vector", mkew)

                def scans(v, dt=dt, b=b):
                    v.tensor_tensor_scan(Sb[:, :], ewp[:, :], EVb[:, :], 0.0,
                                         ALU.mult, ALU.add)
                    v.tensor_tensor_scan(Tb[:, :], ewp[:, :], Eb[:, :], 0.0,
                                         ALU.mult, ALU.add)
                    v.tensor_copy(summ[:, dt, b, 0:1], Sb[:, SL - 1:SL])
                    v.tensor_copy(summ[:, dt, b, 1:2], Tb[:, SL - 1:SL])
                    v.tensor_copy(summ[:, dt, b, 2:6], Sb[:, SL - 4:SL])
                    return v.tensor_copy(summ[:, dt, b, 6:10], Tb[:, SL - 4:SL])
                ph.st("vector", scans)

                def stout(q, dt=dt, col=col, ph=ph):
                    dma_io(ph, q, [(g.ap("Sloc")[:, dt, col:col + SL], Sb[:, :]),
                                   (g.ap("Tloc")[:, dt, col:col + SL], Tb[:, :])])
                ph.st("sync", stout)

        def st_sum(q, ph=ph):
            dma_io(ph, q, [(g.ap("l0sum")[:, :, :, :], summ[:, :, :, :])])
        ph.st("sync", st_sum)
        ph.run()



def ph4_l0fix(g):
    nc = g.nc
    if g.mode == "ladder":
        g.din("l0sum_all", [NCORES, 128, 8, B, 10])
    g.dtmp("x1T", [128, 8, TL]); g.dtmp("xn1T", [128, 8, TL])
    g.dtmp("xn1bf", [128, 8, TL], BF16); g.dtmp("xnr1bf", [128, 8, TL], BF16)
    g.dtmp("xnr1hbf", [128, 8, B, 3], BF16)
    g.dtmp("statrow", [1, 3 * CH])
    with ExitStack() as po:
        so = lambda n, sh, dt=F32: po.enter_context(nc.sbuf_tensor(f"p4_{n}", sh, dt))
        rwkv = so("rwkv", [128, 8, TL], BF16)
        rwkhb = so("rwkhb", [128, 8, B, 3], BF16)
        wo = so("wo", [128, 8, D], BF16)
        etd = so("etd", [128, 8]); negetd = so("negetd", [128, 8])
        eu = so("eu", [128, 8])

        with ExitStack() as ps:
            sb = lambda n, sh, dt=F32: ps.enter_context(nc.sbuf_tensor(f"p4a_{n}", sh, dt))
            td = sb("td", [128, 8]); tf = sb("tf", [128, 8])
            d2k = sb("d2k", [128, 8])
            ew3 = sb("ew3", [128, 8, 3])
            sall = sb("sall", [128, NCORES, 8, B, 10])
            ohbs = sb("ohbs", [128, NCORES]); ohbp = sb("ohbp", [128, NCORES])
            candS = sb("candS", [128, 8, B, NCORES]); candT = sb("candT", [128, 8, B, NCORES])
            CS = sb("CS", [128, 8, B]); CT = sb("CT", [128, 8, B])
            CSp = sb("CSp", [128, 8, B]); CTp = sb("CTp", [128, 8, B])
            SlocP = sb("SlocP", [128, 8, B, 4]); TlocP = sb("TlocP", [128, 8, B, 4])
            Eh = sb("Eh", [128, 8, B, 4]); EVh = sb("EVh", [128, 8, B, 4])
            srh = sb("srh", [128, 8, B, 4])
            numh = sb("numh", [128, 8, B, 3]); denh = sb("denh", [128, 8, B, 3])
            tmph = sb("tmph", [128, 8, B, 3])
            tlocb = sb("tlocb", [128, SL])
            Pw = sb("Pw", [128, SL])
            Eb = sb("Eb", [128, SL]); EVb = sb("EVb", [128, SL])
            Sb = sb("Sb", [128, SL]); Tb = sb("Tb", [128, SL]); srb = sb("srb", [128, SL])
            numb = sb("numb", [128, SL]); denb = sb("denb", [128, SL])
            ph = Phase(g, "p4a")

            def ld0(q, ph=ph):
                dma_io(ph, q, [
                    (td[:, :], g.ap("tdec")), (tf[:, :], g.ap("tfirst")),
                    (sall[:, :, :, :, :],
                     g.ap("l0sum_all").rearrange("c p d b q -> p c d b q")),
                    (ohbs[:, :], g.ap("ohself").partition_broadcast(128)),
                    (ohbp[:, :], g.ap("ohprev").partition_broadcast(128)),
                    (tlocb[:, :], g.ap("tloc").partition_broadcast(128)),
                    (Eh[:, :, :, :], g.ap("Eh")), (EVh[:, :, :, :], g.ap("EVh")),
                    (srh[:, :, :, :], g.ap("srnh")),
                ])
            ph.st("sync", ld0)

            wstg4 = sb("wstg4", [128, 8, D])
            ld_cast(ph, wstg4[:, :, :], wo[:, :, :],
                    g.ap("woT").rearrange("(kt p) n -> p kt n", p=128))

            def acts0(s):
                s.activation(etd[:, :], td[:, :], AF.Exp)
                s.activation(eu[:, :], tf[:, :], AF.Exp)
                s.activation(d2k[:, :], etd[:, :], AF.Exp, scale=-float(SL))
                for i in range(3):
                    s.activation(ew3[:, :, i], etd[:, :], AF.Exp, scale=-float(SL - 3 + i))
            ph.st("scalar", acts0)

            def chain_fn(v):
                v.tensor_scalar_mul(negetd[:, :], etd[:, :], -1.0)
                v.memset(candS[:, :, :, 0], 0.0)
                v.memset(candT[:, :, :, 0], 0.0)
                d2b = d2k[:, :].unsqueeze(2).broadcast_to([128, 8, B])
                for c in range(NCORES - 1):
                    v.tensor_tensor(candS[:, :, :, c + 1], candS[:, :, :, c], d2b, ALU.mult)
                    v.tensor_add(candS[:, :, :, c + 1], candS[:, :, :, c + 1],
                                 sall[:, c, :, :, 0])
                    v.tensor_tensor(candT[:, :, :, c + 1], candT[:, :, :, c], d2b, ALU.mult)
                    v.tensor_add(candT[:, :, :, c + 1], candT[:, :, :, c + 1],
                                 sall[:, c, :, :, 1])
                for dst, cand, oh in ((CS, candS, ohbs), (CT, candT, ohbs),
                                      (CSp, candS, ohbp), (CTp, candT, ohbp)):
                    v.memset(dst[:, :, :], 0.0)
                    for c in range(NCORES):
                        v.scalar_tensor_tensor(dst[:, :, :], cand[:, :, :, c],
                                               oh[:, c:c + 1], dst[:, :, :],
                                               ALU.mult, ALU.add)
                for dst, qi in ((SlocP, 2), (TlocP, 6)):
                    v.memset(dst[:, :, :, :], 0.0)
                    for c in range(NCORES):
                        v.scalar_tensor_tensor(dst[:, :, :, :], sall[:, c, :, :, qi:qi + 4],
                                               ohbp[:, c:c + 1], dst[:, :, :, :],
                                               ALU.mult, ALU.add)
            ph.st("vector", chain_fn)

            def halo_y(v):
                ew3b = ew3[:, :, :].unsqueeze(2).broadcast_to([128, 8, B, 3])
                cspb = CSp[:, :, :].unsqueeze(3).broadcast_to([128, 8, B, 3])
                eub = eu[:, :].unsqueeze(2).unsqueeze(3).broadcast_to([128, 8, B, 3])
                v.tensor_tensor(numh[:, :, :, :], ew3b, cspb, ALU.mult)
                v.tensor_add(numh[:, :, :, :], numh[:, :, :, :], SlocP[:, :, :, 0:3])
                v.tensor_tensor(tmph[:, :, :, :], EVh[:, :, :, 1:4], eub, ALU.mult)
                v.tensor_add(numh[:, :, :, :], numh[:, :, :, :], tmph[:, :, :, :])
                ctpb = CTp[:, :, :].unsqueeze(3).broadcast_to([128, 8, B, 3])
                v.tensor_tensor(denh[:, :, :, :], ew3b, ctpb, ALU.mult)
                v.tensor_add(denh[:, :, :, :], denh[:, :, :, :], TlocP[:, :, :, 0:3])
                v.tensor_tensor(tmph[:, :, :, :], Eh[:, :, :, 1:4], eub, ALU.mult)
                v.tensor_add(denh[:, :, :, :], denh[:, :, :, :], tmph[:, :, :, :])
                v.reciprocal(denh[:, :, :, :], denh[:, :, :, :])
                v.tensor_tensor(numh[:, :, :, :], numh[:, :, :, :], denh[:, :, :, :],
                                ALU.mult)
                v.tensor_tensor(rwkhb[:, :, :, :], numh[:, :, :, :], srh[:, :, :, 1:4],
                                ALU.mult)
            ph.st("vector", halo_y)

            for dt in range(8):
                for b in range(B):
                    col = b * SL

                    def ld(q, dt=dt, col=col, ph=ph):
                        dma_io(ph, q, [(Sb[:, :], g.ap("Sloc")[:, dt, col:col + SL]),
                                       (Tb[:, :], g.ap("Tloc")[:, dt, col:col + SL]),
                                       (Eb[:, :], g.ap("E")[:, dt, col:col + SL]),
                                       (EVb[:, :], g.ap("EV")[:, dt, col:col + SL]),
                                       (srb[:, :], g.ap("srn")[:, dt, col:col + SL])])
                    ph.st("sync", ld)

                    def mk_pw(v, dt=dt):
                        v.tensor_scalar_mul(Pw[:, :], tlocb[:, :], negetd[:, dt:dt + 1])
                    ph.st("vector", mk_pw)

                    ph.st("scalar", lambda s: s.activation(Pw[:, :], Pw[:, :], AF.Exp))

                    def fix(v, dt=dt, b=b, col=col):
                        v.tensor_scalar_mul(numb[:, :], Pw[:, :], CS[:, dt, b:b + 1])
                        v.scalar_tensor_tensor(numb[:, :], EVb[:, :], eu[:, dt:dt + 1],
                                               numb[:, :], ALU.mult, ALU.add)
                        v.tensor_add(numb[:, 1:SL], numb[:, 1:SL], Sb[:, 0:SL - 1])
                        v.tensor_scalar_mul(denb[:, :], Pw[:, :], CT[:, dt, b:b + 1])
                        v.scalar_tensor_tensor(denb[:, :], Eb[:, :], eu[:, dt:dt + 1],
                                               denb[:, :], ALU.mult, ALU.add)
                        v.tensor_add(denb[:, 1:SL], denb[:, 1:SL], Tb[:, 0:SL - 1])
                        v.reciprocal(denb[:, :], denb[:, :])
                        v.tensor_tensor(numb[:, :], numb[:, :], denb[:, :], ALU.mult)
                        v.tensor_tensor(rwkv[:, dt, col:col + SL], numb[:, :], srb[:, :],
                                        ALU.mult)
                    ph.st("vector", fix)
            ph.run()

        with ExitStack() as ps:
            sb = lambda n, sh, dt=F32: ps.enter_context(nc.sbuf_tensor(f"p4b_{n}", sh, dt))
            onecol = sb("onecol", [128, 1])
            s_sb = sb("s_sb", [128, 8, CH])
            gch = sb("gch", [128, 8, CH]); xtc = sb("xtc", [128, 8, CH])
            x1b = sb("x1b", [128, 8, CH])
            xn1b = sb("xn1b", [128, 8, CH])
            obf = sb("obf", [128, 8, CH], BF16); obf2 = sb("obf2", [128, 8, CH], BF16)
            strow = sb("strow", [1, 3, CH]); stbc = sb("stbc", [128, 3, CH])
            svar = sb("svar", [1, CH]); stmp = sb("stmp", [1, CH]); stmp2 = sb("stmp2", [1, CH])
            pm0 = ps.enter_context(nc.psum_tensor("p4_pm0", [128, CH], F32))
            pm1 = ps.enter_context(nc.psum_tensor("p4_pm1", [128, CH], F32))
            pmu = ps.enter_context(nc.psum_tensor("p4_pmu", [1, CH], F32))
            ps2 = ps.enter_context(nc.psum_tensor("p4_ps2", [1, CH], F32))
            ph = Phase(g, "p4b")

            def initb(v):
                v.memset(onecol[:, :], 1.0)
                v.memset(strow[:, :, :], 0.0)
            ph.st("vector", initb)

            for b, c0, cn, halo in _chunks():
                col = b * SL + c0
                nmm = cn if not halo else 3

                if not halo:
                    def ldc(q, col=col, cn=cn, ph=ph):
                        dma_io(ph, q, [(gch[:, :, 0:cn], g.ap("g0")[:, :, col:col + cn]),
                                       (xtc[:, :, 0:cn], g.ap("xT")[:, :, col:col + cn])])
                else:
                    def ldc(q, b=b, ph=ph):
                        dma_io(ph, q, [(gch[:, :, 0:3], g.ap("g0h")[:, :, b, 1:4]),
                                       (xtc[:, :, 0:3], g.ap("xTh")[:, :, b, 1:4])])
                ph.st("sync", ldc)

                for m in range(8):
                    def mmwo(t, m=m, b=b, col=col, halo=halo, nmm=nmm,
                             pm=(pm0 if m % 2 == 0 else pm1)):
                        if halo:
                            rhs = [rwkhb[:, kt, b, :] for kt in range(8)]
                        else:
                            rhs = [rwkv[:, kt, col:col + nmm] for kt in range(8)]
                        return mm_k(t, pm[:, 0:nmm], wo, m * 128, 128, rhs)
                    ph.st("tensor", mmwo, lag=(1 if m > 0 else 0))

                    def evwo(v, m=m, nmm=nmm, pm=(pm0 if m % 2 == 0 else pm1)):
                        v.tensor_copy(s_sb[:, m, 0:nmm], pm[:, 0:nmm])
                    ph.st("vector", evwo)

                def mkx1(v, nmm=nmm):
                    v.tensor_tensor(x1b[:, :, 0:nmm], s_sb[:, :, 0:nmm], gch[:, :, 0:nmm],
                                    ALU.mult)
                    v.tensor_add(x1b[:, :, 0:nmm], x1b[:, :, 0:nmm], xtc[:, :, 0:nmm])
                    v.tensor_tensor(s_sb[:, :, 0:nmm], x1b[:, :, 0:nmm], x1b[:, :, 0:nmm],
                                    ALU.mult)
                ph.st("vector", mkx1)

                def mmstat(t, nmm=nmm):
                    for kt in range(8):
                        t.matmul(pmu[:, 0:nmm], onecol[:, :], x1b[:, kt, 0:nmm],
                                 start=(kt == 0), stop=(kt == 7))
                    r = None
                    for kt in range(8):
                        r = t.matmul(ps2[:, 0:nmm], onecol[:, :], s_sb[:, kt, 0:nmm],
                                     start=(kt == 0), stop=(kt == 7))
                    return r
                ph.st("tensor", mmstat)

                def stat1(v, nmm=nmm):
                    v.tensor_scalar_mul(strow[:, 0, 0:nmm], pmu[:, 0:nmm], 1.0 / D)
                    v.tensor_scalar_mul(svar[:, 0:nmm], ps2[:, 0:nmm], 1.0 / D)
                    v.tensor_tensor(stmp[:, 0:nmm], strow[:, 0, 0:nmm], strow[:, 0, 0:nmm],
                                    ALU.mult)
                    v.tensor_sub(svar[:, 0:nmm], svar[:, 0:nmm], stmp[:, 0:nmm])
                ph.st("vector", stat1)

                def rstd_act(s, nmm=nmm):
                    s.activation(stmp[:, 0:nmm], svar[:, 0:nmm], AF.Ln, bias=EPS_LN)
                    return s.activation(strow[:, 1, 0:nmm], stmp[:, 0:nmm], AF.Exp,
                                        scale=-0.5)
                ph.st("scalar", rstd_act)

                def stat2(v, nmm=nmm):
                    v.tensor_tensor(stmp[:, 0:nmm], strow[:, 1, 0:nmm],
                                    strow[:, 1, 0:nmm], ALU.mult)
                    v.tensor_tensor(stmp2[:, 0:nmm], svar[:, 0:nmm], stmp[:, 0:nmm],
                                    ALU.mult)
                ph.st("vector", stat2)

                def rstd2_act(s, nmm=nmm):
                    s.activation(stmp2[:, 0:nmm], stmp2[:, 0:nmm], AF.Ln, bias=EPS_RMS)
                    return s.activation(strow[:, 2, 0:nmm], stmp2[:, 0:nmm], AF.Exp,
                                        scale=-0.5)
                ph.st("scalar", rstd2_act)

                def strd(q, ph=ph):
                    dma_io(ph, q, [(g.ap("statrow")[:, 0:3 * CH],
                                    strow[:, :, :].rearrange("a b c -> a (b c)"))])
                    dma_io(ph, q, [(stbc[:, :, :].rearrange("a b c -> a (b c)"),
                                    g.ap("statrow").partition_broadcast(128))])
                ph.st("sync", strd)

                def norm(v, nmm=nmm):
                    for kt in range(8):
                        v.tensor_sub(xn1b[:, kt, 0:nmm], x1b[:, kt, 0:nmm],
                                     stbc[:, 0, 0:nmm])
                        v.tensor_tensor(xn1b[:, kt, 0:nmm], xn1b[:, kt, 0:nmm],
                                        stbc[:, 1, 0:nmm], ALU.mult)
                    v.tensor_copy(obf[:, :, 0:nmm], xn1b[:, :, 0:nmm])
                    for kt in range(8):
                        v.tensor_tensor(obf2[:, kt, 0:nmm], xn1b[:, kt, 0:nmm],
                                        stbc[:, 2, 0:nmm], ALU.mult)
                ph.st("vector", norm)

                if not halo:
                    def stc(q, col=col, cn=cn, ph=ph):
                        dma_io(ph, q, [
                            (g.ap("x1T")[:, :, col:col + cn], x1b[:, :, 0:cn]),
                            (g.ap("xn1T")[:, :, col:col + cn], xn1b[:, :, 0:cn]),
                            (g.ap("xn1bf")[:, :, col:col + cn], obf[:, :, 0:cn]),
                            (g.ap("xnr1bf")[:, :, col:col + cn], obf2[:, :, 0:cn]),
                        ])
                else:
                    def stc(q, b=b, ph=ph):
                        dma_io(ph, q, [(g.ap("xnr1hbf")[:, :, b, :], obf2[:, :, 0:3])])
                ph.st("sync", stc)
            ph.run()



def ph5_inproj(g):
    nc = g.nc
    CH5 = 256
    g.dtmp("siluz", [128, 16, TL], BF16)
    g.dtmp("K0", [128, 16, TL], BF16); g.dtmp("K1", [128, 16, TL], BF16)
    g.dtmp("u", [128, 16, TL]); g.dtmp("ubf", [128, 16, TL], BF16)
    with ExitStack() as ps:
        sb = lambda n, sh, dt=F32: ps.enter_context(nc.sbuf_tensor(f"p5_{n}", sh, dt))
        wip = sb("wip", [128, 8, 8192], BF16)
        rms8 = sb("rms8", [128, 8])
        ipb = sb("ipb", [128, 64])
        cwp = sb("cwp", [128, 16, 4]); cbp = sb("cbp", [128, 16])
        hmsk = sb("hmsk", [128, 1])
        xrh = sb("xrh", [128, 8, B, 3], BF16)
        Vhalo = sb("Vhalo", [128, 16, B, 3])
        xrb = sb("xrb", [128, 8, CH5], BF16)
        Vb = sb("Vb", [128, 16, 3 + CH5])
        ub = sb("ub", [128, 16, CH5]); ubfb = sb("ubfb", [128, 16, CH5], BF16)
        cvt = sb("cvt", [128, 16, CH5])
        stg0 = sb("stg0", [128, CH5], BF16); stg1 = sb("stg1", [128, CH5], BF16)
        zstg0 = sb("zstg0", [128, CH5], BF16); zstg1 = sb("zstg1", [128, CH5], BF16)
        pmA = ps.enter_context(nc.psum_tensor("p5_pmA", [128, CH5], F32))
        pmB = ps.enter_context(nc.psum_tensor("p5_pmB", [128, CH5], F32))
        ph = Phase(g, "p5")

        def ld0(q, ph=ph):
            dma_io(ph, q, [(rms8[:, :], g.ap("rmsw8")), (ipb[:, :], g.ap("ipb")),
                           (cwp[:, :, :], g.ap("cw")), (cbp[:, :], g.ap("cb")),
                           (hmsk[:, :], g.ap("halomask").partition_broadcast(128)),
                           (xrh[:, :, :, :], g.ap("xnr1hbf"))])
        ph.st("sync", ld0)

        for kt in range(8):
            for half in range(2):
                lo = half * 4096

                def ldw(q, kt=kt, lo=lo, ph=ph):
                    dma_io(ph, q, [(ub[:, :, :].rearrange("p a b -> p (a b)")[:, 0:4096],
                                    g.ap("ipT")[kt * 128:(kt + 1) * 128, lo:lo + 4096])])
                ph.st("sync", ldw)

                def fold(v, kt=kt, lo=lo):
                    slab = ub[:, :, :].rearrange("p a b -> p (a b)")[:, 0:4096]
                    v.tensor_scalar_mul(slab, slab, rms8[:, kt:kt + 1])
                    v.tensor_copy(wip[:, kt, lo:lo + 4096], slab)
                ph.st("vector", fold)

        for b in range(B):
            for i, m in enumerate(range(48, 64)):
                def mmh(t, m=m, b=b):
                    return mm_k(t, pmA[:, 0:3], wip, m * 128, 128,
                                [xrh[:, kt, b, :] for kt in range(8)])
                ph.st("tensor", mmh)

                def evh(s, m=m, b=b, i=i):
                    return s.activation(Vhalo[:, i, b, :], pmA[:, 0:3], AF.Identity,
                                        bias=ipb[:, m:m + 1])
                ph.st("scalar", evh)

        def maskh(v):
            v.tensor_scalar_mul(Vhalo[:, :, :, :], Vhalo[:, :, :, :], hmsk[:, 0:1])
        ph.st("vector", maskh)

        for b in range(B):
            for c0 in range(0, SL, CH5):
                col = b * SL + c0
                cn = CH5

                def ldx(q, col=col, cn=cn, ph=ph):
                    dma_io(ph, q, [(xrb[:, :, 0:cn], g.ap("xnr1bf")[:, :, col:col + cn])])
                ph.st("sync", ldx)

                if c0 == 0:
                    def carry0(v, b=b):
                        v.tensor_copy(Vb[:, :, 0:3], Vhalo[:, :, b, :])
                    ph.st("vector", carry0)

                for m in range(64):
                    pm = pmA if m % 2 == 0 else pmB
                    lg = 0 if m == 0 else (2 if 16 <= m - 1 < 48 else 1)
                    def mm1(t, m=m, cn=cn, pm=pm):
                        return mm_k(t, pm[:, 0:cn], wip, m * 128, 128,
                                    [xrb[:, kt, 0:cn] for kt in range(8)])
                    ph.st("tensor", mm1, lag=lg)

                    if m < 16:
                        zs = zstg0 if (m // 2) % 2 == 0 else zstg1
                        def ev(v, m=m, cn=cn, pm=pm, zs=zs):
                            v.tensor_scalar_add(cvt[:, m, 0:cn], pm[:, 0:cn],
                                                ipb[:, m:m + 1])
                        ph.st("vector", ev)
                    elif m < 48:
                        ss = stg0 if (m // 2) % 2 == 0 else stg1
                        def ev(v, m=m, cn=cn, pm=pm, ss=ss):
                            v.tensor_scalar_add(ss[:, 0:cn], pm[:, 0:cn], ipb[:, m:m + 1])
                        ph.st("vector", ev)

                        def stv(q, m=m, col=col, cn=cn, ss=ss, ph=ph):
                            if m < 32:
                                dst = g.ap("K0")[:, m - 16, col:col + cn]
                            else:
                                dst = g.ap("K1")[:, m - 32, col:col + cn]
                            dma_io(ph, q, [(dst, ss[:, 0:cn])])
                        ph.st("sync", stv)
                    else:
                        def ev(v, m=m, cn=cn, pm=pm):
                            v.tensor_scalar_add(Vb[:, m - 48, 3:3 + cn], pm[:, 0:cn],
                                                ipb[:, m:m + 1])
                        ph.st("vector", ev)

                def zsilu(s, cn=cn):
                    return s.activation(cvt[:, :, 0:cn], cvt[:, :, 0:cn], AF.Silu)
                ph.st("scalar", zsilu)

                def zst(v, cn=cn):
                    v.tensor_copy(ubfb[:, :, 0:cn], cvt[:, :, 0:cn])
                ph.st("vector", zst)

                def stz(q, col=col, cn=cn, ph=ph):
                    dma_io(ph, q, [(g.ap("siluz")[:, :, col:col + cn], ubfb[:, :, 0:cn])])
                ph.st("sync", stz)

                def conv(v, cn=cn):
                    cw0 = cwp[:, :, 0:1].broadcast_to([128, 16, cn])
                    v.tensor_tensor(cvt[:, :, 0:cn], Vb[:, :, 0:cn], cw0, ALU.mult)
                    for tau in range(1, 4):
                        cwt = cwp[:, :, tau:tau + 1].broadcast_to([128, 16, cn])
                        v.tensor_tensor(ub[:, :, 0:cn], Vb[:, :, tau:tau + cn], cwt,
                                        ALU.mult)
                        v.tensor_add(cvt[:, :, 0:cn], cvt[:, :, 0:cn], ub[:, :, 0:cn])
                    cbb = cbp[:, :].unsqueeze(2).broadcast_to([128, 16, cn])
                    v.tensor_add(cvt[:, :, 0:cn], cvt[:, :, 0:cn], cbb)
                ph.st("vector", conv)

                ph.st("scalar", lambda s, cn=cn: s.activation(ub[:, :, 0:cn],
                                                              cvt[:, :, 0:cn], AF.Silu))

                def cast_u(v, cn=cn):
                    v.tensor_copy(ubfb[:, :, 0:cn], ub[:, :, 0:cn])
                    v.tensor_copy(Vb[:, :, 0:3], Vb[:, :, cn:cn + 3])
                ph.st("vector", cast_u)

                def stc(q, col=col, cn=cn, ph=ph):
                    dma_io(ph, q, [
                        (g.ap("u")[:, :, col:col + cn], ub[:, :, 0:cn]),
                        (g.ap("ubf")[:, :, col:col + cn], ubfb[:, :, 0:cn]),
                    ])
                ph.st("sync", stc)
        ph.run()



def ph6_chains(g):
    nc = g.nc
    for n in ("a11d", "a12d", "betad", "vpugd", "sb0d", "sb1d", "sc0d", "sc1d"):
        g.dtmp(n, [128, TL])
    g.dtmp("Kh0", [128, 16, TL], BF16); g.dtmp("Kh1", [128, 16, TL], BF16)
    g.dtmp("Vg", [128, 16, TL], BF16)
    dbg = "dbg1" in g.dev_out
    if dbg:
        for n in ("dbg1", "dbg2", "dbg3", "dbg4", "dbg5", "dbg6", "dbg7", "dbg8"):
            g.dtmp(n, [128, TL])
    with ExitStack() as ps:
        sb = lambda n, sh, dt=F32: ps.enter_context(nc.sbuf_tensor(f"p6_{n}", sh, dt))
        wsm = sb("wsm", [128, 16, 1408], BF16)
        smb = sb("smb", [128, 11])
        dtc = sb("dtc", [128, 1]); ropec = sb("ropec", [128, 1])
        spdtc = sb("spdtc", [128, 1]); ctmp = sb("ctmp", [128, 1]); ctmp2 = sb("ctmp2", [128, 1])
        posb = sb("posb", [128, SL])
        ubc = sb("ubc", [128, 16, CH], BF16)
        qt = sb("qt", [128, 11, CH])
        t1 = sb("t1", [128, CH]); t2 = sb("t2", [128, CH]); t3 = sb("t3", [128, CH])
        t4 = sb("t4", [128, CH]); t5 = sb("t5", [128, CH]); t6 = sb("t6", [128, CH])
        beto = sb("beto", [128, CH]); rgo = sb("rgo", [128, CH]); ugo = sb("ugo", [128, CH])
        a11o = sb("a11o", [128, CH]); a12o = sb("a12o", [128, CH]); vpo = sb("vpo", [128, CH])
        K0c = sb("K0c", [128, 16, CH], BF16); K1c = sb("K1c", [128, 16, CH], BF16)
        uc = sb("uc", [128, 16, CH], BF16)
        pmA = ps.enter_context(nc.psum_tensor("p6_pmA", [128, CH], F32))
        pmB = ps.enter_context(nc.psum_tensor("p6_pmB", [128, CH], F32))
        ph = Phase(g, "p6")

        def ld0(q, ph=ph):
            dma_io(ph, q, [(smb[:, :], g.ap("smallb")), (dtc[:, :], g.ap("dtc")),
                           (ropec[:, :], g.ap("ropec")),
                           (posb[:, :], g.ap("posrow").partition_broadcast(128))])
        ph.st("sync", ld0)

        wstg6 = sb("wstg6", [128, 8, 1408])
        smv = g.ap("smallT").rearrange("(kt p) n -> p kt n", p=128)
        for hf_ in range(2):
            ld_cast(ph, wstg6[:, :, :], wsm[:, hf_ * 8:hf_ * 8 + 8, :],
                    smv[:, hf_ * 8:hf_ * 8 + 8, :])

        def sp1(v):
            v.tensor_scalar_mul(ctmp[:, :], dtc[:, :], -1.0)
            v.tensor_max(ctmp[:, :], ctmp[:, :], dtc[:, :])
        ph.st("vector", sp1)

        def sp2(s):
            s.activation(ctmp[:, :], ctmp[:, :], AF.Exp, scale=-1.0)
            return s.activation(ctmp[:, :], ctmp[:, :], AF.Ln, bias=1.0)
        ph.st("scalar", sp2)

        def sp3(v):
            v.tensor_scalar_max(ctmp2[:, :], dtc[:, :], 0.0)
            v.tensor_add(spdtc[:, :], ctmp2[:, :], ctmp[:, :])
        ph.st("vector", sp3)

        for b in range(B):
            for c0 in range(0, SL, CH):
                col = b * SL + c0
                cn = CH

                def ldu(q, col=col, cn=cn, ph=ph):
                    dma_io(ph, q, [(ubc[:, :, 0:cn], g.ap("ubf")[:, :, col:col + cn])])
                ph.st("sync", ldu)

                for m in range(11):
                    pm = pmA if m % 2 == 0 else pmB
                    def mm(t, m=m, cn=cn, pm=pm):
                        return mm_k(t, pm[:, 0:cn], wsm, m * 128, 128,
                                    [ubc[:, kt, 0:cn] for kt in range(16)])
                    ph.st("tensor", mm, lag=(1 if m > 0 else 0))

                    def ev(v, m=m, cn=cn, pm=pm):
                        v.tensor_scalar_add(qt[:, m, 0:cn], pm[:, 0:cn], smb[:, m:m + 1])
                    ph.st("vector", ev)

                def ch0(s, cn=cn):
                    s.activation(beto[:, 0:cn], qt[:, 4, 0:cn], AF.Sigmoid)
                    s.activation(rgo[:, 0:cn], qt[:, 5, 0:cn], AF.Sigmoid)
                    return s.activation(ugo[:, 0:cn], qt[:, 6, 0:cn], AF.Sigmoid)
                ph.st("scalar", ch0)

                def ch1(v, cn=cn, c0=c0):
                    v.tensor_scalar_mul(t1[:, 0:cn], qt[:, 0, 0:cn], -1.0)
                    v.tensor_max(t1[:, 0:cn], t1[:, 0:cn], qt[:, 0, 0:cn])
                    v.tensor_scalar_max(t2[:, 0:cn], qt[:, 0, 0:cn], 0.0)
                    v.tensor_add(t3[:, 0:cn], qt[:, 1, 0:cn], qt[:, 2, 0:cn])
                    v.scalar_tensor_tensor(t3[:, 0:cn], posb[:, c0:c0 + cn], ropec[:, 0:1],
                                           t3[:, 0:cn], ALU.mult, ALU.add)
                    v.tensor_scalar_mul(t4[:, 0:cn], qt[:, 3, 0:cn], -1.0)
                    v.tensor_max(t4[:, 0:cn], t4[:, 0:cn], qt[:, 3, 0:cn])
                    v.tensor_scalar_max(t5[:, 0:cn], qt[:, 3, 0:cn], 0.0)
                ph.st("vector", ch1)

                def ch2(s, cn=cn):
                    s.activation(t1[:, 0:cn], t1[:, 0:cn], AF.Exp, scale=-1.0)
                    s.activation(t1[:, 0:cn], t1[:, 0:cn], AF.Ln, bias=1.0)
                    s.activation(t4[:, 0:cn], t4[:, 0:cn], AF.Exp, scale=-1.0)
                    return s.activation(t4[:, 0:cn], t4[:, 0:cn], AF.Ln, bias=1.0)
                ph.st("scalar", ch2)

                def ch3a(v, cn=cn):
                    v.tensor_add(t2[:, 0:cn], t2[:, 0:cn], t1[:, 0:cn])
                    v.tensor_add(t4[:, 0:cn], t4[:, 0:cn], t5[:, 0:cn])
                    v.tensor_scalar_mul(t5[:, 0:cn], t3[:, 0:cn], -1.0)
                    v.tensor_max(t5[:, 0:cn], t5[:, 0:cn], t3[:, 0:cn])
                    v.tensor_add(t5[:, 0:cn], t5[:, 0:cn], t2[:, 0:cn])
                    v.tensor_scalar_add(t5[:, 0:cn], t5[:, 0:cn], 1e-4)
                    v.reciprocal(t5[:, 0:cn], t5[:, 0:cn])
                    v.tensor_scalar_mul(t5[:, 0:cn], t5[:, 0:cn], spdtc[:, 0:1])
                ph.st("vector", ch3a)
                if dbg:
                    def dbg_st2(q, col=col, cn=cn, ph=ph):
                        dma_io(ph, q, [
                            (g.ap("dbg5")[:, col:col + cn], t5[:, 0:cn]),
                            (g.ap("dbg6")[:, col:col + cn], t4[:, 0:cn]),
                            (g.ap("dbg7")[:, col:col + cn], t2[:, 0:cn]),
                            (g.ap("dbg8")[:, col:col + cn], t3[:, 0:cn]),
                        ])
                    ph.st("sync", dbg_st2)

                def ch3(v, cn=cn):
                    v.tensor_add(t4[:, 0:cn], t4[:, 0:cn], t5[:, 0:cn])
                    v.tensor_scalar_mul(t4[:, 0:cn], t4[:, 0:cn], 0.5)
                    v.tensor_tensor(t2[:, 0:cn], t4[:, 0:cn], t2[:, 0:cn], ALU.mult)
                    v.tensor_tensor(t3[:, 0:cn], t4[:, 0:cn], t3[:, 0:cn], ALU.mult)
                ph.st("vector", ch3)

                if dbg:
                    def dbg_st(q, col=col, cn=cn, ph=ph):
                        dma_io(ph, q, [
                            (g.ap("dbg1")[:, col:col + cn], t2[:, 0:cn]),
                            (g.ap("dbg2")[:, col:col + cn], t3[:, 0:cn]),
                            (g.ap("dbg3")[:, col:col + cn], qt[:, 0, 0:cn]),
                            (g.ap("dbg4")[:, col:col + cn], t4[:, 0:cn]),
                        ])
                    ph.st("sync", dbg_st)

                def ch4(v, cn=cn):
                    v.tensor_scalar_add(t4[:, 0:cn], t2[:, 0:cn], 1.0)
                    v.tensor_tensor(t4[:, 0:cn], t4[:, 0:cn], t4[:, 0:cn], ALU.mult)
                    v.tensor_tensor(t5[:, 0:cn], t3[:, 0:cn], t3[:, 0:cn], ALU.mult)
                    v.tensor_scalar(t1[:, 0:cn], t2[:, 0:cn], -1.0, 1.0, ALU.mult, ALU.add)
                    v.tensor_tensor(t1[:, 0:cn], t1[:, 0:cn], t1[:, 0:cn], ALU.mult)
                    v.tensor_tensor(t6[:, 0:cn], t2[:, 0:cn], t2[:, 0:cn], ALU.mult)
                ph.st("vector", ch4)

                def ch5(v, cn=cn):
                    v.tensor_add(t4[:, 0:cn], t4[:, 0:cn], t5[:, 0:cn])
                    v.reciprocal(t4[:, 0:cn], t4[:, 0:cn])
                    v.tensor_add(t1[:, 0:cn], t1[:, 0:cn], t5[:, 0:cn])
                    v.tensor_tensor(t1[:, 0:cn], t1[:, 0:cn], t4[:, 0:cn], ALU.mult)
                    v.tensor_scalar(t6[:, 0:cn], t6[:, 0:cn], -1.0, 1.0, ALU.mult, ALU.add)
                    v.tensor_sub(t6[:, 0:cn], t6[:, 0:cn], t5[:, 0:cn])
                    v.tensor_tensor(a11o[:, 0:cn], t6[:, 0:cn], t4[:, 0:cn], ALU.mult)
                    v.scalar_tensor_tensor(a12o[:, 0:cn], t3[:, 0:cn], 2.0, t4[:, 0:cn],
                                           ALU.mult, ALU.mult)
                ph.st("vector", ch5)

                def ch6(s, cn=cn):
                    return s.activation(t1[:, 0:cn], t1[:, 0:cn], AF.Ln)
                ph.st("scalar", ch6)

                ph.st("vector", lambda v, cn=cn: v.tensor_tensor(
                    t1[:, 0:cn], rgo[:, 0:cn], t1[:, 0:cn], ALU.mult))

                ph.st("scalar", lambda s, cn=cn: s.activation(t1[:, 0:cn], t1[:, 0:cn],
                                                              AF.Exp))

                def ch7(v, cn=cn):
                    v.tensor_scalar(t1[:, 0:cn], t1[:, 0:cn], -1.0, 1.0, ALU.mult, ALU.add)
                    v.tensor_scalar_max(t1[:, 0:cn], t1[:, 0:cn], 1e-6)
                ph.st("vector", ch7)

                def sqr_act(s, cn=cn):
                    s.activation(t1[:, 0:cn], t1[:, 0:cn], AF.Ln)
                    return s.activation(t1[:, 0:cn], t1[:, 0:cn], AF.Exp, scale=0.5)
                ph.st("scalar", sqr_act)

                ph.st("vector", lambda v, cn=cn: v.tensor_tensor(
                    vpo[:, 0:cn], t1[:, 0:cn], ugo[:, 0:cn], ALU.mult))

                def ldk(q, col=col, cn=cn, ph=ph):
                    dma_io(ph, q, [(K0c[:, :, 0:cn], g.ap("K0")[:, :, col:col + cn]),
                                   (K1c[:, :, 0:cn], g.ap("K1")[:, :, col:col + cn]),
                                   (uc[:, :, 0:cn], g.ap("ubf")[:, :, col:col + cn])])
                ph.st("sync", ldk)

                def khvg(v, cn=cn):
                    sb0b = qt[:, 7, 0:cn].unsqueeze(1).broadcast_to([128, 16, cn])
                    sb1b = qt[:, 8, 0:cn].unsqueeze(1).broadcast_to([128, 16, cn])
                    vpb = vpo[:, 0:cn].unsqueeze(1).broadcast_to([128, 16, cn])
                    v.tensor_tensor(K0c[:, :, 0:cn], K0c[:, :, 0:cn], sb0b, ALU.mult)
                    v.tensor_tensor(K1c[:, :, 0:cn], K1c[:, :, 0:cn], sb1b, ALU.mult)
                    v.tensor_tensor(uc[:, :, 0:cn], uc[:, :, 0:cn], vpb, ALU.mult)
                ph.st("vector", khvg)

                def stc(q, col=col, cn=cn, ph=ph):
                    dma_io(ph, q, [
                        (g.ap("a11d")[:, col:col + cn], a11o[:, 0:cn]),
                        (g.ap("a12d")[:, col:col + cn], a12o[:, 0:cn]),
                        (g.ap("betad")[:, col:col + cn], beto[:, 0:cn]),
                        (g.ap("vpugd")[:, col:col + cn], vpo[:, 0:cn]),
                        (g.ap("sb0d")[:, col:col + cn], qt[:, 7, 0:cn]),
                        (g.ap("sb1d")[:, col:col + cn], qt[:, 8, 0:cn]),
                        (g.ap("sc0d")[:, col:col + cn], qt[:, 9, 0:cn]),
                        (g.ap("sc1d")[:, col:col + cn], qt[:, 10, 0:cn]),
                        (g.ap("Kh0")[:, :, col:col + cn], K0c[:, :, 0:cn]),
                        (g.ap("Kh1")[:, :, col:col + cn], K1c[:, :, 0:cn]),
                        (g.ap("Vg")[:, :, col:col + cn], uc[:, :, 0:cn]),
                    ])
                ph.st("sync", stc)
        ph.run()



def ph7_q(g):
    nc = g.nc
    g.dtmp("Q0", [128, 16, TL]); g.dtmp("Q1", [128, 16, TL])
    with ExitStack() as ps:
        sb = lambda n, sh, dt=F32: ps.enter_context(nc.sbuf_tensor(f"p7_{n}", sh, dt))
        wq = sb("wq", [128, 16, FF], BF16)
        ubc = sb("ubc", [128, 16, CH], BF16)
        scc = sb("scc", [128, 2, CH])
        stg0 = sb("stg0", [128, CH]); stg1 = sb("stg1", [128, CH])
        pmA = ps.enter_context(nc.psum_tensor("p7_pmA", [128, CH], F32))
        pmB = ps.enter_context(nc.psum_tensor("p7_pmB", [128, CH], F32))
        ph = Phase(g, "p7")

        wstg7 = sb("wstg7", [128, 2, FF])
        qv = g.ap("qwT").rearrange("(kt p) n -> p kt n", p=128)
        for i8 in range(8):
            ld_cast(ph, wstg7[:, :, :], wq[:, i8 * 2:i8 * 2 + 2, :],
                    qv[:, i8 * 2:i8 * 2 + 2, :])

        for b in range(B):
            for c0 in range(0, SL, CH):
                col = b * SL + c0
                cn = CH

                def ldu(q, col=col, cn=cn, ph=ph):
                    dma_io(ph, q, [(ubc[:, :, 0:cn], g.ap("ubf")[:, :, col:col + cn]),
                                   (scc[:, 0, 0:cn], g.ap("sc0d")[:, col:col + cn]),
                                   (scc[:, 1, 0:cn], g.ap("sc1d")[:, col:col + cn])])
                ph.st("sync", ldu)

                for m in range(32):
                    comp, l = m // 16, m % 16
                    pm = pmA if m % 2 == 0 else pmB
                    ss = stg0 if (m // 2) % 2 == 0 else stg1

                    def mm(t, m=m, cn=cn, pm=pm):
                        return mm_k(t, pm[:, 0:cn], wq, m * 128, 128,
                                    [ubc[:, kt, 0:cn] for kt in range(16)])
                    ph.st("tensor", mm, lag=(2 if m > 0 else 0))

                    def ev(v, comp=comp, cn=cn, pm=pm, ss=ss):
                        v.tensor_tensor(ss[:, 0:cn], pm[:, 0:cn], scc[:, comp, 0:cn],
                                        ALU.mult)
                    ph.st("vector", ev)

                    def stv(q, comp=comp, l=l, col=col, cn=cn, ss=ss, ph=ph):
                        dst = g.ap("Q0" if comp == 0 else "Q1")[:, l, col:col + cn]
                        dma_io(ph, q, [(dst, ss[:, 0:cn])])
                    ph.st("sync", stv)
        ph.run()



def ph8_scan1(g):
    nc = g.nc
    g.dtmp("pc", [B, 2, 8, 128, 8, NSUB, JG, 6])
    g.dtmp("l1subs", [B, 2, 128, 8, NSUB, 6])
    if g.mode == "ladder":
        g.dout("l1sum", [128, 16, B, 6])
    else:
        g.dtmp("l1sum", [128, 16, B, 6])
    with ExitStack() as ps:
        sb = lambda n, sh, dt=F32: ps.enter_context(nc.sbuf_tensor(f"p8_{n}", sh, dt))
        l1s = sb("l1s", [128, 16, B, 6])
        bufs = {}
        for gr in ("a", "b"):
            bufs[gr] = dict(
                kh0=sb(f"kh0{gr}", [128, 8, SL // 2], BF16),
                kh1=sb(f"kh1{gr}", [128, 8, SL // 2], BF16),
                vgg=sb(f"vgg{gr}", [128, 8, SL // 2], BF16),
                a11=sb(f"a11{gr}", [128, SL // 2]), a12=sb(f"a12{gr}", [128, SL // 2]),
                bet=sb(f"bet{gr}", [128, SL // 2]),
                fb=sb(f"fb{gr}", [128, 8, NSUB // 2, JG, 6]),
                tr0=sb(f"tr0{gr}", [128, 8, NSUB // 2]), tr1=sb(f"tr1{gr}", [128, 8, NSUB // 2]),
                tkd=sb(f"tkd{gr}", [128, 8, NSUB // 2]), tx=sb(f"tx{gr}", [128, 8, NSUB // 2]),
                ty=sb(f"ty{gr}", [128, 8, NSUB // 2]),
                Pl=sb(f"Pl{gr}", [128, 8, 4]), Slv=sb(f"Slv{gr}", [128, 8, 2]),
                cp0=sb(f"cp0{gr}", [128, 8]), cp1=sb(f"cp1{gr}", [128, 8]),
                cp2=sb(f"cp2{gr}", [128, 8]), cp3=sb(f"cp3{gr}", [128, 8]),
                subb=sb(f"subb{gr}", [128, 8, NSUB, 6]),
            )
        ph = Phase(g, "p8")

        def emit_iter(b_, hf, gr):
            bf = bufs[gr]
            eng = "vector" if gr == "a" else "gpsimd"
            dq = "sync" if gr == "a" else "scalar"
            dsem = ph.dsem if gr == "a" else ph.sdsem
            col0 = b_ * SL
            lsl = slice(hf * 8, hf * 8 + 8)
            HS = SL // 2
            HN = NSUB // 2
            for sg in range(2):
                col = col0 + sg * HS

                def ld(q, col=col, lsl=lsl, bf=bf, dsem=dsem):
                    dma_io(ph, q, [(bf["kh0"][:, :, :], g.ap("Kh0")[:, lsl, col:col + HS]),
                                   (bf["kh1"][:, :, :], g.ap("Kh1")[:, lsl, col:col + HS]),
                                   (bf["vgg"][:, :, :], g.ap("Vg")[:, lsl, col:col + HS]),
                                   (bf["a11"][:, :], g.ap("a11d")[:, col:col + HS]),
                                   (bf["a12"][:, :], g.ap("a12d")[:, col:col + HS]),
                                   (bf["bet"][:, :], g.ap("betad")[:, col:col + HS])],
                           sem=dsem)
                ph.st(dq, ld, group=gr)

                kv0 = bf["kh0"][:, :, :].rearrange("p l (s j) -> p l s j", j=LSUB)
                kv1 = bf["kh1"][:, :, :].rearrange("p l (s j) -> p l s j", j=LSUB)
                vgv = bf["vgg"][:, :, :].rearrange("p l (s j) -> p l s j", j=LSUB)
                a11v = bf["a11"][:, :].rearrange("p (s j) -> p s j", j=LSUB)
                a12v = bf["a12"][:, :].rearrange("p (s j) -> p s j", j=LSUB)
                betv = bf["bet"][:, :].rearrange("p (s j) -> p s j", j=LSUB)
                SHP = [128, 8, HN]
                fb = bf["fb"]

                for j in range(LSUB):
                    def step(v, j=j, bf=bf, kv0=kv0, kv1=kv1, vgv=vgv,
                             a11v=a11v, a12v=a12v, betv=betv, SHP=SHP, fb=fb):
                        tr0, tr1, tkd = bf["tr0"], bf["tr1"], bf["tkd"]
                        tx, ty = bf["tx"], bf["ty"]
                        k0j = kv0[:, :, :, j]
                        k1j = kv1[:, :, :, j]
                        vgj = vgv[:, :, :, j]
                        a11j = a11v[:, :, j].unsqueeze(1).broadcast_to(SHP)
                        a12j = a12v[:, :, j].unsqueeze(1).broadcast_to(SHP)
                        betj = betv[:, :, j].unsqueeze(1).broadcast_to(SHP)
                        sl_new = fb[:, :, :, j % JG, :]
                        sl_old = fb[:, :, :, (j - 1) % JG, :] if j > 0 else None
                        for pi, (i0, i1, with_in) in enumerate(((0, 2, False),
                                                               (1, 3, False),
                                                               (4, 5, True))):
                            if j == 0:
                                if pi == 0:
                                    v.tensor_copy(tr0[:, :, :], a11j)
                                    v.tensor_scalar_mul(tr1[:, :, :], a12j, -1.0)
                                elif pi == 1:
                                    v.tensor_copy(tr0[:, :, :], a12j)
                                    v.tensor_copy(tr1[:, :, :], a11j)
                                else:
                                    v.memset(tr0[:, :, :], 0.0)
                                    v.memset(tr1[:, :, :], 0.0)
                            else:
                                x0 = sl_old[:, :, :, i0]
                                x1 = sl_old[:, :, :, i1]
                                v.tensor_tensor(tx[:, :, :], a11j, x0, ALU.mult)
                                v.tensor_tensor(ty[:, :, :], a12j, x1, ALU.mult)
                                v.tensor_add(tr0[:, :, :], tx[:, :, :], ty[:, :, :])
                                v.tensor_tensor(tx[:, :, :], a11j, x1, ALU.mult)
                                v.tensor_tensor(ty[:, :, :], a12j, x0, ALU.mult)
                                v.tensor_sub(tr1[:, :, :], tx[:, :, :], ty[:, :, :])
                            v.tensor_tensor(tx[:, :, :], k0j, tr0[:, :, :], ALU.mult)
                            v.tensor_tensor(ty[:, :, :], k1j, tr1[:, :, :], ALU.mult)
                            v.tensor_add(tkd[:, :, :], tx[:, :, :], ty[:, :, :])
                            if with_in:
                                v.tensor_sub(tkd[:, :, :], vgj, tkd[:, :, :])
                            v.tensor_tensor(tkd[:, :, :], betj, tkd[:, :, :], ALU.mult)
                            op = ALU.add if with_in else ALU.subtract
                            v.tensor_tensor(tx[:, :, :], tkd[:, :, :], k0j, ALU.mult)
                            v.tensor_tensor(sl_new[:, :, :, i0], tr0[:, :, :],
                                            tx[:, :, :], op)
                            v.tensor_tensor(ty[:, :, :], tkd[:, :, :], k1j, ALU.mult)
                            v.tensor_tensor(sl_new[:, :, :, i1], tr1[:, :, :],
                                            ty[:, :, :], op)
                    ph.st(eng, step, group=gr)

                    if j % JG == JG - 1:
                        def flush(q, b_=b_, hf=hf, jg=j // JG, sg=sg, bf=bf, dsem=dsem):
                            s0 = sg * HN
                            dma_io(ph, q, [(g.ap("pc")[b_, hf, jg, :, :, s0:s0 + HN, :, :],
                                            bf["fb"][:, :, :, :, :])], sem=dsem)
                        ph.st(dq, flush, group=gr)

                def chainseg(v, b_=b_, hf=hf, sg=sg, bf=bf):
                    Pl, Slv, subb = bf["Pl"], bf["Slv"], bf["subb"]
                    cp0, cp1, cp2, cp3 = bf["cp0"], bf["cp1"], bf["cp2"], bf["cp3"]
                    st6v = bf["fb"][:, :, :, (LSUB - 1) % JG, :]
                    if sg == 0:
                        v.memset(Pl[:, :, 0], 1.0); v.memset(Pl[:, :, 1], 0.0)
                        v.memset(Pl[:, :, 2], 0.0); v.memset(Pl[:, :, 3], 1.0)
                        v.memset(Slv[:, :, :], 0.0)
                    for s in range(HN):
                        sglob = sg * HN + s
                        for i in range(4):
                            v.tensor_copy(subb[:, :, sglob, i], Pl[:, :, i])
                        for i in range(2):
                            v.tensor_copy(subb[:, :, sglob, 4 + i], Slv[:, :, i])
                        P00, P01 = st6v[:, :, s, 0], st6v[:, :, s, 1]
                        P10, P11 = st6v[:, :, s, 2], st6v[:, :, s, 3]
                        c0_, c1_ = st6v[:, :, s, 4], st6v[:, :, s, 5]
                        v.tensor_tensor(cp0[:, :], P00, Pl[:, :, 0], ALU.mult)
                        v.tensor_tensor(cp1[:, :], P01, Pl[:, :, 2], ALU.mult)
                        v.tensor_tensor(cp2[:, :], P00, Pl[:, :, 1], ALU.mult)
                        v.tensor_tensor(cp3[:, :], P01, Pl[:, :, 3], ALU.mult)
                        v.tensor_add(cp0[:, :], cp0[:, :], cp1[:, :])
                        v.tensor_add(cp2[:, :], cp2[:, :], cp3[:, :])
                        v.tensor_tensor(cp1[:, :], P10, Pl[:, :, 0], ALU.mult)
                        v.tensor_tensor(cp3[:, :], P11, Pl[:, :, 2], ALU.mult)
                        v.tensor_add(cp1[:, :], cp1[:, :], cp3[:, :])
                        v.tensor_tensor(cp3[:, :], P10, Pl[:, :, 1], ALU.mult)
                        v.tensor_copy(Pl[:, :, 0], cp0[:, :])
                        v.tensor_tensor(cp0[:, :], P11, Pl[:, :, 3], ALU.mult)
                        v.tensor_add(cp3[:, :], cp3[:, :], cp0[:, :])
                        v.tensor_copy(Pl[:, :, 1], cp2[:, :])
                        v.tensor_copy(Pl[:, :, 2], cp1[:, :])
                        v.tensor_copy(Pl[:, :, 3], cp3[:, :])
                        v.tensor_tensor(cp0[:, :], P00, Slv[:, :, 0], ALU.mult)
                        v.tensor_tensor(cp1[:, :], P01, Slv[:, :, 1], ALU.mult)
                        v.tensor_add(cp0[:, :], cp0[:, :], cp1[:, :])
                        v.tensor_add(cp0[:, :], cp0[:, :], c0_)
                        v.tensor_tensor(cp1[:, :], P10, Slv[:, :, 0], ALU.mult)
                        v.tensor_tensor(cp2[:, :], P11, Slv[:, :, 1], ALU.mult)
                        v.tensor_add(cp1[:, :], cp1[:, :], cp2[:, :])
                        v.tensor_add(cp1[:, :], cp1[:, :], c1_)
                        v.tensor_copy(Slv[:, :, 0], cp0[:, :])
                        v.tensor_copy(Slv[:, :, 1], cp1[:, :])
                ph.st(eng, chainseg, group=gr)

            def st_l1s(v, b_=b_, hf=hf, bf=bf):
                for i in range(4):
                    v.tensor_copy(l1s[:, hf * 8:hf * 8 + 8, b_, i], bf["Pl"][:, :, i])
                for i in range(2):
                    v.tensor_copy(l1s[:, hf * 8:hf * 8 + 8, b_, 4 + i], bf["Slv"][:, :, i])
            ph.st(eng, st_l1s, group=gr)

            def stsub(q, b_=b_, hf=hf, bf=bf, dsem=dsem):
                dma_io(ph, q, [(g.ap("l1subs")[b_, hf, :, :, :, :], bf["subb"][:, :, :, :])],
                       sem=dsem)
            ph.st(dq, stsub, group=gr)

        emit_iter(0, 0, "a")
        emit_iter(1, 0, "b")
        emit_iter(0, 1, "a")
        emit_iter(1, 1, "b")

        def stsum(q, ph=ph):
            dma_io(ph, q, [(g.ap("l1sum")[:, :, :, :], l1s[:, :, :, :])])
        ph.st("sync", stsum)
        ph.run()



def ph9_l1fix(g):
    nc = g.nc
    if g.mode == "ladder":
        g.din("l1sum_all", [NCORES, 128, 16, B, 6])
        g.dout("l1stats", [128, B, 2])
    else:
        g.dtmp("l1stats", [128, B, 2])
    g.dtmp("yq", [128, 16, TL])
    HSL = SL // 2
    HSUB = NSUB // 2
    with ExitStack() as ps:
        sb = lambda n, sh, dt=F32: ps.enter_context(nc.sbuf_tensor(f"p9_{n}", sh, dt))
        sall = sb("sall", [128, NCORES, 16, B, 6])
        ohbs = sb("ohbs", [128, NCORES])
        cand = sb("cand", [128, 16, B, 2, NCORES])
        ci = sb("ci", [128, 16, B, 2])
        t0 = sb("t0", [128, 16, B]); t1 = sb("t1", [128, 16, B])
        subl = sb("subl", [128, 8, NSUB, 6])
        sini = sb("sini", [128, 8, NSUB, 2])
        ta = sb("ta", [128, 8, NSUB]); tb = sb("tb", [128, 8, NSUB])
        slab = sb("slab", [128, 8, HSUB, JG, 6])
        Y0 = sb("Y0", [128, 8, HSL]); Y1 = sb("Y1", [128, 8, HSL])
        Q0b = sb("Q0b", [128, 8, HSL]); Q1b = sb("Q1b", [128, 8, HSL])
        yb = sb("yb", [128, 8, HSL])
        stt = sb("stt", [128, B, 2])
        red = sb("red", [128, 1])
        ph = Phase(g, "p9")

        def ld0(q, ph=ph):
            dma_io(ph, q, [(sall[:, :, :, :, :],
                            g.ap("l1sum_all").rearrange("c p l b q -> p c l b q")),
                           (ohbs[:, :], g.ap("ohself").partition_broadcast(128))])
        ph.st("sync", ld0)

        def chain(v):
            v.memset(cand[:, :, :, :, 0], 0.0)
            v.memset(stt[:, :, :], 0.0)
            for c in range(NCORES - 1):
                P00 = sall[:, c, :, :, 0]; P01 = sall[:, c, :, :, 1]
                P10 = sall[:, c, :, :, 2]; P11 = sall[:, c, :, :, 3]
                i0 = cand[:, :, :, 0, c]; i1 = cand[:, :, :, 1, c]
                v.tensor_tensor(t0[:, :, :], P00, i0, ALU.mult)
                v.tensor_tensor(t1[:, :, :], P01, i1, ALU.mult)
                v.tensor_add(t0[:, :, :], t0[:, :, :], t1[:, :, :])
                v.tensor_add(cand[:, :, :, 0, c + 1], t0[:, :, :], sall[:, c, :, :, 4])
                v.tensor_tensor(t0[:, :, :], P10, i0, ALU.mult)
                v.tensor_tensor(t1[:, :, :], P11, i1, ALU.mult)
                v.tensor_add(t0[:, :, :], t0[:, :, :], t1[:, :, :])
                v.tensor_add(cand[:, :, :, 1, c + 1], t0[:, :, :], sall[:, c, :, :, 5])
            v.memset(ci[:, :, :, :], 0.0)
            for c in range(NCORES):
                v.scalar_tensor_tensor(ci[:, :, :, :], cand[:, :, :, :, c],
                                       ohbs[:, c:c + 1], ci[:, :, :, :],
                                       ALU.mult, ALU.add)
        ph.st("vector", chain)

        for b in range(B):
            for hf in range(2):
                lsl = slice(hf * 8, hf * 8 + 8)

                def ldsub(q, b=b, hf=hf, ph=ph):
                    dma_io(ph, q, [(subl[:, :, :, :], g.ap("l1subs")[b, hf, :, :, :, :])])
                ph.st("sync", ldsub)

                def mksini(v, b=b, hf=hf, lsl=lsl):
                    SHP = [128, 8, NSUB]
                    ci0 = ci[:, lsl, b, 0].unsqueeze(2).broadcast_to(SHP)
                    ci1 = ci[:, lsl, b, 1].unsqueeze(2).broadcast_to(SHP)
                    v.tensor_tensor(ta[:, :, :], subl[:, :, :, 0], ci0, ALU.mult)
                    v.tensor_tensor(tb[:, :, :], subl[:, :, :, 1], ci1, ALU.mult)
                    v.tensor_add(ta[:, :, :], ta[:, :, :], tb[:, :, :])
                    v.tensor_add(sini[:, :, :, 0], ta[:, :, :], subl[:, :, :, 4])
                    v.tensor_tensor(ta[:, :, :], subl[:, :, :, 2], ci0, ALU.mult)
                    v.tensor_tensor(tb[:, :, :], subl[:, :, :, 3], ci1, ALU.mult)
                    v.tensor_add(ta[:, :, :], ta[:, :, :], tb[:, :, :])
                    v.tensor_add(sini[:, :, :, 1], ta[:, :, :], subl[:, :, :, 5])
                ph.st("vector", mksini)

                for ch2 in range(2):
                    s0 = ch2 * HSUB
                    colh = b * SL + hf * 0 + ch2 * HSL

                    for jg in range(JG):
                        def ldsl(q, b=b, hf=hf, jg=jg, s0=s0, ph=ph):
                            dma_io(ph, q, [(slab[:, :, :, :, :],
                                            g.ap("pc")[b, hf, jg, :, :, s0:s0 + HSUB, :, :])])
                        ph.st("sync", ldsl)

                        def fixup(v, jg=jg, s0=s0):
                            SH = [128, 8, HSUB, JG]
                            I0 = sini[:, :, s0:s0 + HSUB, 0].unsqueeze(3).broadcast_to(SH)
                            I1 = sini[:, :, s0:s0 + HSUB, 1].unsqueeze(3).broadcast_to(SH)
                            y0v = Y0[:, :, :].rearrange("p l (s j) -> p l s j", j=LSUB)[
                                :, :, :, jg * JG:(jg + 1) * JG]
                            y1v = Y1[:, :, :].rearrange("p l (s j) -> p l s j", j=LSUB)[
                                :, :, :, jg * JG:(jg + 1) * JG]
                            v.tensor_tensor(y0v, slab[:, :, :, :, 0], I0, ALU.mult)
                            v.scalar_tensor_tensor(y0v, slab[:, :, :, :, 1], 1.0, I1,
                                                   ALU.mult, ALU.mult) if False else None
                            v.tensor_tensor(yb[:, :, 0:HSUB * JG].rearrange(
                                "p l (s j) -> p l s j", j=JG), slab[:, :, :, :, 1], I1,
                                ALU.mult)
                            v.tensor_add(y0v, y0v, yb[:, :, 0:HSUB * JG].rearrange(
                                "p l (s j) -> p l s j", j=JG))
                            v.tensor_add(y0v, y0v, slab[:, :, :, :, 4])
                            v.tensor_tensor(y1v, slab[:, :, :, :, 2], I0, ALU.mult)
                            v.tensor_tensor(yb[:, :, 0:HSUB * JG].rearrange(
                                "p l (s j) -> p l s j", j=JG), slab[:, :, :, :, 3], I1,
                                ALU.mult)
                            v.tensor_add(y1v, y1v, yb[:, :, 0:HSUB * JG].rearrange(
                                "p l (s j) -> p l s j", j=JG))
                            v.tensor_add(y1v, y1v, slab[:, :, :, :, 5])
                        ph.st("vector", fixup)

                    col = b * SL + ch2 * HSL

                    def ldq(q, col=col, lsl=lsl, ph=ph):
                        dma_io(ph, q, [(Q0b[:, :, :], g.ap("Q0")[:, lsl, col:col + HSL]),
                                       (Q1b[:, :, :], g.ap("Q1")[:, lsl, col:col + HSL])])
                    ph.st("sync", ldq)

                    def mky(v, b=b):
                        v.tensor_tensor(yb[:, :, :], Q0b[:, :, :], Y0[:, :, :], ALU.mult)
                        v.tensor_tensor(Q1b[:, :, :], Q1b[:, :, :], Y1[:, :, :], ALU.mult)
                        v.tensor_add(yb[:, :, :], yb[:, :, :], Q1b[:, :, :])
                        v.tensor_reduce(red[:, :], yb[:, :, :], AX.XY, ALU.add)
                        v.tensor_add(stt[:, b, 0:1], stt[:, b, 0:1], red[:, :])
                        v.tensor_tensor(Q0b[:, :, :], yb[:, :, :], yb[:, :, :], ALU.mult)
                        v.tensor_reduce(red[:, :], Q0b[:, :, :], AX.XY, ALU.add)
                        v.tensor_add(stt[:, b, 1:2], stt[:, b, 1:2], red[:, :])
                    ph.st("vector", mky)

                    def sty(q, col=col, lsl=lsl, ph=ph):
                        dma_io(ph, q, [(g.ap("yq")[:, lsl, col:col + HSL], yb[:, :, :])])
                    ph.st("sync", sty)

        def stst(q, ph=ph):
            dma_io(ph, q, [(g.ap("l1stats")[:, :, :], stt[:, :, :])])
        ph.st("sync", stst)
        ph.run()



def ph10_attnout(g):
    nc = g.nc
    if g.mode == "ladder":
        g.din("stats_all", [NCORES, 128, B, 2])
    g.dtmp("sattn", [128, 8, TL])
    NTOT = float(HD * S)
    with ExitStack() as ps:
        sb = lambda n, sh, dt=F32: ps.enter_context(nc.sbuf_tensor(f"p10_{n}", sh, dt))
        sall = sb("sall", [128, NCORES, B, 2])
        stt = sb("stt", [128, B, 2])
        gmk = sb("gmk", [128, 32]); gmkT = sb("gmkT", [32, 128])
        mus32 = sb("mus32", [32, B, 2]); tmp32 = sb("tmp32", [32, B])
        musp = sb("musp", [128, B, 2])
        gnw = sb("gnw", [128, 16]); gnb = sb("gnb", [128, 16]); Dpl = sb("Dpl", [128, 16])
        wow = sb("wow", [128, 16, D], BF16)
        yc = sb("yc", [128, 16, CH])
        zc = sb("zc", [128, 16, CH], BF16)
        uc = sb("uc", [128, 16, CH])
        ybf = sb("ybf", [128, 16, CH], BF16)
        xnc = sb("xnc", [128, 8, CH])
        sac = sb("sac", [128, 8, CH])
        pmA = ps.enter_context(nc.psum_tensor("p10_pmA", [128, CH], F32))
        pmB = ps.enter_context(nc.psum_tensor("p10_pmB", [128, CH], F32))
        pst = ps.enter_context(nc.psum_tensor("p10_pst", [32, B * 2], F32))
        psb = ps.enter_context(nc.psum_tensor("p10_psb", [128, B * 2], F32))
        ph = Phase(g, "p10")

        def ld0(q, ph=ph):
            dma_io(ph, q, [
                (sall[:, :, :, :], g.ap("stats_all").rearrange("c p b q -> p c b q")),
                (gmk[:, :], g.ap("grpmask")),
                (gmkT[:, :], g.ap("grpmaskT")),
                (gnw[:, :], g.ap("gnw")), (gnb[:, :], g.ap("gnb")),
                (Dpl[:, :], g.ap("Dpl")),
            ])
        ph.st("sync", ld0)

        wstg10 = sb("wstg10", [128, 8, D])
        ov = g.ap("owT").rearrange("(kt p) n -> p kt n", p=128)
        for hf_ in range(2):
            ld_cast(ph, wstg10[:, :, :], wow[:, hf_ * 8:hf_ * 8 + 8, :],
                    ov[:, hf_ * 8:hf_ * 8 + 8, :])

        def sum_cores(v):
            v.tensor_copy(stt[:, :, :], sall[:, 0, :, :])
            for c in range(1, NCORES):
                v.tensor_add(stt[:, :, :], stt[:, :, :], sall[:, c, :, :])
        ph.st("vector", sum_cores)

        ph.st("tensor", lambda t: t.matmul(
            pst[:, :], gmk[:, :], stt[:, :, :].rearrange("p b q -> p (b q)"),
            start=True, stop=True))

        def stats32(v):
            v.tensor_scalar_mul(mus32[:, :, :], pst[:, :].rearrange(
                "p (b q) -> p b q", q=2), 1.0 / NTOT)
            v.tensor_tensor(tmp32[:, :], mus32[:, :, 0], mus32[:, :, 0], ALU.mult)
            v.tensor_sub(mus32[:, :, 1], mus32[:, :, 1], tmp32[:, :])
        ph.st("vector", stats32)

        def gn_rstd(s):
            s.activation(mus32[:, :, 1], mus32[:, :, 1], AF.Ln, bias=EPS_GN)
            return s.activation(mus32[:, :, 1], mus32[:, :, 1], AF.Exp, scale=-0.5)
        ph.st("scalar", gn_rstd)

        ph.st("tensor", lambda t: t.matmul(
            psb[:, :], gmkT[:, :], mus32[:, :, :].rearrange("p b q -> p (b q)"),
            start=True, stop=True))

        ph.st("vector", lambda v: v.tensor_copy(
            musp[:, :, :], psb[:, :].rearrange("p (b q) -> p b q", q=2)))

        for b in range(B):
            for c0 in range(0, SL, CH):
                col = b * SL + c0
                cn = CH

                def ldc(q, col=col, cn=cn, ph=ph):
                    dma_io(ph, q, [(yc[:, :, 0:cn], g.ap("yq")[:, :, col:col + cn]),
                                   (zc[:, :, 0:cn], g.ap("siluz")[:, :, col:col + cn]),
                                   (uc[:, :, 0:cn], g.ap("u")[:, :, col:col + cn]),
                                   (xnc[:, :, 0:cn], g.ap("xn1T")[:, :, col:col + cn])])
                ph.st("sync", ldc)

                def gnorm(v, b=b, cn=cn):
                    v.tensor_scalar_sub(yc[:, :, 0:cn], yc[:, :, 0:cn], musp[:, b, 0:1])
                    v.tensor_scalar_mul(yc[:, :, 0:cn], yc[:, :, 0:cn], musp[:, b, 1:2])
                    gwb = gnw[:, :].unsqueeze(2).broadcast_to([128, 16, cn])
                    gbb = gnb[:, :].unsqueeze(2).broadcast_to([128, 16, cn])
                    v.tensor_tensor(yc[:, :, 0:cn], yc[:, :, 0:cn], gwb, ALU.mult)
                    v.tensor_add(yc[:, :, 0:cn], yc[:, :, 0:cn], gbb)
                    v.tensor_tensor(yc[:, :, 0:cn], yc[:, :, 0:cn], zc[:, :, 0:cn],
                                    ALU.mult)
                    dub = Dpl[:, :].unsqueeze(2).broadcast_to([128, 16, cn])
                    v.tensor_tensor(uc[:, :, 0:cn], uc[:, :, 0:cn], dub, ALU.mult)
                    v.tensor_add(yc[:, :, 0:cn], yc[:, :, 0:cn], uc[:, :, 0:cn])
                    v.tensor_copy(ybf[:, :, 0:cn], yc[:, :, 0:cn])
                ph.st("vector", gnorm)

                for m in range(8):
                    pm = pmA if m % 2 == 0 else pmB
                    def mmo(t, m=m, cn=cn, pm=pm):
                        return mm_k(t, pm[:, 0:cn], wow, m * 128, 128,
                                    [ybf[:, kt, 0:cn] for kt in range(16)])
                    ph.st("tensor", mmo, lag=(1 if m > 0 else 0))

                    def evo(v, m=m, cn=cn, pm=pm):
                        v.tensor_add(sac[:, m, 0:cn], pm[:, 0:cn], xnc[:, m, 0:cn])
                    ph.st("vector", evo)

                def stc(q, col=col, cn=cn, ph=ph):
                    dma_io(ph, q, [(g.ap("sattn")[:, :, col:col + cn], sac[:, :, 0:cn])])
                ph.st("sync", stc)
        ph.run()



def ph11_moe(g):
    nc = g.nc
    g.dtmp("cT", [NE, TL], BF16)
    g.dtmp("g1", [128, 8, TL])
    with ExitStack() as ps:
        sb = lambda n, sh, dt=F32: ps.enter_context(nc.sbuf_tensor(f"p11_{n}", sh, dt))
        accbf = sb("accbf", [128, 8, TL], BF16)
        bce = sb("bce", [128, TL], BF16)
        wstg11 = sb("wstg11", [128, 4096])
        wstg_a = wstg11.ap().rearrange("p (a n) -> p a n", n=2048)
        wstg_b = wstg11.ap().rearrange("p (a n) -> p a n", n=1024)
        wstg_c = wstg11.ap().rearrange("p (a n) -> p a n", n=512)
        wstg_f = wstg11.ap().rearrange("p (a n) -> p a n", n=CH)
        xbc = sb("xbc", [128, 2, 8, CH], BF16)
        tmpc = sb("tmpc", [128, CH])
        pmA = ps.enter_context(nc.psum_tensor("p11_pmA", [128, CH], F32))
        pmB = ps.enter_context(nc.psum_tensor("p11_pmB", [128, CH], F32))
        ph = Phase(g, "p11")

        with ExitStack() as gs:
            sg_ = lambda n, sh, dt=F32: gs.enter_context(nc.sbuf_tensor(f"p11g_{n}", sh, dt))
            gw = sg_("gw", [128, 8, NE], BF16)
            ident = sg_("ident", [128, 128])
            lgt = sg_("lgt", [NE, TL])
            ltok = sg_("ltok", [128, NE])
            m1n = sg_("m1n", [128, 1]); m2 = sg_("m2", [128, 1])
            eqv = sg_("eqv", [128, NE]); l2v = sg_("l2v", [128, NE])
            selv = sg_("selv", [128, NE]); ev = sg_("ev", [128, NE])
            ssum = sg_("ssum", [128, 1])
            ctok = sg_("ctok", [128, NE])
            ctt = sg_("ctt", [NE, TL], BF16)
            pmt = gs.enter_context(nc.psum_tensor("p11_pmt", [128, NE], F32))
            pmt2 = gs.enter_context(nc.psum_tensor("p11_pmt2", [128, 128], F32))

            def ld0(q, ph=ph):
                dma_io(ph, q, [(ident[:, :], g.ap("identf"))])
            ph.st("sync", ld0)

            ld_cast(ph, wstg_c[:, 0:8, 0:NE], gw[:, :, :],
                    g.ap("gwT").rearrange("(kt p) n -> p kt n", p=128))

            for ci in range(NCH):
                col = ci * CH

                def ldx(q, col=col, ph=ph):
                    dma_io(ph, q, [(xbc[:, 0, :, :], g.ap("xn1bf")[:, :, col:col + CH])])
                ph.st("sync", ldx)

                def mmg(t, col=col):
                    return mm_k(t, pmA[0:NE, 0:CH], gw, 0, NE,
                                [xbc[:, 0, kt, :] for kt in range(8)])
                ph.st("tensor", mmg)

                def evg(v, col=col):
                    v.tensor_copy(lgt[:, col:col + CH], pmA[0:NE, 0:CH])
                ph.st("vector", evg)

            for tt in range(TL // 128):
                def tpl(t, tt=tt):
                    return t.transpose(pmt[:, 0:NE], lgt[0:NE, tt * 128:(tt + 1) * 128],
                                       ident[0:NE, 0:NE])
                ph.st("tensor", tpl)

                def top2(v, tt=tt):
                    v.tensor_copy(ltok[:, :], pmt[:, 0:NE])
                    v.tensor_reduce(m1n[:, :], ltok[:, :], AX.X, ALU.max, negate=True)
                    v.tensor_scalar(eqv[:, :], ltok[:, :], m1n[:, 0:1], None, ALU.add)
                    v.tensor_single_scalar(eqv[:, :], eqv[:, :], 0.0, ALU.is_ge)
                    v.scalar_tensor_tensor(l2v[:, :], eqv[:, :], -1e30, ltok[:, :],
                                           ALU.mult, ALU.add)
                    v.tensor_reduce(m2[:, :], l2v[:, :], AX.X, ALU.max)
                    v.tensor_scalar(selv[:, :], ltok[:, :], m2[:, 0:1], None, ALU.is_ge)
                ph.st("vector", top2)

                def expn(s, tt=tt):
                    return s.activation(ev[:, :], ltok[:, :], AF.Exp, bias=m1n[:, 0:1])
                ph.st("scalar", expn)

                def wts(v, tt=tt):
                    v.tensor_tensor(ev[:, :], ev[:, :], selv[:, :], ALU.mult)
                    v.tensor_reduce(ssum[:, :], ev[:, :], AX.X, ALU.add)
                    v.reciprocal(ssum[:, :], ssum[:, :])
                    v.tensor_scalar_mul(ctok[:, :], ev[:, :], ssum[:, 0:1])
                ph.st("vector", wts)

                def tpc(t, tt=tt):
                    return t.transpose(pmt2[0:NE, :], ctok[:, :], ident[:, :])
                ph.st("tensor", tpc)

                def evc(v, tt=tt):
                    v.tensor_copy(ctt[:, tt * 128:(tt + 1) * 128], pmt2[0:NE, :])
                ph.st("vector", evc)

            def stct(q, ph=ph):
                dma_io(ph, q, [(g.ap("cT")[:, :], ctt[:, :])])
            ph.st("sync", stct)

        es = ps
        w1e = sb("w1e", [128, 8, FE], BF16)
        w2e = sb("w2e", [128, 16, D], BF16)
        b1e = sb("b1e", [128, 16]); b2e = sb("b2e", [128, 8])
        hh = sb("hh", [128, 16, CH], BF16)
        outb = sb("outb", [128, 2, 8, CH], BF16)
        gchunk = 0
        for e in range(NE):
            w1v = g.ap("moew1")[e].rearrange("(kt p) n -> p kt n", p=128)
            w2v = g.ap("moew2")[e].rearrange("(kt p) n -> p kt n", p=128)
            for hf_ in range(2):
                ld_cast(ph, wstg_a[:, 0:2, :], w1e[:, hf_ * 4:hf_ * 4 + 2, :],
                        w1v[:, hf_ * 4:hf_ * 4 + 2, :])
                ld_cast(ph, wstg_a[:, 0:2, :], w1e[:, hf_ * 4 + 2:hf_ * 4 + 4, :],
                        w1v[:, hf_ * 4 + 2:hf_ * 4 + 4, :])
                ld_cast(ph, wstg_b[:, 0:4, :], w2e[:, hf_ * 8:hf_ * 8 + 4, :],
                        w2v[:, hf_ * 8:hf_ * 8 + 4, :])
                ld_cast(ph, wstg_b[:, 0:4, :], w2e[:, hf_ * 8 + 4:hf_ * 8 + 8, :],
                        w2v[:, hf_ * 8 + 4:hf_ * 8 + 8, :])

            def ldbe(q, e=e, ph=ph):
                dma_io(ph, q, [(b1e[:, :], g.ap("moeb1")[e]), (b2e[:, :], g.ap("moeb2")[e]),
                               (bce[:, :], g.ap("cT")[e:e + 1, :].partition_broadcast(128))])
            ph.st("sync", ldbe)

            for ci in range(NCH):
                col = ci * CH
                par = gchunk % 2
                gchunk += 1
                first = (e == 0 and ci == 0)

                if ci == 0:
                    def ldx2(q, col=col, par=par, ph=ph):
                        dma_io(ph, q, [(xbc[:, par, :, :],
                                        g.ap("xn1bf")[:, :, col:col + CH])])
                    ph.st("sync", ldx2)

                for fm in range(16):
                    pm = pmA if fm % 2 == 0 else pmB
                    def mm1(t, fm=fm, pm=pm, par=par):
                        return mm_k(t, pm[:, :], w1e, fm * 128, 128,
                                    [xbc[:, par, kt, :] for kt in range(8)])
                    ph.st("tensor", mm1, lag=(1 if fm > 0 else (2 if not first else 0)))

                    def ev1(s, fm=fm, pm=pm):
                        return s.activation(hh[:, fm, :], pm[:, :], AF.Gelu,
                                            bias=b1e[:, fm:fm + 1])
                    ph.st("scalar", ev1)

                for dm in range(8):
                    pm = pmA if dm % 2 == 0 else pmB
                    def mm2(t, dm=dm, pm=pm):
                        return mm_k(t, pm[:, :], w2e, dm * 128, 128,
                                    [hh[:, kt, :] for kt in range(16)])
                    ph.st("tensor", mm2, lag=(2 if dm > 0 else 0))

                    def ev2(s, dm=dm, pm=pm, par=par):
                        return s.activation(outb[:, par, dm, :], pm[:, :], AF.Identity,
                                            bias=b2e[:, dm:dm + 1])
                    ph.st("scalar", ev2)

                    def combdm(v, e=e, dm=dm, col=col, par=par):
                        v.tensor_tensor(tmpc[:, :], outb[:, par, dm, :],
                                        bce[:, col:col + CH], ALU.mult)
                        if e == 0:
                            v.tensor_copy(accbf[:, dm, col:col + CH], tmpc[:, :])
                        else:
                            v.tensor_add(accbf[:, dm, col:col + CH],
                                         accbf[:, dm, col:col + CH], tmpc[:, :])
                    ph.st("vector", combdm)

                    if dm == 3 and ci < NCH - 1:
                        ncol = (ci + 1) * CH
                        npar = gchunk % 2
                        def ldxn(q, ncol=ncol, npar=npar, ph=ph):
                            dma_io(ph, q, [(xbc[:, npar, :, :],
                                            g.ap("xn1bf")[:, :, ncol:ncol + CH])])
                        ph.st("sync", ldxn, lag=8)

        for ci in range(NCH):
            col = ci * CH

            def ldf(q, col=col, ph=ph):
                dma_io(ph, q, [(wstg_f[:, :, :], g.ap("xn1T")[:, :, col:col + CH])])
            ph.st("sync", ldf)

            def fin(v, col=col):
                v.tensor_add(wstg_f[:, :, :], wstg_f[:, :, :], accbf[:, :, col:col + CH])
                v.tensor_scalar(wstg_f[:, :, :], wstg_f[:, :, :], 100.0, -100.0,
                                ALU.min, ALU.max)
            ph.st("vector", fin)

            ph.st("scalar", lambda s: s.activation(wstg_f[:, :, :], wstg_f[:, :, :],
                                                   AF.Silu))

            def stg1(q, col=col, ph=ph):
                dma_io(ph, q, [(g.ap("g1")[:, :, col:col + CH], wstg_f[:, :, :])])
            ph.st("sync", stg1)
        ph.run()



def ph12_out(g):
    nc = g.nc
    g.dout("out", [B, SL, D])
    with ExitStack() as ps:
        sb = lambda n, sh, dt=F32: ps.enter_context(nc.sbuf_tensor(f"p12_{n}", sh, dt))
        ident = sb("ident", [128, 128])
        x1c = sb("x1c", [128, 8, CH])
        sac = sb("sac", [128, 8, CH])
        g1c = sb("g1c", [128, 8, CH])
        otok = sb("otok", [128, 4, D])
        ptA = ps.enter_context(nc.psum_tensor("p12_ptA", [128, 8, 128], F32))
        ph = Phase(g, "p12")

        def ld0(q, ph=ph):
            dma_io(ph, q, [(ident[:, :], g.ap("identf"))])
        ph.st("sync", ld0)

        for b in range(B):
            for c0 in range(0, SL, CH):
                col = b * SL + c0

                def ldc(q, col=col, ph=ph):
                    dma_io(ph, q, [(x1c[:, :, :], g.ap("x1T")[:, :, col:col + CH]),
                                   (sac[:, :, :], g.ap("sattn")[:, :, col:col + CH]),
                                   (g1c[:, :, :], g.ap("g1")[:, :, col:col + CH])])
                ph.st("sync", ldc)

                def fin(v):
                    v.tensor_tensor(sac[:, :, :], sac[:, :, :], g1c[:, :, :], ALU.mult)
                    v.tensor_add(x1c[:, :, :], x1c[:, :, :], sac[:, :, :])
                ph.st("vector", fin)

                for tt in range(4):
                    def tpo(t, tt=tt):
                        r = None
                        for dt in range(8):
                            r = t.transpose(ptA[:, dt, :], x1c[:, dt, tt * 128:(tt + 1) * 128],
                                            ident[:, :])
                        return r
                    ph.st("tensor", tpo)

                    def evo(v, tt=tt):
                        v.tensor_copy(otok[:, tt, :],
                                      ptA[:, :, :].rearrange("a b c -> a (b c)")
                                      if False else ptA[:, :, :].rearrange("p d t -> p (d t)"))
                    ph.st("vector", evo)

                def sto(q, b=b, c0=c0, ph=ph):
                    dma_io(ph, q, [(g.ap("out")[b, c0:c0 + CH, :].rearrange(
                        "(t p) d -> p t d", p=128), otok[:, :, :])])
                ph.st("sync", sto)
        ph.run()



def _mk_cc(src_name, dst_name, shape):
    def cc(g):
        nc = g.nc
        g.dtmp(dst_name, [NCORES] + list(shape))
        sem = g.sem(f"cc_{src_name}")
        with nc.Block() as blk:
            @blk.gpsimd
            def _(q):
                q.collective_compute(
                    "AllGather", ALU.bypass,
                    replica_groups=[list(range(NCORES))],
                    ins=[g.ap(src_name).opt()],
                    outs=[g.ap(dst_name).opt()],
                ).then_inc(sem.h)
                q.wait_ge(sem.h, 1)
    cc.__name__ = f"cc_{src_name}"
    return cc


PHASES = [ph0_ternary, ph1_l0a, ph2_ffn0, ph3_l0scan, ph4_l0fix, ph5_inproj, ph6_chains, ph7_q, ph8_scan1, ph9_l1fix, ph10_attnout, ph11_moe, ph12_out]

FUSED_PHASES = [ph0_ternary, ph1_l0a, ph2_ffn0, ph3_l0scan,
                _mk_cc("l0sum", "l0sum_all", [128, 8, B, 10]),
                ph4_l0fix, ph5_inproj, ph6_chains, ph7_q, ph8_scan1,
                _mk_cc("l1sum", "l1sum_all", [128, 16, B, 6]),
                ph9_l1fix,
                _mk_cc("l1stats", "stats_all", [128, B, 2]),
                ph10_attnout, ph11_moe, ph12_out]


def declare_inputs(g):
    names_2d = {
        "wkT": [D, D], "wvT": [D, D], "wrT": [D, D], "woT": [D, D],
        "w1T": [D, FF], "w2T": [FF, D], "b1f": [128, 32], "b2f": [128, 8],
        "tdec": [128, 8], "tfirst": [128, 8], "tmk": [128, 8], "tmv": [128, 8],
        "tmr": [128, 8], "ipT": [D, 8192], "ipb": [128, 64], "rmsw8": [128, 8],
        "cb": [128, 16], "Dpl": [128, 16], "gnw": [128, 16], "gnb": [128, 16],
        "smallT": [DI, 1408], "smallb": [128, 11], "dtc": [128, 1], "ropec": [128, 1],
        "qwT": [DI, FF], "owT": [DI, D], "gwT": [D, NE],
        "identf": [128, 128], "tloc": [1, SL], "grpmaskT": [32, 128],
        "grpmask": [128, 32],
        "posrow": [1, SL], "ohself": [1, NCORES], "ohprev": [1, NCORES],
        "halomask": [1, 1],
    }
    for n, sh in names_2d.items():
        g.din(n, sh)
    g.din("x", [B, SL + 5, D])
    g.din("cw", [128, 16, 4])
    g.din("moew1", [NE, D, FE]); g.din("moew2", [NE, FE, D])
    g.din("moeb1", [NE, 128, 16]); g.din("moeb2", [NE, 128, 8])


def build(upto, mode="ladder", carries=(), dev_out=()):
    nc = bass.Bass("TRN2", target_bir_lowering=False)
    g = G(nc, mode, dev_out)
    declare_inputs(g)
    if "l0" in carries:
        g.din("l0sum_all", [NCORES, 128, 8, B, 10])
    if "l1" in carries:
        g.din("l1sum_all", [NCORES, 128, 16, B, 6])
    if "gs" in carries:
        g.din("stats_all", [NCORES, 128, B, 2])
    phl = FUSED_PHASES if mode == "fused" else PHASES
    with g.stack:
        g.init_consts([EPS_LN, EPS_RMS, EPS_GN, 1.0, 1e-4, 1e-6])
        for f in phl[:upto]:
            f(g)
    return g


def kernel(**inputs):
    cores = stage_host(inputs)
    g = build(len(FUSED_PHASES), mode="fused")
    res = run_bass_kernel_spmd(g.nc, [dict(c) for c in cores],
                               core_ids=list(range(NCORES)))
    out = np.zeros((B, S, D), np.float32)
    for c in range(NCORES):
        out[:, c * SL:(c + 1) * SL, :] = res.results[c]["out"]
    return out
